# version 04
# baseline (speedup 1.0000x reference)
import os
import numpy as np
import concourse.bass as bass
import concourse.mybir as mybir
from concourse.bass import IndirectOffsetOnAxis
from concourse.bass_utils import run_bass_kernel_spmd

F32 = mybir.dt.float32
BF16 = mybir.dt.bfloat16
I32 = mybir.dt.int32
AF = mybir.ActivationFunctionType
ALU = mybir.AluOpType
AX = mybir.AxisListType

NCORE = 8
D = 1024
NDC = 8
NTOK = 2048
NT = 16
Q = 512
NQ = 4
E = 8
CAP = 2560
IR = 4096
IS = 8192
DFF = 4096
LN_EPS = 1e-5
WKV_EPS = 1e-8
EPS32 = float(np.finfo(np.float32).eps)
BIG = float(2 ** 28)
OOB = (1 << 30)
CORRW = 256

ENGINES = ["sync", "scalar", "vector", "tensor", "gpsimd"]
RG = [list(range(NCORE))]


class Sched:
    def __init__(self):
        self.dry = True
        self.counts = {}
        self.cur = {}
        self.final = {}
        self.ssn = {}

    def bump(self, sem, n):
        self.counts[sem] = self.counts.get(sem, 0) + n

    def cp(self, label, sem):
        self.cur[label] = (sem, self.counts.get(sem, 0))

    def start_real(self):
        self.final = dict(self.cur)
        self.cur = {}
        self.counts = {}
        self.ssn = {}
        self.dry = False

    def check(self):
        assert self.cur == self.final, "dry/real pass divergence"


class IW:
    __slots__ = ("s", "sems", "inst")

    def __init__(self, s, sems, inst):
        self.s, self.sems, self.inst = s, sems, inst

    def then_inc(self, semname, n=1):
        self.s.bump(semname, n)
        if self.inst is not None:
            self.inst.then_inc(self.sems[semname], n)
        return self


class EW:
    def __init__(self, sched, sems, name, real):
        self.s = sched
        self.sems = sems
        self.name = name
        self.real = real

    def cp(self, label, sem):
        self.s.cp(label, sem)

    def wait(self, label):
        if self.real is not None:
            sem, v = self.s.final[label]
            if v > 0:
                self.real.wait_ge(self.sems[sem], v)

    def __getattr__(self, attr):
        def wrap(*a, **k):
            inst = getattr(self.real, attr)(*a, **k) if self.real is not None else None
            return IW(self.s, self.sems, inst)
        return wrap


def build_nc():
    nc = bass.Bass("TRN2", target_bir_lowering=False, detect_race_conditions=False)

    def inp(name, shape, dt=F32):
        return nc.dram_tensor(name, shape, dt, kind="ExternalInput")

    xs = inp("xs", [NTOK, D])
    xhalo = inp("xhalo", [1, D])
    wk_d = inp("wk", [D, D]); wv_d = inp("wv", [D, D]); wr_d = inp("wr", [D, D])
    wo_d = inp("wo", [D, D]); wfr_d = inp("wfr", [D, D])
    wfk_d = inp("wfk", [D, DFF]); wfv_d = inp("wfv", [DFF, D])
    sg_d = inp("sgw", [D, IS]); su_d = inp("suw", [D, IS]); sd_d = inp("sdw", [IS, D])
    rg_d = inp("rgw", [D, IR]); ru_d = inp("ruw", [D, IR]); rd_d = inp("rdw", [IR, D])
    rw_d = inp("rww", [E, D])
    vecs_d = inp("vecs", [9, D])
    m_selprev = inp("m_selprev", [E, 1])
    m_myid = inp("m_myid", [128, 1])
    m_flags = inp("m_flags", [1, 4])
    m_caridx = inp("m_caridx", [128, 1], I32)
    m_rowidx = inp("m_rowidx", [1, 1], I32)
    m_predoh = inp("m_predoh", [E, 1])

    out_d = nc.dram_tensor("out", [NTOK, D], F32, kind="ExternalOutput")

    _dbg = os.environ.get("BASS_KERNEL_DEBUG_DUMPS") == "1"
    _ik = dict(kind="ExternalOutput") if _dbg else {}
    xn32_d = nc.dram_tensor("xn32", [NTOK, D], F32, **_ik)
    x2_d = nc.dram_tensor("x2d", [NTOK, D], F32, **_ik)
    s32_d = nc.dram_tensor("s32d", [NTOK, D], F32, **_ik)
    xnfm_d = nc.dram_tensor("xnfm", [D, NTOK], BF16, **_ik)
    z1fm_d = nc.dram_tensor("z1fm", [D, NTOK], BF16, **_ik)
    z2fm_d = nc.dram_tensor("z2fm", [D, NTOK], BF16, **_ik)
    agx_in = nc.dram_tensor("agx_in", [NTOK, D], BF16)
    agx_out = nc.dram_tensor("agx_out", [NCORE * NTOK, D], BF16)
    agcnt_in = nc.dram_tensor("agcnt_in", [1, E], F32)
    agcnt_out = nc.dram_tensor("agcnt_out", [NCORE, E], F32)
    agrec_in = nc.dram_tensor("agrec_in", [128, 64], F32)
    agrec_out = nc.dram_tensor("agrec_out", [NCORE * 128, 64], F32)
    agcar_in = nc.dram_tensor("agcar_in", [128, 16], F32)
    agcar_out = nc.dram_tensor("agcar_out", [NCORE * 128 + 1, 16], F32)
    agx2_in = nc.dram_tensor("agx2_in", [1, D], F32)
    agx2_out = nc.dram_tensor("agx2_out", [NCORE, D], F32)
    ageo_in = nc.dram_tensor("ageo_in", [CAP, D], F32)
    ageo_out = nc.dram_tensor("ageo_out", [NCORE * CAP + 1, D], F32)
    xg_d = nc.dram_tensor("xg16", [CAP + 1, D], BF16, **_ik)
    dbg_rt = nc.dram_tensor("dbg_rt", [128, 1024], F32, kind="ExternalOutput") if _dbg else None
    dbg_sc = nc.dram_tensor("dbg_sc", [128, 20480], F32, kind="ExternalOutput") if _dbg else None
    dbg_mx = nc.dram_tensor("dbg_mx", [128, 8192], BF16, kind="ExternalOutput") if _dbg else None
    dbg_wk = nc.dram_tensor("dbg_wk", [128, 4096], BF16, kind="ExternalOutput") if _dbg else None
    dbg_h = nc.dram_tensor("dbg_h", [128, 64], F32, kind="ExternalOutput") if _dbg else None
    dbg_h16 = nc.dram_tensor("dbg_h16", [128, 8], BF16, kind="ExternalOutput") if _dbg else None
    dbg_hr = nc.dram_tensor("dbg_hr", [4, D], F32, kind="ExternalOutput") if _dbg else None

    ident32 = nc.inline_tensor(np.eye(128, dtype=np.float32), name="c_ident32")
    u_np = np.zeros((128, 128), np.float32)
    for p in range(128):
        u_np[p, p + 1:] = 1.0
    ustrict = nc.inline_tensor(u_np, name="c_ustrict")
    c_np = np.zeros((128, 128), np.float32)
    for p in range(128):
        for qq in range(128):
            if (p % 8) == (qq % 8) and (p // 8) < (qq // 8):
                c_np[p, qq] = 1.0
    ctile = nc.inline_tensor(c_np, name="c_ctile")
    ones_row = nc.inline_tensor(np.ones((1, 128), np.float32), name="c_ones_row")
    ones_col = nc.inline_tensor(np.ones((128, 1), np.float32), name="c_ones_col")
    cvals_np = np.zeros((128, 8), np.float32)
    cvals_np[:, 0] = LN_EPS
    cvals_np[:, 1] = EPS32
    cvals_np[:, 2] = WKV_EPS
    cvals_np[:, 3] = -0.5
    cvals_np[:, 4] = -1.0
    cvals_d = nc.inline_tensor(cvals_np, name="c_cvals")

    def sb(name, shape, dt):
        return nc.alloc_sbuf_tensor(name, shape, dt)

    vecr = sb("vecr", [9, D], F32)
    vcol = sb("vcol", [128, 9, 8], F32)
    ewcol = sb("ewcol", [128, 8], F32)
    netd = sb("netd", [128, 8], F32)
    tfm = sb("tfm", [128, 8], F32)
    rw16 = sb("rw16", [128, 8, 8], BF16)
    rwrow16 = sb("rwrow16", [8, D], BF16)
    haloc = sb("haloc", [128, 8], BF16)
    yhaloc = sb("yhaloc", [128, 8], BF16)
    ynprev = sb("ynprev", [128, 8], BF16)
    selprev = sb("selprev", [8, 1], F32)
    myid = sb("myid", [128, 1], F32)
    flags = sb("flags", [1, 4], F32)
    caridx = sb("caridx", [128, 1], I32)
    rowidx = sb("rowidx", [1, 1], I32)
    predoh = sb("predoh", [8, 1], F32)
    hrow = sb("hrow", [1, D], F32)
    hrow2 = sb("hrow2", [1, D], F32)
    hrow16 = sb("hrow16", [1, D], BF16)
    hstat = sb("hstat", [1, 8], F32)
    iota256 = sb("iota256", [128, CORRW], F32)
    iota8t = sb("iota8t", [128, 128], F32)
    ident_s = sb("ident_s", [128, 128], F32)
    ustr_s = sb("ustr_s", [128, 128], F32)
    ctile_s = sb("ctile_s", [128, 128], F32)
    onesr_s = sb("onesr_s", [1, 128], F32)
    onesc_s = sb("onesc_s", [128, 1], F32)
    cvals = sb("cvals", [128, 8], F32)
    flagc = sb("flagc", [128, 4], F32)

    stat = sb("stat", [128, NT, 8], F32)
    w1c = sb("w1c", [128, NT], F32)
    w2c = sb("w2c", [128, NT], F32)
    gidx1 = sb("gidx1", [128, NT], I32)
    gidx2 = sb("gidx2", [128, NT], I32)
    sidx = sb("sidx", [128, 128], I32)
    scst = sb("scst", [128, 16], F32)
    carg = sb("carg", [128, 16], F32)
    lgbuf = sb("lgbuf", [128, NT, 8], F32)
    routc = sb("routc", [128, NT, 8], F32)
    routc2 = sb("routc2", [128, NT, 8], F32)
    prefx = sb("prefx", [128, 128], F32)
    tsum = sb("tsum", [128, 1], F32)
    offsc = sb("offsc", [128, 1], F32)
    offsr = sb("offsr", [1, 128], F32)
    offsb = sb("offsb", [128, 128], F32)
    offm = sb("offm", [1, 8], F32)
    offmb = sb("offmb", [128, 8], F32)
    cntrow = sb("cntrow", [1, 128], F32)
    cnts = sb("cnts", [8, 8], F32)
    r16a = sb("r16a", [128, NT], F32)
    r16b = sb("r16b", [128, NT], F32)
    i16a = sb("i16a", [128, NT], F32)
    i16b = sb("i16b", [128, NT], F32)
    tmpc = sb("tmpc", [128, NT], F32)
    tmpc2 = sb("tmpc2", [128, NT], F32)
    recs = sb("recs", [128, 64], F32)
    rec3 = sb("rec3", [128, 16], F32)
    rec4 = sb("rec4", [128, 16], F32)
    rec5 = sb("rec5", [128, 16], F32)

    xt = sb("xt", [128, 2, D], F32)
    xnt = sb("xnt", [128, 2, D], F32)
    z1t = sb("z1t", [128, D], F32)
    sqj = sb("sqj", [128, D], BF16)
    xn16t = sb("xn16t", [128, 2, D], BF16)
    trs = sb("trs", [128, 2, 8, 128], BF16)
    g1t = sb("g1t", [128, 2, D], F32)
    g2t = sb("g2t", [128, 2, D], F32)
    xg16t = sb("xg16t", [128, 2, D], BF16)
    pool3 = sb("pool3", [128, 4096], mybir.dt.uint8)
    _b3 = nc.lookup_mloc(pool3).addr
    xgf32 = nc.alloc_sbuf_tensor_at("xgf32", [128, D], F32, offset=_b3)
    gs1 = nc.alloc_sbuf_tensor_at("gs1", [128, D], F32, offset=_b3)

    wka = sb("wka", [128, 8, D], BF16)
    wkb = sb("wkb", [128, 8, D], BF16)
    wkc = sb("wkc", [128, 8, D], BF16)

    kch = sb("kch", [128, Q], F32)
    vch = sb("vch", [128, Q], F32)
    ekb = sb("ekb", [128, Q], F32)
    ekv = sb("ekv", [128, Q], F32)
    ewb = sb("ewb", [128, Q], F32)
    wkvh = sb("wkvh", [128, 8, Q], BF16)
    corra = sb("corra", [128, 8, CORRW], F32)
    corrb = sb("corrb", [128, 8, CORRW], F32)
    pcoef = sb("pcoef", [128, CORRW], F32)
    msb = sb("msb", [128, 8, 256], BF16)

    pool1 = sb("pool1", [128, 32800], mybir.dt.uint8)
    base1 = nc.lookup_mloc(pool1).addr

    def at1(name, shape, dt, off):
        return nc.alloc_sbuf_tensor_at(name, shape, dt, offset=base1 + off)

    mixa = at1("mixa", [128, 8, Q], BF16, 0)
    mixb = at1("mixb", [128, 8, Q], BF16, 8192)
    z1h = at1("z1h", [128, 8, Q], BF16, 16384)
    xn1b = at1("xn1b", [128, 8, Q + 1], BF16, 24576)
    yn = at1("yn", [128, 8, NTOK + 1], BF16, 0)

    pool2 = sb("pool2", [128, 8192], mybir.dt.uint8)
    base2 = nc.lookup_mloc(pool2).addr

    def at2(name, shape, dt, off):
        return nc.alloc_sbuf_tensor_at(name, shape, dt, offset=base2 + off)

    rcorr = at2("rcorr", [128, 8, CORRW], BF16, 0)
    cw = at2("cw", [128, 8, CORRW], BF16, 4096)
    kkc = at2("kkc", [128, 2, 4, 256], BF16, 0)
    rrg = at2("rrg", [128, 2, D], BF16, 4096)

    PB = [nc.alloc_psum_tensor(f"pb{i}", [128, 512], F32) for i in range(8)]

    sems = {}
    for s in ["ld", "la", "lb", "sa", "sb", "sf", "sc", "st",
              "gp", "ga", "gb", "gc", "gr", "gs",
              "po", "cc", "cd", "ce", "pe", "ve", "ac"]:
        sems[s] = nc.alloc_semaphore(s)

    sched = Sched()

    def ssync(w, sem):
        n = sched.ssn.get(sem, 0)
        sched.ssn[sem] = n + 1
        lbl = f"zz_{sem}_{n}"
        w.cp(lbl, sem)
        w.wait(lbl)

    def vc(nm, dc):
        j = {"g1": 0, "b1": 1, "g2": 2, "b2": 3, "mk": 4, "mv": 5, "mr": 6,
             "td": 7, "tf": 8}[nm]
        return vcol[:, j, dc:dc + 1]

    def p3(pb, w_):
        return pb[:, :].rearrange("p (c t) -> p c t", t=w_)

    C_LN = lambda: cvals[:, 0:1]
    C_E32 = lambda: cvals[:, 1:2]
    C_WKV = lambda: cvals[:, 2:3]
    C_NH = lambda: cvals[:, 3:4]
    C_N1 = lambda: cvals[:, 4:5]
    C1_LN = lambda: cvals[0:1, 0:1]
    C1_NH = lambda: cvals[0:1, 3:4]

    def ph0(en, w):
        if en == "sync":
            for j in range(9):
                w.dma_start(out=vecr[j:j + 1, :], in_=vecs_d.ap()[j:j + 1, :]).then_inc("ld", 16)
            w.dma_start(out=hrow[:, :], in_=xhalo.ap()).then_inc("ld", 16)
            w.dma_start(out=selprev[:, :], in_=m_selprev.ap()).then_inc("ld", 16)
            w.dma_start(out=myid[:, :], in_=m_myid.ap()).then_inc("ld", 16)
            w.dma_start(out=flags[:, :], in_=m_flags.ap()).then_inc("ld", 16)
            w.dma_start(out=caridx[:, :], in_=m_caridx.ap()).then_inc("ld", 16)
            w.dma_start(out=rowidx[:, :], in_=m_rowidx.ap()).then_inc("ld", 16)
            w.dma_start(out=predoh[:, :], in_=m_predoh.ap()).then_inc("ld", 16)
            w.dma_start(out=ident_s[:, :], in_=ident32.ap()).then_inc("ld", 16)
            w.dma_start(out=ustr_s[:, :], in_=ustrict.ap()).then_inc("ld", 16)
            w.dma_start(out=ctile_s[:, :], in_=ctile.ap()).then_inc("ld", 16)
            w.dma_start(out=onesr_s[:, :], in_=ones_row.ap()).then_inc("ld", 16)
            w.dma_start(out=onesc_s[:, :], in_=ones_col.ap()).then_inc("ld", 16)
            w.dma_start(out=cvals[:, :], in_=cvals_d.ap()).then_inc("ld", 16)
            w.cp("ph0_ld", "ld")
        if en == "gpsimd":
            w.iota(iota256[:, :], [[1, CORRW]], base=1, channel_multiplier=0,
                   allow_small_or_imprecise_dtypes=True).then_inc("po", 1)
            w.iota(iota8t[:, :], [[0, 16], [1, 8]], base=0, channel_multiplier=0,
                   allow_small_or_imprecise_dtypes=True).then_inc("po", 1)
            w.cp("ph0_iota", "po")
            w.dma_start(out=rwrow16[:, :], in_=rw_d.ap()).then_inc("gp", 16)
            w.cp("ph0_rw16", "gp")
            w.dma_start(out=wka[:, :, :],
                        in_=wk_d.ap().rearrange("(a p) n -> p a n", p=128)).then_inc("ga", 16)
            w.cp("w_wk", "ga")
            w.dma_start(out=wkb[:, :, :],
                        in_=wv_d.ap().rearrange("(a p) n -> p a n", p=128)).then_inc("gb", 16)
            w.cp("w_wv", "gb")
        if en == "tensor":
            w.wait("ph0_ld")
            for dc in range(NDC):
                w.matmul(PB[0][:, dc * 16: dc * 16 + 9],
                         vecr[0:9, dc * 128:(dc + 1) * 128],
                         ident_s[0:9, 0:9], start=True, stop=True,
                         is_transpose=True).then_inc("pe", 1)
            w.cp("ph0_vtr", "pe")
            w.matmul(PB[3][:, 0:4], onesr_s[0:1, :], flags[0:1, :],
                     start=True, stop=True).then_inc("pe", 1)
            w.cp("ph0_flagb", "pe")
        if en == "vector":
            w.wait("ph0_vtr")
            for dc in range(NDC):
                w.tensor_copy(vcol[:, :, dc], PB[0][:, dc * 16:dc * 16 + 9]).then_inc("ve", 1)
            w.cp("ph0_vcol", "ve")

    def ph0rw(en, w):
        if en == "vector":
            w.wait("ph0_rw16")
            w.tensor_copy(xgf32[0:8, :], rwrow16[0:8, :]).then_inc("ve", 1)
            w.cp("ph0_rw32", "ve")
            w.wait("ph0_rwtr2")
            for dc in range(NDC):
                w.tensor_copy(rw16[:, dc, :], PB[1][:, dc * 16:dc * 16 + 8]).then_inc("ve", 1)
            w.cp("ph0_rw16c", "ve")
        if en == "tensor":
            w.wait("ph0_rw32")
            for dc in range(NDC):
                w.matmul(PB[1][:, dc * 16: dc * 16 + 8],
                         xgf32[0:8, dc * 128:(dc + 1) * 128],
                         ident_s[0:8, 0:8], start=True, stop=True,
                         is_transpose=True).then_inc("pe", 1)
            w.cp("ph0_rwtr2", "pe")

    def ph0b(en, w):
        if en == "scalar":
            w.wait("ph0_vcol")
            w.activation(netd[:, :], vcol[:, 7, :], AF.Exp).then_inc("ac", 1)
            w.cp("ph0_expt", "ac")
            w.wait("ph0_negt")
            w.activation(ewcol[:, :], netd[:, :], AF.Exp).then_inc("ac", 1)
            w.cp("ph0_ew", "ac")
        if en == "vector":
            w.wait("ph0_expt")
            w.tensor_scalar(netd[:, :], netd[:, :], -1.0, None, ALU.mult).then_inc("ve", 1)
            w.cp("ph0_negt", "ve")
            w.wait("ph0_flagb")
            w.wait("ph0_vcol")
            w.tensor_copy(flagc[:, :], PB[3][:, 0:4]).then_inc("ve", 1)
            w.cp("ph0_flagc", "ve")
            for dc in range(NDC):
                w.scalar_tensor_tensor(tfm[:, dc:dc + 1], vcol[:, 8, dc:dc + 1],
                                       1.0, PB[3][:, 0:1], ALU.mult,
                                       ALU.mult).then_inc("ve", 1)
            w.cp("ph0_tfm", "ve")

    def lnrow_chain(en, w, tag, src_row, dst_row):
        if en == "vector":
            w.wait(f"{tag}_go")
            w.tensor_reduce(hstat[0:1, 0:1], src_row, AX.X, ALU.add,
                            negate=True).then_inc("ve", 1)
            w.cp(f"{tag}_c0", "ve")
            w.wait(f"{tag}_c1")
            w.tensor_scalar(hstat[0:1, 2:3], hstat[0:1, 0:1], 1.0 / D, None,
                            ALU.mult).then_inc("ve", 1)
            w.tensor_scalar(hstat[0:1, 3:4], hstat[0:1, 1:2], 1.0 / D, None,
                            ALU.mult).then_inc("ve", 1)
            ssync(w, "ve")
            w.tensor_tensor(hstat[0:1, 4:5], hstat[0:1, 2:3], hstat[0:1, 2:3],
                            ALU.mult).then_inc("ve", 1)
            ssync(w, "ve")
            w.tensor_tensor(hstat[0:1, 4:5], hstat[0:1, 3:4], hstat[0:1, 4:5],
                            ALU.subtract).then_inc("ve", 1)
            w.cp(f"{tag}_var", "ve")
            w.wait(f"{tag}_rstd")
            w.tensor_tensor(hstat[0:1, 6:7], hstat[0:1, 2:3], hstat[0:1, 5:6],
                            ALU.mult).then_inc("ve", 1)
            w.cp(f"{tag}_bias", "ve")
            ssync(w, "ve")
            w.tensor_scalar(dst_row, src_row, hstat[0:1, 5:6], hstat[0:1, 6:7],
                            ALU.mult, ALU.add).then_inc("ve", 1)
            w.cp(f"{tag}_z", "ve")
            w.cp(f"{tag}_h16", "ve")
        if en == "scalar":
            w.wait(f"{tag}_c0")
            w.activation(sqj[0:1, :], src_row, AF.Square,
                         accum_out=hstat[0:1, 1:2]).then_inc("ac", 1)
            w.cp(f"{tag}_c1", "ac")
            w.wait(f"{tag}_var")
            w.activation(hstat[0:1, 5:6], hstat[0:1, 4:5], AF.Ln,
                         bias=C1_LN()).then_inc("ac", 1)
            ssync(w, "ac")
            w.activation(hstat[0:1, 5:6], hstat[0:1, 5:6], AF.Exp,
                         scale=C1_NH()).then_inc("ac", 1)
            w.cp(f"{tag}_rstd", "ac")
        if en == "tensor":
            w.wait(f"{tag}_h16")
            for dc in range(NDC):
                w.matmul(PB[2][:, dc * 2:dc * 2 + 1],
                         dst_row[:, dc * 128:(dc + 1) * 128] if False else
                         bass.AP(dst_row.tensor, dst_row.offset + dc * 128,
                                 [[dst_row.ap[0][0], 1], [1, 128]]),
                         ident_s[0:1, 0:1], start=True, stop=True,
                         is_transpose=True).then_inc("pe", 1)
            w.cp(f"{tag}_htr", "pe")

    def ph0c(en, w):
        if en == "sync" and dbg_hr is not None:
            w.wait("ph0_ld")
            w.dma_start(out=dbg_hr.ap()[3:4, :], in_=hrow[0:1, :]).then_inc("st", 16)
            w.cp("dbg_pre", "st")
            w.wait("hx_xn")
            w.dma_start(out=dbg_h.ap()[0:1, 44:51], in_=hstat[0:1, 0:7]).then_inc("st", 16)
            w.cp("dbg_hx", "st")
        if en == "vector":
            w.wait("ph0_ld")
            if dbg_hr is not None:
                w.wait("dbg_pre")
            w.tensor_reduce(hstat[0:1, 0:1], hrow[0:1, :], AX.X, ALU.add,
                            negate=True).then_inc("ve", 1)
            w.cp("hx_c0", "ve")
            w.wait("hx_c1")
            w.tensor_scalar(hstat[0:1, 2:3], hstat[0:1, 0:1], 1.0 / D, None,
                            ALU.mult).then_inc("ve", 1)
            w.tensor_scalar(hstat[0:1, 3:4], hstat[0:1, 1:2], 1.0 / D, None,
                            ALU.mult).then_inc("ve", 1)
            ssync(w, "ve")
            w.tensor_tensor(hstat[0:1, 4:5], hstat[0:1, 2:3], hstat[0:1, 2:3],
                            ALU.mult).then_inc("ve", 1)
            ssync(w, "ve")
            w.tensor_tensor(hstat[0:1, 4:5], hstat[0:1, 3:4], hstat[0:1, 4:5],
                            ALU.subtract).then_inc("ve", 1)
            w.cp("hx_var", "ve")
            w.wait("hx_rstd")
            w.tensor_tensor(hstat[0:1, 6:7], hstat[0:1, 2:3], hstat[0:1, 5:6],
                            ALU.mult).then_inc("ve", 1)
            w.cp("hx_bias", "ve")
            ssync(w, "ve")
            w.tensor_scalar(hrow[0:1, :], hrow[0:1, :], hstat[0:1, 5:6],
                            hstat[0:1, 6:7], ALU.mult, ALU.add).then_inc("ve", 1)
            w.cp("hx_xn", "ve")
            if dbg_hr is not None:
                w.wait("dbg_hx")
            w.cp("h1_go", "ve")
        if en == "scalar":
            w.wait("hx_c0")
            w.activation(sqj[0:1, :], hrow[0:1, :], AF.Square,
                         accum_out=hstat[0:1, 1:2]).then_inc("ac", 1)
            w.cp("hx_c1", "ac")
            w.wait("hx_var")
            w.activation(hstat[0:1, 5:6], hstat[0:1, 4:5], AF.Ln,
                         bias=C1_LN()).then_inc("ac", 1)
            ssync(w, "ac")
            w.activation(hstat[0:1, 5:6], hstat[0:1, 5:6], AF.Exp,
                         scale=C1_NH()).then_inc("ac", 1)
            w.cp("hx_rstd", "ac")
        lnrow_chain(en, w, "h1", hrow[0:1, :], hrow2[0:1, :])
        if en == "vector":
            w.wait("h1_htr")
            w.wait("ph0_flagc")
            for dc in range(NDC):
                w.tensor_scalar(haloc[:, dc:dc + 1], PB[2][:, dc * 2:dc * 2 + 1],
                                vc("g1", dc), vc("b1", dc), ALU.mult,
                                ALU.add).then_inc("ve", 1)
            ssync(w, "ve")
            w.tensor_scalar(haloc[:, :], haloc[:, :], flagc[:, 1:2], None,
                            ALU.mult).then_inc("ve", 1)
            w.cp("ph0_haloc", "ve")

    def ph1(en, w):
        if en == "sync":
            def ph1_store(i):
                ssem = "sa" if i % 2 == 0 else "sb"
                w.wait(f"e_xn_{i}")
                w.dma_start(out=xn32_d.ap()[i * 128:(i + 1) * 128, :],
                            in_=xnt[:, i % 2, :]).then_inc(ssem, 16)
                w.cp(f"e_xnst_{i}", ssem)
                w.wait(f"e_xn16_{i}")
                w.dma_start(out=agx_in.ap()[i * 128:(i + 1) * 128, :],
                            in_=xn16t[:, i % 2, :]).then_inc(ssem, 16)
                w.cp(f"e_agst_{i}", ssem)
                w.wait(f"e_tre_{i}")
                w.dma_start(
                    out=xnfm_d.ap().rearrange("(a p) t -> p a t", p=128)[:, :, i * 128:(i + 1) * 128],
                    in_=trs[:, 0, :, :]).then_inc("sf", 16)
                w.dma_start(
                    out=z1fm_d.ap().rearrange("(a p) t -> p a t", p=128)[:, :, i * 128:(i + 1) * 128],
                    in_=trs[:, 1, :, :]).then_inc("sf", 16)
                w.cp(f"e_st_{i}", "sf")
            for i in range(NT):
                lsem = "la" if i % 2 == 0 else "lb"
                if i >= 2:
                    w.wait(f"e_tr_{i - 2}")
                w.dma_start(out=xt[:, i % 2, :],
                            in_=xs.ap()[i * 128:(i + 1) * 128, :]).then_inc(lsem, 16)
                w.cp(f"e_x_{i}", lsem)
                if i >= 1:
                    ph1_store(i - 1)
            ph1_store(NT - 1)
            w.cp("e_sa_done", "sa")
            w.cp("e_sb_done", "sb")
            w.cp("e_sf_done", "sf")
        if en == "vector":
            for i in range(NT):
                st8 = stat[:, i, :]
                w.wait(f"e_x_{i}")
                w.tensor_reduce(st8[:, 0:1], xt[:, i % 2, :], AX.X, ALU.add,
                                negate=True).then_inc("ve", 1)
                w.cp(f"e_c0_{i}", "ve")
                w.wait(f"e_sq_{i}")
                w.tensor_scalar(st8[:, 0:1], st8[:, 0:1], 1.0 / D, None,
                                ALU.mult).then_inc("ve", 1)
                w.tensor_scalar(st8[:, 1:2], st8[:, 1:2], 1.0 / D, None,
                                ALU.mult).then_inc("ve", 1)
                ssync(w, "ve")
                w.tensor_tensor(st8[:, 2:3], st8[:, 0:1], st8[:, 0:1],
                                ALU.mult).then_inc("ve", 1)
                ssync(w, "ve")
                w.tensor_tensor(st8[:, 2:3], st8[:, 1:2], st8[:, 2:3],
                                ALU.subtract).then_inc("ve", 1)
                w.cp(f"e_var_{i}", "ve")
                w.wait(f"e_rstd_{i}")
                w.tensor_tensor(st8[:, 3:4], st8[:, 0:1], st8[:, 2:3],
                                ALU.mult).then_inc("ve", 1)
                w.cp(f"e_bias_{i}", "ve")
                if i >= 2:
                    w.wait(f"e_xnst_{i - 2}")
                    w.wait(f"e_petr_{i - 2}")
                ssync(w, "ve")
                w.tensor_scalar(xnt[:, i % 2, :], xt[:, i % 2, :], st8[:, 2:3],
                                st8[:, 3:4], ALU.mult, ALU.add).then_inc("ve", 1)
                w.cp(f"e_xn_{i}", "ve")
                w.tensor_reduce(st8[:, 4:5], xnt[:, i % 2, :], AX.X, ALU.add,
                                negate=True).then_inc("ve", 1)
                if i >= 2:
                    w.wait(f"e_agst_{i - 2}")
                w.tensor_copy(xn16t[:, i % 2, :], xnt[:, i % 2, :]).then_inc("ve", 1)
                w.cp(f"e_xn16_{i}", "ve")
                w.wait(f"e_sq1_{i}")
                w.tensor_scalar(st8[:, 4:5], st8[:, 4:5], 1.0 / D, None,
                                ALU.mult).then_inc("ve", 1)
                w.tensor_scalar(st8[:, 5:6], st8[:, 5:6], 1.0 / D, None,
                                ALU.mult).then_inc("ve", 1)
                ssync(w, "ve")
                w.tensor_tensor(st8[:, 6:7], st8[:, 4:5], st8[:, 4:5],
                                ALU.mult).then_inc("ve", 1)
                ssync(w, "ve")
                w.tensor_tensor(st8[:, 6:7], st8[:, 5:6], st8[:, 6:7],
                                ALU.subtract).then_inc("ve", 1)
                w.cp(f"e_var1_{i}", "ve")
                w.wait(f"e_rstd1_{i}")
                w.tensor_tensor(st8[:, 7:8], st8[:, 4:5], st8[:, 6:7],
                                ALU.mult).then_inc("ve", 1)
                w.cp(f"e_bias1_{i}", "ve")
                if i >= 1:
                    w.wait(f"e_petr_{i - 1}")
                ssync(w, "ve")
                w.tensor_scalar(z1t[:, :], xnt[:, i % 2, :], st8[:, 6:7],
                                st8[:, 7:8], ALU.mult, ALU.add).then_inc("ve", 1)
                w.cp(f"e_z1_{i}", "ve")
                w.wait(f"e_petr_{i}")
                if i >= 1:
                    w.wait(f"e_st_{i - 1}")
                w.tensor_copy(trs[:, 0, 0:4, :], p3(PB[0], 128)).then_inc("ve", 1)
                w.tensor_copy(trs[:, 0, 4:8, :], p3(PB[1], 128)).then_inc("ve", 1)
                w.tensor_copy(trs[:, 1, 0:4, :], p3(PB[2], 128)).then_inc("ve", 1)
                w.tensor_copy(trs[:, 1, 4:8, :], p3(PB[3], 128)).then_inc("ve", 1)
                w.cp(f"e_tre_{i}", "ve")
                w.cp(f"e_tr_{i}", "ve")
        if en == "scalar":
            for i in range(NT):
                st8 = stat[:, i, :]
                w.wait(f"e_c0_{i}")
                w.activation(sqj[:, :], xt[:, i % 2, :], AF.Square,
                             accum_out=st8[:, 1:2]).then_inc("ac", 1)
                w.cp(f"e_sq_{i}", "ac")
                w.wait(f"e_var_{i}")
                w.activation(st8[:, 2:3], st8[:, 2:3], AF.Ln, bias=C_LN()).then_inc("ac", 1)
                ssync(w, "ac")
                w.activation(st8[:, 2:3], st8[:, 2:3], AF.Exp, scale=C_NH()).then_inc("ac", 1)
                w.cp(f"e_rstd_{i}", "ac")
                w.wait(f"e_xn_{i}")
                w.activation(sqj[:, :], xnt[:, i % 2, :], AF.Square,
                             accum_out=st8[:, 5:6]).then_inc("ac", 1)
                w.cp(f"e_sq1_{i}", "ac")
                w.wait(f"e_var1_{i}")
                w.activation(st8[:, 5:6], st8[:, 5:6], AF.Ln, bias=C_E32()).then_inc("ac", 1)
                ssync(w, "ac")
                w.activation(st8[:, 5:6], st8[:, 5:6], AF.Exp, scale=C_NH()).then_inc("ac", 1)
                w.activation(st8[:, 6:7], st8[:, 6:7], AF.Ln, bias=C_LN()).then_inc("ac", 1)
                ssync(w, "ac")
                w.activation(st8[:, 6:7], st8[:, 6:7], AF.Exp, scale=C_NH()).then_inc("ac", 1)
                w.cp(f"e_rstd1_{i}", "ac")
        if en == "tensor":
            for i in range(NT):
                w.wait(f"e_z1_{i}")
                if i == 0:
                    w.wait("ph0_vcol")
                    w.wait("ph0_rw16c")
                    w.wait("h1_htr")
                    w.wait("ph0_haloc")
                    w.wait("ph0_tfm")
                if i >= 1:
                    w.wait(f"e_tre_{i - 1}")
                for dc in range(NDC):
                    w.matmul(PB[dc // 4][:, (dc % 4) * 128:(dc % 4 + 1) * 128],
                             xnt[:, i % 2, dc * 128:(dc + 1) * 128],
                             ident_s[:, :], start=True, stop=True,
                             is_transpose=True).then_inc("pe", 1)
                for dc in range(NDC):
                    w.matmul(PB[2 + dc // 4][:, (dc % 4) * 128:(dc % 4 + 1) * 128],
                             z1t[:, dc * 128:(dc + 1) * 128],
                             ident_s[:, :], start=True, stop=True,
                             is_transpose=True).then_inc("pe", 1)
                w.cp(f"e_petr_{i}", "pe")
        if en == "gpsimd":
            w.wait(f"e_agst_{NT - 1}")
            w.collective_compute("AllGather", ALU.bypass, replica_groups=RG,
                                 ins=[agx_in.ap()], outs=[agx_out.ap()]).then_inc("cc", 1)
            w.cp("agx_done", "cc")

    def ph2(en, w):
        if en == "sync":
            w.wait("e_sa_done")
            w.wait("e_sb_done")
            w.wait("e_sf_done")
            for i in range(NT):
                lsem = "la" if i % 2 == 0 else "lb"
                if i >= 2:
                    w.wait(f"r_lg_{i - 2}")
                w.dma_start(
                    out=trs[:, i % 2, :, :],
                    in_=xnfm_d.ap().rearrange("(a p) t -> p a t", p=128)[:, :, i * 128:(i + 1) * 128],
                ).then_inc(lsem, 16)
                w.cp(f"r_fm_{i}", lsem)
            if dbg_rt is not None:
                w.wait(f"r_lg_{NT - 1}")
                w.dma_start(out=dbg_rt.ap()[:, 370:498],
                            in_=lgbuf[:, :, :].rearrange("p a b -> p (a b)")).then_inc("st", 16)
                w.cp("dbg_lg1", "st")
                w.wait("r_sub")
                w.dma_start(out=dbg_rt.ap()[:, 498:626],
                            in_=routc[:, :, :].rearrange("p a b -> p (a b)")).then_inc("st", 16)
                w.cp("dbg_lg2", "st")
                w.wait("r_exp")
                w.dma_start(out=dbg_rt.ap()[:, 626:754],
                            in_=routc2[:, :, :].rearrange("p a b -> p (a b)")).then_inc("st", 16)
                w.cp("dbg_lg3", "st")
                w.wait("r_gidx")
                w.dma_start(out=dbg_rt.ap()[:, 0:64], in_=recs[:, :]).then_inc("st", 16)
                w.dma_start(out=dbg_rt.ap()[:, 64:192], in_=prefx[:, :]).then_inc("st", 16)
                w.dma_start(out=dbg_rt.ap()[:, 192:320], in_=offsb[:, :]).then_inc("st", 16)
                w.dma_start(out=dbg_rt.ap()[:, 320:328], in_=offmb[:, :]).then_inc("st", 16)
                w.dma_start(out=dbg_rt.ap()[:, 328:329], in_=tsum[:, :], allow_slow_non_contiguous=True).then_inc("st", 16)
                w.dma_start(out=dbg_rt.ap()[:, 329:330], in_=offsc[:, :], allow_slow_non_contiguous=True).then_inc("st", 16)
                w.dma_start(out=dbg_rt.ap()[0:8, 330:338], in_=cnts[:, :]).then_inc("st", 16)
                w.dma_start(out=dbg_rt.ap()[:, 338:354], in_=r16a[:, :]).then_inc("st", 16)
                w.dma_start(out=dbg_rt.ap()[:, 354:370], in_=i16a[:, :]).then_inc("st", 16)
                w.cp("dbg_rt_st", "st")
        if en == "tensor":
            for i in range(NT):
                w.wait(f"r_fm_{i}")
                if i >= 4:
                    w.wait(f"r_lg_{i - 4}")
                for dc in range(NDC):
                    w.matmul(PB[4 + i % 4][:, 0:8],
                             trs[:, i % 2, dc, :], rw16[:, dc, :],
                             start=(dc == 0), stop=(dc == NDC - 1)).then_inc("pe", 1)
                w.cp(f"r_mm_{i}", "pe")
            w.wait("r_mask")
            w.matmul(PB[5][:, 0:128], ustr_s[:, :],
                     routc[:, :, :].rearrange("p a b -> p (a b)"),
                     start=True, stop=True).then_inc("pe", 1)
            w.matmul(PB[5][:, 128:129],
                     routc[:, :, :].rearrange("p a b -> p (a b)"),
                     onesc_s[:, :], start=True, stop=True).then_inc("pe", 1)
            w.cp("r_pfx", "pe")
            w.wait("r_pfxe")
            w.matmul(PB[5][:, 129:130], ctile_s[:, :], tsum[:, :],
                     start=True, stop=True).then_inc("pe", 1)
            w.cp("r_offs", "pe")
            w.wait("r_offse")
            w.matmul(PB[5][0:1, 130:258], offsc[:, :], ident_s[:, :],
                     start=True, stop=True, is_transpose=True).then_inc("pe", 1)
            w.cp("r_offtr", "pe")
            w.wait("r_offre")
            w.matmul(PB[6][:, 0:128], onesr_s[0:1, :], offsr[0:1, :],
                     start=True, stop=True).then_inc("pe", 1)
            w.matmul(PB[6][0:1, 128:256], onesc_s[:, 0:1],
                     routc[:, :, :].rearrange("p a b -> p (a b)"),
                     start=True, stop=True).then_inc("pe", 1)
            w.cp("r_offb", "pe")
            w.wait("r_cnts_ld")
            w.wait("r_cnt8")
            w.matmul(PB[6][0:1, 256:264], selprev[:, :], cnts[:, :],
                     start=True, stop=True).then_inc("pe", 1)
            w.cp("r_offm_mm", "pe")
            w.wait("r_offm")
            w.matmul(PB[6][:, 264:272], onesr_s[0:1, :], offm[0:1, :],
                     start=True, stop=True).then_inc("pe", 1)
            w.cp("r_offmb_mm", "pe")
        if en == "vector":
            for i in range(NT):
                w.wait(f"r_mm_{i}")
                w.tensor_scalar(lgbuf[:, i, :], PB[4 + i % 4][:, 0:8],
                                stat[:, i, 5:6], None, ALU.mult).then_inc("ve", 1)
                w.cp(f"r_lg_{i}", "ve")
            L3 = lgbuf[:, :, :]
            i8 = iota8t[:, :].rearrange("p (a b) -> p a b", b=8)
            w.wait("ph0_iota")
            ssync(w, "ve")
            w.tensor_reduce(tmpc[:, :], L3, AX.X, ALU.max, negate=True).then_inc("ve", 1)
            ssync(w, "ve")
            a_, b_ = bass.broadcast_tensor_aps(L3, tmpc[:, :, None])
            w.tensor_tensor(routc[:, :, :], a_, b_, ALU.add).then_inc("ve", 1)
            w.cp("r_sub", "ve")
            w.wait("r_exp")
            w.tensor_reduce(tmpc[:, :], routc2[:, :, :], AX.X, ALU.add).then_inc("ve", 1)
            ssync(w, "ve")
            w.reciprocal(tmpc2[:, :], tmpc[:, :]).then_inc("ve", 1)
            ssync(w, "ve")
            if dbg_rt is not None:
                w.wait("dbg_lg1")
            a_, b_ = bass.broadcast_tensor_aps(routc2[:, :, :], tmpc2[:, :, None])
            w.tensor_tensor(lgbuf[:, :, :], a_, b_, ALU.mult).then_inc("ve", 1)
            w.tensor_reduce(w1c[:, :], lgbuf[:, :, :], AX.X, ALU.max).then_inc("ve", 1)
            ssync(w, "ve")
            if dbg_rt is not None:
                w.wait("dbg_lg2")
            a_, b_ = bass.broadcast_tensor_aps(lgbuf[:, :, :], w1c[:, :, None])
            w.tensor_tensor(routc[:, :, :], a_, b_, ALU.is_ge).then_inc("ve", 1)
            if dbg_rt is not None:
                w.wait("dbg_lg3")
            w.tensor_scalar(routc2[:, :, :], routc[:, :, :], -BIG, BIG, ALU.mult,
                            ALU.add).then_inc("ve", 1)
            w.tensor_tensor(routc2[:, :, :], i8, routc2[:, :, :],
                            ALU.add).then_inc("ve", 1)
            w.tensor_reduce(i16a[:, :], routc2[:, :, :], AX.X, ALU.min).then_inc("ve", 1)
            ssync(w, "ve")
            a_, b_ = bass.broadcast_tensor_aps(i8, i16a[:, :, None])
            w.tensor_tensor(routc[:, :, :], a_, b_, ALU.is_equal).then_inc("ve", 1)
            w.tensor_scalar(routc2[:, :, :], routc[:, :, :], -BIG, None,
                            ALU.mult).then_inc("ve", 1)
            w.tensor_tensor(routc2[:, :, :], lgbuf[:, :, :], routc2[:, :, :],
                            ALU.add).then_inc("ve", 1)
            w.tensor_reduce(w2c[:, :], routc2[:, :, :], AX.X, ALU.max).then_inc("ve", 1)
            ssync(w, "ve")
            a_, b_ = bass.broadcast_tensor_aps(routc2[:, :, :], w2c[:, :, None])
            w.tensor_tensor(routc2[:, :, :], a_, b_, ALU.is_ge).then_inc("ve", 1)
            w.tensor_scalar(routc2[:, :, :], routc2[:, :, :], -BIG, BIG, ALU.mult,
                            ALU.add).then_inc("ve", 1)
            w.tensor_tensor(routc2[:, :, :], i8, routc2[:, :, :],
                            ALU.add).then_inc("ve", 1)
            w.tensor_reduce(i16b[:, :], routc2[:, :, :], AX.X, ALU.min).then_inc("ve", 1)
            ssync(w, "ve")
            a_, b_ = bass.broadcast_tensor_aps(i8, i16b[:, :, None])
            w.tensor_tensor(routc2[:, :, :], a_, b_, ALU.is_equal).then_inc("ve", 1)
            w.tensor_tensor(routc[:, :, :], routc[:, :, :], routc2[:, :, :],
                            ALU.add).then_inc("ve", 1)
            w.cp("r_mask", "ve")
            w.tensor_tensor(tmpc[:, :], w1c[:, :], w2c[:, :], ALU.add).then_inc("ve", 1)
            ssync(w, "ve")
            w.reciprocal(tmpc[:, :], tmpc[:, :]).then_inc("ve", 1)
            ssync(w, "ve")
            w.tensor_tensor(w1c[:, :], w1c[:, :], tmpc[:, :], ALU.mult).then_inc("ve", 1)
            w.tensor_tensor(w2c[:, :], w2c[:, :], tmpc[:, :], ALU.mult).then_inc("ve", 1)
            w.cp("r_w", "ve")
            w.wait("r_pfx")
            w.tensor_copy(prefx[:, :], PB[5][:, 0:128]).then_inc("ve", 1)
            w.tensor_copy(tsum[:, :], PB[5][:, 128:129]).then_inc("ve", 1)
            w.cp("r_pfxe", "ve")
            w.wait("r_offs")
            w.tensor_copy(offsc[:, :], PB[5][:, 129:130]).then_inc("ve", 1)
            w.cp("r_offse", "ve")
            w.wait("r_offtr")
            w.tensor_copy(offsr[0:1, :], PB[5][0:1, 130:258]).then_inc("ve", 1)
            w.cp("r_offre", "ve")
            w.wait("r_offb")
            w.tensor_copy(offsb[:, :], PB[6][:, 0:128]).then_inc("ve", 1)
            w.tensor_copy(cntrow[0:1, :], PB[6][0:1, 128:256]).then_inc("ve", 1)
            w.tensor_reduce(offm[0:1, 0:8],
                            bass.AP(cntrow, 0, [[128, 1], [1, 8], [8, 16]]),
                            AX.X, ALU.add).then_inc("ve", 1)
            w.cp("r_cnt8", "ve")
            w.wait("r_offm_mm")
            w.tensor_copy(offm[0:1, :], PB[6][0:1, 256:264]).then_inc("ve", 1)
            w.cp("r_offm", "ve")
            w.wait("r_offmb_mm")
            w.tensor_copy(offmb[:, :], PB[6][:, 264:272]).then_inc("ve", 1)
            pf3 = prefx[:, :].rearrange("p (a b) -> p a b", b=8)
            ob3 = offsb[:, :].rearrange("p (a b) -> p a b", b=8)
            w.tensor_tensor(routc2[:, :, :], pf3, ob3, ALU.add).then_inc("ve", 1)
            w.tensor_tensor(routc2[:, :, :], routc2[:, :, :],
                            bass.AP(offmb, 0, [[8, 128], [0, 16], [1, 8]]),
                            ALU.add).then_inc("ve", 1)
            w.cp("r_rank3", "ve")
            for (idxs, ranks) in [(i16a, r16a), (i16b, r16b)]:
                a_, b_ = bass.broadcast_tensor_aps(i8, idxs[:, :, None])
                w.tensor_tensor(lgbuf[:, :, :], a_, b_, ALU.is_equal).then_inc("ve", 1)
                w.tensor_tensor(lgbuf[:, :, :], lgbuf[:, :, :], routc2[:, :, :],
                                ALU.mult).then_inc("ve", 1)
                w.tensor_reduce(ranks[:, :], lgbuf[:, :, :], AX.X,
                                ALU.add).then_inc("ve", 1)
            w.cp("r_rank", "ve")
            ssync(w, "ve")
            w.tensor_copy(recs[:, 0:16], i16a[:, :]).then_inc("ve", 1)
            w.tensor_copy(recs[:, 16:32], r16a[:, :]).then_inc("ve", 1)
            w.tensor_copy(recs[:, 32:48], i16b[:, :]).then_inc("ve", 1)
            w.tensor_copy(recs[:, 48:64], r16b[:, :]).then_inc("ve", 1)
            w.cp("r_recs", "ve")
            ZROW = float(NCORE * CAP)
            for (idxs, ranks, gout) in [(i16a, r16a, gidx1), (i16b, r16b, gidx2)]:
                w.tensor_scalar(tmpc[:, :], ranks[:, :], float(CAP), None,
                                ALU.is_lt).then_inc("ve", 1)
                w.tensor_scalar(tmpc2[:, :], idxs[:, :], float(CAP), None,
                                ALU.mult).then_inc("ve", 1)
                ssync(w, "ve")
                w.tensor_tensor(tmpc2[:, :], tmpc2[:, :], ranks[:, :],
                                ALU.add).then_inc("ve", 1)
                ssync(w, "ve")
                w.tensor_scalar(tmpc2[:, :], tmpc2[:, :], -ZROW, None,
                                ALU.add).then_inc("ve", 1)
                ssync(w, "ve")
                w.tensor_tensor(tmpc2[:, :], tmpc2[:, :], tmpc[:, :],
                                ALU.mult).then_inc("ve", 1)
                ssync(w, "ve")
                w.tensor_scalar(tmpc2[:, :], tmpc2[:, :], ZROW, None,
                                ALU.add).then_inc("ve", 1)
                ssync(w, "ve")
                w.tensor_copy(gout[:, :], tmpc2[:, :]).then_inc("ve", 1)
            w.cp("r_gidx", "ve")
        if en == "scalar":
            w.wait("r_sub")
            w.activation(routc2[:, :, :], routc[:, :, :], AF.Exp).then_inc("ac", 1)
            w.cp("r_exp", "ac")
        if en == "gpsimd":
            w.wait("r_cnt8")
            w.dma_start(out=agcnt_in.ap(), in_=offm[0:1, 0:8]).then_inc("gr", 16)
            w.cp("r_cnt_st", "gr")
            w.wait("r_cnt_st")
            w.collective_compute("AllGather", ALU.bypass, replica_groups=RG,
                                 ins=[agcnt_in.ap()], outs=[agcnt_out.ap()]).then_inc("ce", 1)
            w.cp("agcnt_done", "ce")
            w.wait("agcnt_done")
            w.dma_start(out=cnts[:, :], in_=agcnt_out.ap()).then_inc("gr", 16)
            w.cp("r_cnts_ld", "gr")
            w.wait("r_recs")
            w.wait("r_offm_mm")
            w.dma_start(out=agrec_in.ap(), in_=recs[:, :]).then_inc("gr", 16)
            w.cp("r_rec_st", "gr")
            w.wait("r_rec_st")
            w.collective_compute("AllGather", ALU.bypass, replica_groups=RG,
                                 ins=[agrec_in.ap()], outs=[agrec_out.ap()]).then_inc("cd", 1)
            w.cp("agrec_done", "cd")

    def ph3q(en, w, q):
        t0 = q * Q
        if en == "sync":
            if q == 0:
                w.wait("e_sa_done")
                w.wait("e_sb_done")
                w.wait("e_sf_done")
            if q >= 1:
                w.wait(f"a_rwkv_{q - 1}_7")
            w.dma_start(out=z1h[:, :, :],
                        in_=z1fm_d.ap().rearrange("(a p) t -> p a t", p=128)[:, :, t0:t0 + Q],
                        ).then_inc("ld", 16)
            w.cp(f"a_z1_{q}", "ld")
            if dbg_sc is not None and q == 0:
                w.wait("ph0_haloc")
                w.wait("ph0_tfm")
                w.dma_start(out=dbg_h16.ap(), in_=haloc[:, :]).then_inc("st", 16)
                w.dma_start(out=dbg_h.ap()[:, 8:12], in_=flagc[:, :], allow_slow_non_contiguous=True).then_inc("st", 16)
                w.dma_start(out=dbg_h.ap()[:, 12:20], in_=vcol[:, 0, :], allow_slow_non_contiguous=True).then_inc("st", 16)
                w.dma_start(out=dbg_h.ap()[:, 20:28], in_=vcol[:, 1, :], allow_slow_non_contiguous=True).then_inc("st", 16)
                w.dma_start(out=dbg_h.ap()[:, 28:36], in_=tfm[:, :], allow_slow_non_contiguous=True).then_inc("st", 16)
                w.dma_start(out=dbg_h.ap()[:, 36:44], in_=ewcol[:, :], allow_slow_non_contiguous=True).then_inc("st", 16)
                w.wait("h1_z")
                w.dma_start(out=dbg_hr.ap()[0:1, :], in_=hrow[0:1, :]).then_inc("st", 16)
                w.dma_start(out=dbg_hr.ap()[1:2, :], in_=hrow2[0:1, :]).then_inc("st", 16)
                w.dma_start(out=dbg_hr.ap()[2:3, 0:7], in_=hstat[0:1, 0:7]).then_inc("st", 16)
                w.cp("dbg_h_st", "st")
                w.wait(f"a_mixk_{q}")
                w.dma_start(out=dbg_mx.ap()[:, 0:4096],
                            in_=mixa[:, :, :]).then_inc("st", 16)
                w.wait(f"a_mixv_{q}")
                w.dma_start(out=dbg_mx.ap()[:, 4096:8192],
                            in_=mixb[:, :, :]).then_inc("st", 16)
                w.cp("dbg_mx_st", "st")
                for dc2 in range(NDC):
                    w.wait(f"a_kvev_{q}_{dc2}")
                    w.dma_start(out=dbg_sc.ap()[:, 12288 + dc2 * 512:12288 + (dc2 + 1) * 512],
                                in_=kch[:, :]).then_inc("st", 16)
                    w.dma_start(out=dbg_sc.ap()[:, 16384 + dc2 * 512:16384 + (dc2 + 1) * 512],
                                in_=vch[:, :]).then_inc("st", 16)
                    w.cp(f"dbg_kv_{dc2}", "st")
                    w.wait(f"a_scan_{q}_{dc2}")
                    w.dma_start(out=dbg_sc.ap()[:, dc2 * 512:(dc2 + 1) * 512],
                                in_=vch[:, :]).then_inc("st", 16)
                    w.dma_start(out=dbg_sc.ap()[:, 4096 + dc2 * 512:4096 + (dc2 + 1) * 512],
                                in_=kch[:, :]).then_inc("st", 16)
                    w.wait(f"a_recip_{q}_{dc2}")
                    w.dma_start(out=dbg_sc.ap()[:, 8192 + dc2 * 512:8192 + (dc2 + 1) * 512],
                                in_=ekb[:, :]).then_inc("st", 16)
                    w.cp(f"dbg_sc_{dc2}", "st")
                w.wait("a_rwkv_0_7")
                w.dma_start(out=dbg_wk.ap(),
                            in_=wkvh[:, :, :]).then_inc("st", 16)
                w.cp("dbg_wk_st", "st")
            for j in range(4):
                m = q * 4 + j
                lsem = "la" if j % 2 == 0 else "lb"
                if m < 2:
                    w.wait(f"r_lg_{NT - 1}")
                if m >= 2:
                    w.wait(f"a_x2_{(m - 2) // 4}_{(m - 2) % 4}")
                w.dma_start(out=xt[:, j % 2, :],
                            in_=xn32_d.ap()[t0 + j * 128: t0 + j * 128 + 128, :],
                            ).then_inc(lsem, 16)
                w.cp(f"a_xnrow_{q}_{j}", lsem)
            for j in range(4):
                ssem = "sa" if j % 2 == 0 else "sb"
                w.wait(f"a_x2_{q}_{j}")
                if q == 0 and j == 1:
                    w.wait("agx_done")
                w.dma_start(out=x2_d.ap()[t0 + j * 128: t0 + j * 128 + 128, :],
                            in_=xnt[:, j % 2, :]).then_inc(ssem, 16)
                w.cp(f"a_x2st_{q}_{j}", ssem)
                w.wait(f"a_ztre_{q}_{j}")
                w.dma_start(
                    out=z2fm_d.ap().rearrange("(a p) t -> p a t", p=128)[:, :, t0 + j * 128:t0 + (j + 1) * 128],
                    in_=trs[:, j % 2, :, :]).then_inc(ssem, 16)
                w.cp(f"a_z2st_{q}_{j}", ssem)
            if q == NQ - 1:
                w.dma_start(out=agx2_in.ap(), in_=xnt[127:128, 1, :]).then_inc("sc", 16)
                w.cp("x2row_pub", "sc")
                w.wait("a_scstw_3")
                w.dma_start(out=agcar_in.ap(), in_=scst[:, :]).then_inc("sc", 16)
                w.cp("car_pub", "sc")
        if en == "gpsimd":
            if q >= 1:
                w.wait(f"a_womm_{q - 1}_3")
            w.dma_start(out=wkc[:, :, :],
                        in_=wr_d.ap().rearrange("(a p) n -> p a n", p=128)).then_inc("gc", 16)
            w.cp(f"w_wr_{q}", "gc")
            w.wait(f"a_rmm_{q}_7")
            w.dma_start(out=wkc[:, :, :],
                        in_=wo_d.ap().rearrange("(a p) n -> p a n", p=128)).then_inc("gc", 16)
            w.cp(f"w_wo_{q}", "gc")
        if en == "vector":
            w.wait(f"a_z1_{q}")
            if q == 0:
                w.wait("ph0_vcol")
                w.wait("ph0_haloc")
            for dc in range(NDC):
                w.tensor_scalar(xn1b[:, dc, 1:Q + 1], z1h[:, dc, :],
                                vc("g1", dc), vc("b1", dc), ALU.mult,
                                ALU.add).then_inc("ve", 1)
            if q == 0:
                w.tensor_copy(xn1b[:, :, 0:1], haloc[:, :, None]).then_inc("ve", 1)
            else:
                w.tensor_copy(xn1b[:, :, 0:1], ynprev[:, :, None]).then_inc("ve", 1)
            ssync(w, "ve")
            w.tensor_copy(ynprev[:, :, None], xn1b[:, :, Q:Q + 1]).then_inc("ve", 1)
            ssync(w, "ve")
            for dc in range(NDC):
                w.tensor_tensor(z1h[:, dc, :], xn1b[:, dc, 1:Q + 1],
                                xn1b[:, dc, 0:Q], ALU.subtract).then_inc("ve", 1)
            w.cp(f"a_diff_{q}", "ve")
            for dc in range(NDC):
                w.scalar_tensor_tensor(mixa[:, dc, :], z1h[:, dc, :], vc("mk", dc),
                                       xn1b[:, dc, 0:Q], ALU.mult,
                                       ALU.add).then_inc("ve", 1)
            w.cp(f"a_mixk_{q}", "ve")
            for dc in range(NDC):
                w.scalar_tensor_tensor(mixb[:, dc, :], z1h[:, dc, :], vc("mv", dc),
                                       xn1b[:, dc, 0:Q], ALU.mult,
                                       ALU.add).then_inc("ve", 1)
            w.cp(f"a_mixv_{q}", "ve")
            for dc2 in range(NDC):
                w.wait(f"a_kvmm_{q}_{dc2}")
                if dc2 >= 1:
                    w.wait(f"a_recip_{q}_{dc2 - 1}")
                    if dbg_sc is not None and q == 0:
                        w.wait(f"dbg_sc_{dc2 - 1}")
                w.tensor_copy(kch[:, :], PB[0 + (dc2 % 2) * 2][:, :]).then_inc("ve", 1)
                w.tensor_copy(vch[:, :], PB[1 + (dc2 % 2) * 2][:, :]).then_inc("ve", 1)
                w.cp(f"a_kvev_{q}_{dc2}", "ve")
                w.wait(f"a_ek_{q}_{dc2}")
                w.tensor_tensor(ekv[:, :], ekb[:, :], vch[:, :], ALU.mult).then_inc("ve", 1)
                w.tensor_scalar(ewb[:, :], kch[:, :], 0.0, ewcol[:, dc2:dc2 + 1],
                                ALU.mult, ALU.add).then_inc("ve", 1)
                ini_a = scst[:, dc2:dc2 + 1] if q > 0 else 0.0
                ini_b = scst[:, 8 + dc2:8 + dc2 + 1] if q > 0 else 0.0
                if dbg_sc is not None and q == 0:
                    w.wait(f"dbg_kv_{dc2}")
                w.tensor_tensor_scan(vch[:, :], ewb[:, :], ekv[:, :], ini_a,
                                     ALU.mult, ALU.add).then_inc("ve", 1)
                w.tensor_tensor_scan(kch[:, :], ewb[:, :], ekb[:, :], ini_b,
                                     ALU.mult, ALU.add).then_inc("ve", 1)
                w.tensor_copy(scst[:, dc2:dc2 + 1], vch[:, Q - 1:Q]).then_inc("ve", 1)
                ssync(w, "ve")
                w.tensor_copy(scst[:, 8 + dc2:8 + dc2 + 1],
                              kch[:, Q - 1:Q]).then_inc("ve", 1)
                if q == 0:
                    w.tensor_copy(corra[:, dc2, :], vch[:, 0:CORRW]).then_inc("ve", 1)
                    w.tensor_copy(corrb[:, dc2, :], kch[:, 0:CORRW]).then_inc("ve", 1)
                w.cp(f"a_scan_{q}_{dc2}", "ve")
                if q == NQ - 1 and dc2 == NDC - 1:
                    w.cp("a_scstw_3", "ve")
                w.wait(f"a_recip_{q}_{dc2}")
                w.tensor_tensor(wkvh[:, dc2, :], vch[:, :], ekb[:, :],
                                ALU.mult).then_inc("ve", 1)
                w.cp(f"a_wkv_{q}_{dc2}", "ve")
            w.wait(f"a_kvmm_{q}_7")
            if dbg_sc is not None and q == 0:
                w.wait("dbg_mx_st")
            for dc in range(NDC):
                w.scalar_tensor_tensor(mixa[:, dc, :], z1h[:, dc, :], vc("mr", dc),
                                       xn1b[:, dc, 0:Q], ALU.mult,
                                       ALU.add).then_inc("ve", 1)
            w.cp(f"a_rmix_{q}", "ve")
            for dc2 in range(NDC):
                w.wait(f"a_rsig_{q}_{dc2}")
                if q == 0:
                    w.tensor_copy(rcorr[:, dc2, :], z1h[:, dc2, 0:CORRW]).then_inc("ve", 1)
                w.tensor_tensor(wkvh[:, dc2, :], wkvh[:, dc2, :], z1h[:, dc2, :],
                                ALU.mult).then_inc("ve", 1)
                w.cp(f"a_rwkv_{q}_{dc2}", "ve")
            for j in range(4):
                w.wait(f"a_womm_{q}_{j}")
                w.wait(f"a_xnrow_{q}_{j}")
                w.tensor_tensor(xnt[:, j % 2, 0:512], PB[4 + (j % 2) * 2][:, :],
                                xt[:, j % 2, 0:512], ALU.add).then_inc("ve", 1)
                w.tensor_tensor(xnt[:, j % 2, 512:1024], PB[5 + (j % 2) * 2][:, :],
                                xt[:, j % 2, 512:1024], ALU.add).then_inc("ve", 1)
                w.cp(f"a_x2_{q}_{j}", "ve")
                w.tensor_reduce(tmpc[:, 0:1], xnt[:, j % 2, :], AX.X, ALU.add,
                                negate=True).then_inc("ve", 1)
                w.cp(f"a_c2_{q}_{j}", "ve")
                w.wait(f"a_sq2_{q}_{j}")
                w.tensor_scalar(tmpc[:, 0:1], tmpc[:, 0:1], 1.0 / D, None,
                                ALU.mult).then_inc("ve", 1)
                w.tensor_scalar(tmpc[:, 1:2], tmpc[:, 1:2], 1.0 / D, None,
                                ALU.mult).then_inc("ve", 1)
                ssync(w, "ve")
                w.tensor_tensor(tmpc[:, 2:3], tmpc[:, 0:1], tmpc[:, 0:1],
                                ALU.mult).then_inc("ve", 1)
                ssync(w, "ve")
                w.tensor_tensor(tmpc[:, 2:3], tmpc[:, 1:2], tmpc[:, 2:3],
                                ALU.subtract).then_inc("ve", 1)
                w.cp(f"a_var2_{q}_{j}", "ve")
                w.wait(f"a_rstd2_{q}_{j}")
                w.tensor_tensor(tmpc[:, 3:4], tmpc[:, 0:1], tmpc[:, 2:3],
                                ALU.mult).then_inc("ve", 1)
                w.cp(f"a_bias2_{q}_{j}", "ve")
                if j >= 1:
                    w.wait(f"a_ztr_{q}_{j - 1}")
                elif q >= 1:
                    w.wait(f"a_ztr_{q - 1}_3")
                ssync(w, "ve")
                w.tensor_scalar(z1t[:, :], xnt[:, j % 2, :], tmpc[:, 2:3],
                                tmpc[:, 3:4], ALU.mult, ALU.add).then_inc("ve", 1)
                w.cp(f"a_z2_{q}_{j}", "ve")
                w.wait(f"a_ztr_{q}_{j}")
                if q == 0 and j == 0:
                    w.wait("r_offmb_mm")
                else:
                    pj = q * 4 + j - 1
                    w.wait(f"a_z2st_{pj // 4}_{pj % 4}")
                w.tensor_copy(trs[:, j % 2, 0:4, :], p3(PB[0], 128)).then_inc("ve", 1)
                w.tensor_copy(trs[:, j % 2, 4:8, :], p3(PB[1], 128)).then_inc("ve", 1)
                w.cp(f"a_ztre_{q}_{j}", "ve")
        if en == "scalar":
            for dc2 in range(NDC):
                w.wait(f"a_kvev_{q}_{dc2}")
                w.activation(ekb[:, :], kch[:, :], AF.Exp).then_inc("ac", 1)
                if q == 0:
                    w.wait("ph0_tfm")
                    w.activation(ekb[:, 0:1], kch[:, 0:1], AF.Exp,
                                 bias=tfm[:, dc2:dc2 + 1]).then_inc("ac", 1)
                w.cp(f"a_ek_{q}_{dc2}", "ac")
                w.wait(f"a_scan_{q}_{dc2}")
                w.activation(ekb[:, :], kch[:, :], AF.Ln, bias=C_WKV()).then_inc("ac", 1)
                w.activation(ekb[:, :], ekb[:, :], AF.Exp, scale=C_N1()).then_inc("ac", 1)
                w.cp(f"a_recip_{q}_{dc2}", "ac")
            for dc2 in range(NDC):
                w.wait(f"a_rmm_{q}_{dc2}")
                w.activation(z1h[:, dc2, :], PB[4 + dc2 % 4][:, :],
                             AF.Sigmoid).then_inc("ac", 1)
                w.cp(f"a_rsig_{q}_{dc2}", "ac")
            for j in range(4):
                w.wait(f"a_c2_{q}_{j}")
                w.activation(sqj[:, :], xnt[:, j % 2, :], AF.Square,
                             accum_out=tmpc[:, 1:2]).then_inc("ac", 1)
                w.cp(f"a_sq2_{q}_{j}", "ac")
                w.wait(f"a_var2_{q}_{j}")
                w.activation(tmpc[:, 2:3], tmpc[:, 2:3], AF.Ln,
                             bias=C_LN()).then_inc("ac", 1)
                ssync(w, "ac")
                w.activation(tmpc[:, 2:3], tmpc[:, 2:3], AF.Exp,
                             scale=C_NH()).then_inc("ac", 1)
                w.cp(f"a_rstd2_{q}_{j}", "ac")
        if en == "tensor":
            for dc2 in range(NDC):
                if dc2 == 0:
                    w.wait(f"a_mixk_{q}")
                    w.wait(f"a_mixv_{q}")
                    w.wait("w_wk")
                    w.wait("w_wv")
                    if q == 0:
                        w.wait("r_gidx")
                        w.wait(f"e_tr_{NT - 1}")
                    else:
                        w.wait(f"a_ztre_{q - 1}_3")
                if dc2 >= 2:
                    w.wait(f"a_kvev_{q}_{dc2 - 2}")
                for dcK in range(NDC):
                    w.matmul(PB[0 + (dc2 % 2) * 2][:, :],
                             wka[:, dcK, dc2 * 128:(dc2 + 1) * 128],
                             mixa[:, dcK, :],
                             start=(dcK == 0), stop=(dcK == NDC - 1)).then_inc("pe", 1)
                for dcK in range(NDC):
                    w.matmul(PB[1 + (dc2 % 2) * 2][:, :],
                             wkb[:, dcK, dc2 * 128:(dc2 + 1) * 128],
                             mixb[:, dcK, :],
                             start=(dcK == 0), stop=(dcK == NDC - 1)).then_inc("pe", 1)
                w.cp(f"a_kvmm_{q}_{dc2}", "pe")
            for dc2 in range(NDC):
                if dc2 == 0:
                    w.wait(f"a_rmix_{q}")
                    w.wait(f"w_wr_{q}")
                    if q == 0:
                        w.wait("r_gidx")
                    else:
                        w.wait(f"a_x2_{q - 1}_3")
                if dc2 >= 4:
                    w.wait(f"a_rsig_{q}_{dc2 - 4}")
                for dcK in range(NDC):
                    w.matmul(PB[4 + dc2 % 4][:, :],
                             wkc[:, dcK, dc2 * 128:(dc2 + 1) * 128],
                             mixa[:, dcK, :],
                             start=(dcK == 0), stop=(dcK == NDC - 1)).then_inc("pe", 1)
                w.cp(f"a_rmm_{q}_{dc2}", "pe")
            for j in range(4):
                if j == 0:
                    w.wait(f"w_wo_{q}")
                    w.wait(f"a_rwkv_{q}_7")
                    w.wait(f"a_rsig_{q}_7")
                if j >= 2:
                    w.wait(f"a_x2_{q}_{j - 2}")
                for n in range(2):
                    for dcK in range(NDC):
                        w.matmul(PB[4 + (j % 2) * 2 + n][:, :],
                                 wkvh[:, dcK, j * 128:(j + 1) * 128],
                                 wkc[:, dcK, n * 512:(n + 1) * 512],
                                 start=(dcK == 0), stop=(dcK == NDC - 1)).then_inc("pe", 1)
                w.cp(f"a_womm_{q}_{j}", "pe")
                w.wait(f"a_z2_{q}_{j}")
                if not (q == 0 and j == 0):
                    pj = q * 4 + j - 1
                    w.wait(f"a_ztre_{pj // 4}_{pj % 4}")
                for dc in range(NDC):
                    w.matmul(PB[dc // 4][:, (dc % 4) * 128:(dc % 4 + 1) * 128],
                             z1t[:, dc * 128:(dc + 1) * 128],
                             ident_s[:, :], start=True, stop=True,
                             is_transpose=True).then_inc("pe", 1)
                w.cp(f"a_ztr_{q}_{j}", "pe")

    def ph3fix(en, w):
        if en == "gpsimd":
            w.wait("car_pub")
            w.collective_compute("AllGather", ALU.bypass, replica_groups=RG,
                                 ins=[agcar_in.ap()], outs=[agcar_out.ap()[0:NCORE * 128, :]]).then_inc("ce", 1)
            w.cp("agcar_done", "ce")
            w.wait("agcar_done")
            w.memset(carg[:, :], 0.0).then_inc("po", 1)
            w.cp("car_z", "po")
            w.wait("car_z")
            w.dma_start(out=agcar_out.ap()[NCORE * 128:NCORE * 128 + 1, :],
                        in_=carg[0:1, :]).then_inc("gp", 16)
            w.cp("car_zrow", "gp")
            w.wait("car_zrow")
            w.indirect_dma_start(out=carg[:, :], out_offset=None,
                                 in_=agcar_out.ap(),
                                 in_offset=IndirectOffsetOnAxis(caridx[:, :], axis=0),
                                 ).then_inc("gp", 16)
            w.cp("car_g", "gp")
            w.wait("car_pub")
            w.collective_compute("AllGather", ALU.bypass, replica_groups=RG,
                                 ins=[agx2_in.ap()], outs=[agx2_out.ap()]).then_inc("ce", 1)
            w.cp("agx2_done", "ce")
            w.wait("agx2_done")
            w.wait("cf_p_0")
            w.dma_start(out=xgf32[0:8, :], in_=agx2_out.ap()).then_inc("gp", 16)
            w.cp("x2rows_ld", "gp")
        if en == "scalar":
            for dc in range(NDC):
                if dc == 0:
                    w.wait("car_g")
                    w.wait("ph0_iota")
                    w.wait("ph0_negt")
                w.activation(pcoef[:, :], iota256[:, :], AF.Exp,
                             scale=netd[:, dc:dc + 1]).then_inc("ac", 1)
                w.cp(f"cf_p_{dc}", "ac")
                w.wait(f"cf_ab_{dc}")
                w.activation(corrb[:, dc, :], corrb[:, dc, :], AF.Ln,
                             bias=C_WKV()).then_inc("ac", 1)
                ssync(w, "ac")
                w.activation(corrb[:, dc, :], corrb[:, dc, :], AF.Exp,
                             scale=C_N1()).then_inc("ac", 1)
                w.cp(f"cf_recip_{dc}", "ac")
            for j in range(2):
                w.wait(f"cf_c2_{j}")
                w.activation(sqj[:, :], xnt[:, j, :], AF.Square,
                             accum_out=tmpc[:, 1:2]).then_inc("ac", 1)
                w.cp(f"cf_sq2_{j}", "ac")
                w.wait(f"cf_var2_{j}")
                w.activation(tmpc[:, 2:3], tmpc[:, 2:3], AF.Ln,
                             bias=C_LN()).then_inc("ac", 1)
                ssync(w, "ac")
                w.activation(tmpc[:, 2:3], tmpc[:, 2:3], AF.Exp,
                             scale=C_NH()).then_inc("ac", 1)
                w.cp(f"cf_rstd2_{j}", "ac")
        if en == "vector":
            w.wait("y_hsel")
            w.wait("h1_z")
            w.tensor_copy(hrow[0:1, 0:512], PB[2][0:1, 0:512]).then_inc("ve", 1)
            w.tensor_copy(hrow[0:1, 512:1024], PB[3][0:1, 0:512]).then_inc("ve", 1)
            w.cp("y_hrow", "ve")
            for dc in range(NDC):
                w.wait(f"cf_p_{dc}")
                w.scalar_tensor_tensor(corra[:, dc, :], pcoef[:, :],
                                       carg[:, dc:dc + 1], corra[:, dc, :],
                                       ALU.mult, ALU.add).then_inc("ve", 1)
                w.scalar_tensor_tensor(corrb[:, dc, :], pcoef[:, :],
                                       carg[:, 8 + dc:8 + dc + 1], corrb[:, dc, :],
                                       ALU.mult, ALU.add).then_inc("ve", 1)
                w.cp(f"cf_ab_{dc}", "ve")
                w.wait(f"cf_recip_{dc}")
                w.tensor_tensor(corra[:, dc, :], corra[:, dc, :], corrb[:, dc, :],
                                ALU.mult).then_inc("ve", 1)
                w.tensor_tensor(cw[:, dc, :], corra[:, dc, :], rcorr[:, dc, :],
                                ALU.mult).then_inc("ve", 1)
            w.cp("cf_cw", "ve")
            for j in range(2):
                w.wait(f"cf_womm_{j}")
                w.wait(f"cf_xn_{j}")
                w.wait(f"a_x2st_3_{2 + j}")
                if j == 1:
                    w.wait("car_pub")
                w.tensor_tensor(xnt[:, j, 0:512], PB[4 + j * 2][:, :],
                                xt[:, j, 0:512], ALU.add).then_inc("ve", 1)
                w.tensor_tensor(xnt[:, j, 512:1024], PB[5 + j * 2][:, :],
                                xt[:, j, 512:1024], ALU.add).then_inc("ve", 1)
                w.cp(f"cf_x2_{j}", "ve")
                w.tensor_reduce(tmpc[:, 0:1], xnt[:, j, :], AX.X, ALU.add,
                                negate=True).then_inc("ve", 1)
                w.cp(f"cf_c2_{j}", "ve")
                w.wait(f"cf_sq2_{j}")
                w.tensor_scalar(tmpc[:, 0:1], tmpc[:, 0:1], 1.0 / D, None,
                                ALU.mult).then_inc("ve", 1)
                w.tensor_scalar(tmpc[:, 1:2], tmpc[:, 1:2], 1.0 / D, None,
                                ALU.mult).then_inc("ve", 1)
                ssync(w, "ve")
                w.tensor_tensor(tmpc[:, 2:3], tmpc[:, 0:1], tmpc[:, 0:1],
                                ALU.mult).then_inc("ve", 1)
                ssync(w, "ve")
                w.tensor_tensor(tmpc[:, 2:3], tmpc[:, 1:2], tmpc[:, 2:3],
                                ALU.subtract).then_inc("ve", 1)
                w.cp(f"cf_var2_{j}", "ve")
                w.wait(f"cf_rstd2_{j}")
                w.tensor_tensor(tmpc[:, 3:4], tmpc[:, 0:1], tmpc[:, 2:3],
                                ALU.mult).then_inc("ve", 1)
                w.cp(f"cf_bias2_{j}", "ve")
                if j == 1:
                    w.wait("cf_ztr_0")
                else:
                    w.wait(f"a_ztr_{NQ - 1}_3")
                ssync(w, "ve")
                w.tensor_scalar(z1t[:, :], xnt[:, j, :], tmpc[:, 2:3],
                                tmpc[:, 3:4], ALU.mult, ALU.add).then_inc("ve", 1)
                w.cp(f"cf_z2_{j}", "ve")
                w.wait(f"cf_ztr_{j}")
                if j == 0:
                    w.wait("a_z2st_3_3")
                else:
                    w.wait("cf_z2st_0")
                w.tensor_copy(trs[:, j, 0:4, :], p3(PB[0], 128)).then_inc("ve", 1)
                w.tensor_copy(trs[:, j, 4:8, :], p3(PB[1], 128)).then_inc("ve", 1)
                w.cp(f"cf_ztre_{j}", "ve")
        if en == "tensor":
            w.wait("x2rows_ld")
            w.wait(f"a_kvev_{NQ - 1}_7")
            w.matmul(PB[2][0:1, 0:512], predoh[:, :], xgf32[0:8, 0:512],
                     start=True, stop=True).then_inc("pe", 1)
            w.matmul(PB[3][0:1, 0:512], predoh[:, :], xgf32[0:8, 512:1024],
                     start=True, stop=True).then_inc("pe", 1)
            w.cp("y_hsel", "pe")
            for j in range(2):
                if j == 0:
                    w.wait("cf_cw")
                    w.wait(f"a_ztr_{NQ - 1}_3")
                    w.wait(f"a_x2_{NQ - 1}_3")
                for n in range(2):
                    for dcK in range(NDC):
                        w.matmul(PB[4 + j * 2 + n][:, :],
                                 cw[:, dcK, j * 128:(j + 1) * 128],
                                 wkc[:, dcK, n * 512:(n + 1) * 512],
                                 start=(dcK == 0), stop=(dcK == NDC - 1)).then_inc("pe", 1)
                w.cp(f"cf_womm_{j}", "pe")
                w.wait(f"cf_z2_{j}")
                if j == 0:
                    w.wait(f"a_ztre_{NQ - 1}_3")
                else:
                    w.wait("cf_ztre_0")
                for dc in range(NDC):
                    w.matmul(PB[dc // 4][:, (dc % 4) * 128:(dc % 4 + 1) * 128],
                             z1t[:, dc * 128:(dc + 1) * 128],
                             ident_s[:, :], start=True, stop=True,
                             is_transpose=True).then_inc("pe", 1)
                w.cp(f"cf_ztr_{j}", "pe")
        if en == "sync":
            for j in range(2):
                w.wait("cf_cw")
                w.wait(f"a_x2_3_{2 + j}")
                w.dma_start(out=xt[:, j, :],
                            in_=xn32_d.ap()[j * 128:(j + 1) * 128, :]).then_inc("la" if j == 0 else "lb", 16)
                w.cp(f"cf_xn_{j}", "la" if j == 0 else "lb")
            for j in range(2):
                ssem = "sa" if j == 0 else "sb"
                w.wait(f"cf_x2_{j}")
                w.dma_start(out=x2_d.ap()[j * 128:(j + 1) * 128, :],
                            in_=xnt[:, j, :]).then_inc(ssem, 16)
                w.cp(f"cf_x2st_{j}", ssem)
            for j in range(2):
                ssem = "sa" if j == 0 else "sb"
                w.wait(f"cf_ztre_{j}")
                w.dma_start(
                    out=z2fm_d.ap().rearrange("(a p) t -> p a t", p=128)[:, :, j * 128:(j + 1) * 128],
                    in_=trs[:, j, :, :]).then_inc(ssem, 16)
                w.cp(f"cf_z2st_{j}", ssem)

    def ph3yh(en, w):
        if en == "vector":
            w.wait("y_hrow")
            w.cp("yh_go", "ve")
        lnrow_chain(en, w, "yh", hrow[0:1, :], hrow2[0:1, :])
        if en == "vector":
            w.wait("yh_htr")
            for dc in range(NDC):
                w.tensor_scalar(yhaloc[:, dc:dc + 1], PB[2][:, dc * 2:dc * 2 + 1],
                                vc("g2", dc), vc("b2", dc), ALU.mult,
                                ALU.add).then_inc("ve", 1)
            ssync(w, "ve")
            w.tensor_scalar(yhaloc[:, :], yhaloc[:, :], flagc[:, 1:2], None,
                            ALU.mult).then_inc("ve", 1)
            w.cp("y_haloc", "ve")

    def ph3kk(en, w):
        if en == "gpsimd":
            w.wait("a_rwkv_3_7")
            w.wait("kk_rrwka_free")
            w.dma_start(out=wka[:, :, :],
                        in_=wfr_d.ap().rearrange("(a p) n -> p a n", p=128)).then_inc("ga", 16)
            w.cp("w_wfr", "ga")
            for g in range(8):
                for ic in range(8):
                    n_slot = g * 8 + ic
                    gsem = "ga" if ic % 2 == 0 else "gb"
                    if n_slot == 0:
                        w.wait("cf_womm_1")
                    if n_slot < 2:
                        w.wait("kk_rrmm_0_0")
                    if n_slot >= 2:
                        pg, pic = (n_slot - 2) // 8, (n_slot - 2) % 8
                        w.wait(f"kk_fkmm_{pg}_{pic}")
                        w.wait(f"kk_fv_{pg}_{pic}")
                    w.dma_start(out=wkb[:, :, (ic % 2) * 512:(ic % 2) * 512 + 512],
                                in_=wfk_d.ap().rearrange("(a p) n -> p a n", p=128)[:, :, ic * 512:(ic + 1) * 512],
                                ).then_inc(gsem, 16)
                    w.dma_start(out=wkc[:, (ic % 2) * 4:(ic % 2) * 4 + 4, :],
                                in_=wfv_d.ap().rearrange("(a p) n -> p a n", p=128)[:, ic * 4:(ic + 1) * 4, :],
                                ).then_inc(gsem, 16)
                    w.cp(f"kk_w_{g}_{ic}", gsem)
        if en == "sync":
            w.wait("cf_z2st_1")
            for qq in range(NQ):
                for j in range(4):
                    w.wait(f"a_z2st_{qq}_{j}")
            w.wait("a_rmm_3_7")
            w.wait("a_rwkv_3_7")
            w.dma_start(out=yn[:, :, 1:NTOK + 1],
                        in_=z2fm_d.ap().rearrange("(a p) t -> p a t", p=128),
                        ).then_inc("ld", 16)
            w.cp("kk_yn_ld", "ld")
            for g in range(8):
                for j in range(2):
                    lsem = "la" if j == 0 else "lb"
                    if g == 0:
                        w.wait("cf_x2st_1")
                    if g >= 1:
                        w.wait(f"kk_s_{g - 1}_{j}")
                    w.dma_start(out=g1t[:, j, :],
                                in_=x2_d.ap()[g * 256 + j * 128: g * 256 + j * 128 + 128, :],
                                ).then_inc(lsem, 16)
                    w.cp(f"kk_x2_{g}_{j}", lsem)
                for j in range(2):
                    ssem = "sa" if j == 0 else "sb"
                    w.wait(f"kk_s_{g}_{j}")
                    w.dma_start(out=s32_d.ap()[g * 256 + j * 128: g * 256 + j * 128 + 128, :],
                                in_=g2t[:, j, :]).then_inc(ssem, 16)
                    w.cp(f"kk_sst_{g}_{j}", ssem)
            w.cp("kk_s_all_a", "sa")
            w.cp("kk_s_all_b", "sb")
        if en == "vector":
            w.wait("kk_yn_ld")
            w.wait("y_haloc")
            for dc in range(NDC):
                w.tensor_scalar(yn[:, dc, 1:NTOK + 1], yn[:, dc, 1:NTOK + 1],
                                vc("g2", dc), vc("b2", dc), ALU.mult,
                                ALU.add).then_inc("ve", 1)
            w.tensor_copy(yn[:, :, 0:1], yhaloc[:, :, None]).then_inc("ve", 1)
            w.cp("kk_yn", "ve")
            for g in range(8):
                for dc in range(NDC):
                    w.tensor_tensor(msb[:, dc, :], yn[:, dc, 1 + g * 256:1 + (g + 1) * 256],
                                    yn[:, dc, g * 256:(g + 1) * 256],
                                    ALU.add).then_inc("ve", 1)
                w.cp(f"kk_ms_{g}", "ve")
                for ic in range(8):
                    w.wait(f"kk_fkmm_{g}_{ic}")
                    if g == 0 and ic == 0:
                        w.wait("cf_womm_1")
                    pa, pb_ = (ic % 2) * 2, (ic % 2) * 2 + 1
                    w.tensor_scalar(kkc[:, ic % 2, 0:2, :], p3(PB[pa], 256),
                                    0.5, 0.0, ALU.mult, ALU.max).then_inc("ve", 1)
                    w.tensor_scalar(kkc[:, ic % 2, 2:4, :], p3(PB[pb_], 256),
                                    0.5, 0.0, ALU.mult, ALU.max).then_inc("ve", 1)
                    w.tensor_tensor(kkc[:, ic % 2, :, :], kkc[:, ic % 2, :, :],
                                    kkc[:, ic % 2, :, :], ALU.mult).then_inc("ve", 1)
                    w.cp(f"kk_kk_{g}_{ic}", "ve")
                for j in range(2):
                    w.wait(f"kk_fvmm_{g}")
                    w.wait(f"kk_x2_{g}_0")
                    w.wait(f"kk_x2_{g}_1")
                    w.wait(f"kk_rrs_{g}_{j}")
                    if g >= 1:
                        w.wait(f"kk_sst_{g - 1}_{j}")
                    w.tensor_tensor(gs1[:, 0:512], rrg[:, j, 0:512],
                                    PB[4 + j * 2][:, :], ALU.mult).then_inc("ve", 1)
                    w.tensor_tensor(gs1[:, 512:1024], rrg[:, j, 512:1024],
                                    PB[5 + j * 2][:, :], ALU.mult).then_inc("ve", 1)
                    w.tensor_tensor(g2t[:, j, :], gs1[:, :], g1t[:, j, :],
                                    ALU.add).then_inc("ve", 1)
                    w.cp(f"kk_s_{g}_{j}", "ve")
        if en == "tensor":
            for g in range(8):
                for j in range(2):
                    if g == 0 and j == 0:
                        w.wait("w_wfr")
                        w.wait("kk_yn")
                        w.wait("cf_ztre_1")
                        w.wait("y_haloc")
                    else:
                        pg, pj = (g, 0) if j == 1 else (g - 1, 1)
                        w.wait(f"kk_rrs_{pg}_{pj}")
                        if j == 0:
                            w.wait(f"kk_kk_{g - 1}_7")
                    for n in range(2):
                        for dcK in range(NDC):
                            w.matmul(PB[2 + n][:, :],
                                     yn[:, dcK, 1 + g * 256 + j * 128: 1 + g * 256 + (j + 1) * 128],
                                     wka[:, dcK, n * 512:(n + 1) * 512],
                                     start=(dcK == 0), stop=(dcK == NDC - 1)).then_inc("pe", 1)
                    w.cp(f"kk_rrmm_{g}_{j}", "pe")
                    if g == 7 and j == 1:
                        w.cp("kk_rr_last", "pe")
                for ic in range(8):
                    w.wait(f"kk_w_{g}_{ic}")
                    w.wait(f"kk_ms_{g}")
                    if g == 0 and ic < 2:
                        w.wait("cf_ztre_1")
                        w.wait(f"kk_rrs_0_{1}") if False else None
                    if ic >= 2:
                        w.wait(f"kk_kk_{g}_{ic - 2}")
                    pa, pb_ = (ic % 2) * 2, (ic % 2) * 2 + 1
                    if ic % 2 == 1 and g == 0 and ic == 1:
                        w.wait("kk_rrs_0_1")
                    if ic % 2 == 1:
                        w.wait(f"kk_rrs_{g}_1")
                    for sub in range(2):
                        for dcK in range(NDC):
                            w.matmul(PB[pa][:, sub * 256:(sub + 1) * 256],
                                     wkb[:, dcK, (ic % 2) * 512 + sub * 128:(ic % 2) * 512 + (sub + 1) * 128],
                                     msb[:, dcK, :],
                                     start=(dcK == 0), stop=(dcK == NDC - 1)).then_inc("pe", 1)
                    for sub in range(2):
                        for dcK in range(NDC):
                            w.matmul(PB[pb_][:, sub * 256:(sub + 1) * 256],
                                     wkb[:, dcK, (ic % 2) * 512 + (2 + sub) * 128:(ic % 2) * 512 + (3 + sub) * 128],
                                     msb[:, dcK, :],
                                     start=(dcK == 0), stop=(dcK == NDC - 1)).then_inc("pe", 1)
                    w.cp(f"kk_fkmm_{g}_{ic}", "pe")
                    w.wait(f"kk_kk_{g}_{ic}")
                    if ic == 0:
                        if g >= 1:
                            w.wait(f"kk_s_{g - 1}_1")
                        else:
                            w.wait("cf_x2_1")
                    for j in range(2):
                        for n in range(2):
                            for sub in range(4):
                                w.matmul(PB[4 + j * 2 + n][:, :],
                                         kkc[:, ic % 2, sub, j * 128:(j + 1) * 128],
                                         wkc[:, (ic % 2) * 4 + sub, n * 512:(n + 1) * 512],
                                         start=(ic == 0 and sub == 0),
                                         stop=(ic == 7 and sub == 3)).then_inc("pe", 1)
                    w.cp(f"kk_fv_{g}_{ic}", "pe")
                    if ic == 7:
                        w.cp(f"kk_fvmm_{g}", "pe")
            w.cp("kk_pe_done", "pe")
        if en == "scalar":
            for g in range(8):
                for j in range(2):
                    w.wait(f"kk_rrmm_{g}_{j}")
                    if g == 0 and j == 0:
                        w.wait("cf_womm_1")
                    w.activation(rrg[:, j, 0:512], PB[2][:, :], AF.Sigmoid).then_inc("ac", 1)
                    w.activation(rrg[:, j, 512:1024], PB[3][:, :], AF.Sigmoid).then_inc("ac", 1)
                    w.cp(f"kk_rrs_{g}_{j}", "ac")

    def ph3kk_pre(en, w):
        if en == "tensor":
            pass
        if en == "gpsimd":
            pass
        if en == "vector":
            pass

    def ph4(en, w):
        if en == "sync":
            for cc in range(NCORE):
                if cc == 0:
                    w.wait("agrec_done")
                    w.wait("kk_yn")
                if cc >= 1:
                    w.wait(f"x_sidx_{cc - 1}")
                w.dma_start(out=recs[:, :],
                            in_=agrec_out.ap()[cc * 128:(cc + 1) * 128, :]).then_inc("ld", 16)
                w.cp(f"x_rec_{cc}", "ld")
            w.wait("x_zt")
            for j in range(CAP // 128):
                w.dma_start(out=xg_d.ap()[j * 128:(j + 1) * 128, :],
                            in_=xg16t[:, 0, :]).then_inc("sa", 16)
            w.cp("x_zeroed", "sa")
            for i in range(NCORE * NT):
                cc, t = i // NT, i % NT
                lsem = "la" if i % 2 == 0 else "lb"
                if i < 2:
                    w.wait("agx_done")
                    w.wait("x_zeroed")
                if i >= 2:
                    w.wait(f"x_scat_{i - 2}")
                w.dma_start(out=xg16t[:, i % 2, :],
                            in_=agx_out.ap()[cc * NTOK + t * 128: cc * NTOK + t * 128 + 128, :],
                            ).then_inc(lsem, 16)
                w.cp(f"x_row_{i}", lsem)
            for sgi in range(10):
                if sgi == 0:
                    w.wait("x_scat_all_r")
                    w.wait("x_scat_all_s")
                for j in range(2):
                    lsem = "la" if j == 0 else "lb"
                    if sgi >= 1:
                        w.wait(f"x_xg32_{sgi - 1}_{j}")
                    w.dma_start(out=xg16t[:, j, :],
                                in_=xg_d.ap()[sgi * 256 + j * 128: sgi * 256 + j * 128 + 128, :],
                                ).then_inc(lsem, 16)
                    w.cp(f"x_xgld_{sgi}_{j}", lsem)
                for j in range(2):
                    ssem = "sa" if j == 0 else "sb"
                    w.wait(f"x_eoev_{sgi}_{j}")
                    w.dma_start(out=ageo_in.ap()[sgi * 256 + j * 128: sgi * 256 + j * 128 + 128, :],
                                in_=g1t[:, j, :]).then_inc(ssem, 16)
                    w.cp(f"x_eost_{sgi}_{j}", ssem)
        if en == "vector":
            w.wait("kk_s_all_a")
            w.wait("kk_s_all_b")
            w.memset(xg16t[:, 0, :], 0.0).then_inc("ve", 1)
            w.cp("x_zt", "ve")
            for cc in range(NCORE):
                w.wait(f"x_rec_{cc}")
                w.tensor_scalar(rec3[:, :], recs[:, 0:16], myid[:, :], None,
                                ALU.is_equal).then_inc("ve", 1)
                ssync(w, "ve")
                w.tensor_tensor(rec4[:, :], rec3[:, :], recs[:, 16:32],
                                ALU.mult).then_inc("ve", 1)
                w.tensor_scalar(rec5[:, :], recs[:, 32:48], myid[:, :], None,
                                ALU.is_equal).then_inc("ve", 1)
                ssync(w, "ve")
                w.tensor_tensor(tmpc[:, :], rec5[:, :], recs[:, 48:64],
                                ALU.mult).then_inc("ve", 1)
                ssync(w, "ve")
                w.tensor_tensor(rec4[:, :], rec4[:, :], tmpc[:, :],
                                ALU.add).then_inc("ve", 1)
                w.tensor_tensor(rec3[:, :], rec3[:, :], rec5[:, :],
                                ALU.add).then_inc("ve", 1)
                ssync(w, "ve")
                w.tensor_scalar(rec3[:, :], rec3[:, :], -1.0, 1.0,
                                ALU.mult, ALU.add).then_inc("ve", 1)
                ssync(w, "ve")
                w.scalar_tensor_tensor(rec4[:, :], rec3[:, :], float(CAP),
                                       rec4[:, :], ALU.mult,
                                       ALU.add).then_inc("ve", 1)
                ssync(w, "ve")
                w.tensor_scalar(rec4[:, :], rec4[:, :], float(CAP), None,
                                ALU.min).then_inc("ve", 1)
                ssync(w, "ve")
                w.tensor_copy(sidx[:, cc * 16:(cc + 1) * 16], rec4[:, :]).then_inc("ve", 1)
                w.cp(f"x_sidx_{cc}", "ve")
            w.cp("x_sidx", "ve")
            for sgi in range(10):
                for j in range(2):
                    w.wait(f"x_xgld_{sgi}_{j}")
                    w.tensor_copy(xgf32[:, :], xg16t[:, j, :]).then_inc("ve", 1)
                    w.cp(f"x_xg32_{sgi}_{j}", "ve")
                    w.wait(f"x_xgtr_{sgi}_{j}")
                    if sgi == 0 and j == 0:
                        w.wait("kk_rr_last")
                    w.tensor_copy(z1h[:, 0:4, j * 128:(j + 1) * 128],
                                  p3(PB[0], 128)).then_inc("ve", 1)
                    w.tensor_copy(z1h[:, 4:8, j * 128:(j + 1) * 128],
                                  p3(PB[1], 128)).then_inc("ve", 1)
                    w.cp(f"x_xgf_{sgi}_{j}", "ve")
                for ic in range(8):
                    w.wait(f"x_upmm_{sgi}_{ic}")
                    w.tensor_copy(mixa[:, 0:2, 0:256], p3(PB[2], 256)).then_inc("ve", 1)
                    w.tensor_copy(mixa[:, 2:4, 0:256], p3(PB[3], 256)).then_inc("ve", 1)
                    w.cp(f"x_upev_{sgi}_{ic}", "ve")
                    w.wait(f"x_gsilu_{sgi}_{ic}")
                    w.tensor_tensor(mixa[:, 0:4, 0:256], mixb[:, 0:4, 0:256],
                                    mixa[:, 0:4, 0:256], ALU.mult).then_inc("ve", 1)
                    w.cp(f"x_hp_{sgi}_{ic}", "ve")
                for j in range(2):
                    w.wait(f"x_eomm_{sgi}")
                    if sgi >= 1:
                        w.wait(f"x_eost_{sgi - 1}_{j}")
                    else:
                        w.wait("kk_s_7_1")
                        w.wait(f"kk_sst_7_{j}")
                    w.tensor_copy(g1t[:, j, 0:512], PB[4 + j * 2][:, :]).then_inc("ve", 1)
                    w.tensor_copy(g1t[:, j, 512:1024], PB[5 + j * 2][:, :]).then_inc("ve", 1)
                    w.cp(f"x_eoev_{sgi}_{j}", "ve")
        if en == "scalar":
            for sgi in range(10):
                for ic in range(8):
                    w.wait(f"x_gmm_{sgi}_{ic}")
                    w.activation(mixb[:, 0:2, 0:256], PB[0][:, :], AF.Silu).then_inc("ac", 1)
                    w.activation(mixb[:, 2:4, 0:256], PB[1][:, :], AF.Silu).then_inc("ac", 1)
                    w.cp(f"x_gsilu_{sgi}_{ic}", "ac")
        if en == "gpsimd":
            for i in range(NCORE * NT):
                cc, t = i // NT, i % NT
                gsem = "gr" if i % 2 == 0 else "gs"
                if i == 0:
                    w.wait("x_zeroed")
                    w.wait("x_sidx")
                w.wait(f"x_row_{i}")
                w.indirect_dma_start(
                    out=xg_d.ap(),
                    out_offset=IndirectOffsetOnAxis(sidx[:, cc * 16 + t: cc * 16 + t + 1], axis=0),
                    in_=xg16t[:, i % 2, :], in_offset=None).then_inc(gsem, 16)
                w.cp(f"x_scat_{i}", gsem)
            w.cp("x_scat_all_r", "gr")
            w.cp("x_scat_all_s", "gs")
            for sgi in range(10):
                for ic in range(8):
                    n_slot = sgi * 8 + ic
                    gsem = "ga" if ic % 2 == 0 else "gb"
                    if n_slot == 0:
                        w.wait("kk_pe_done")
                        w.wait("kk_rr_last")
                    if n_slot >= 2:
                        pg, pic = (n_slot - 2) // 8, (n_slot - 2) % 8
                        w.wait(f"x_upmm_{pg}_{pic}")
                        w.wait(f"x_eo_{pg}_{pic}")
                    w.dma_start(out=wka[:, :, (ic % 2) * 512:(ic % 2) * 512 + 512],
                                in_=rg_d.ap().rearrange("(a p) n -> p a n", p=128)[:, :, ic * 512:(ic + 1) * 512],
                                ).then_inc(gsem, 16)
                    w.dma_start(out=wkb[:, :, (ic % 2) * 512:(ic % 2) * 512 + 512],
                                in_=ru_d.ap().rearrange("(a p) n -> p a n", p=128)[:, :, ic * 512:(ic + 1) * 512],
                                ).then_inc(gsem, 16)
                    w.dma_start(out=wkc[:, (ic % 2) * 4:(ic % 2) * 4 + 4, :],
                                in_=rd_d.ap().rearrange("(a p) n -> p a n", p=128)[:, ic * 4:(ic + 1) * 4, :],
                                ).then_inc(gsem, 16)
                    w.cp(f"x_w_{sgi}_{ic}", gsem)
            w.wait("x_eost_9_0")
            w.wait("x_eost_9_1")
            w.collective_compute("AllGather", ALU.bypass, replica_groups=RG,
                                 ins=[ageo_in.ap()], outs=[ageo_out.ap()[0:NCORE * CAP, :]]).then_inc("ce", 1)
            w.cp("ageo_done", "ce")
        if en == "tensor":
            for sgi in range(10):
                for j in range(2):
                    w.wait(f"x_xg32_{sgi}_{j}")
                    if sgi == 0 and j == 0:
                        w.wait("kk_kk_7_7")
                    else:
                        pg = sgi * 2 + j - 1
                        w.wait(f"x_xgf_{pg // 2}_{pg % 2}")
                    if sgi >= 1 and j == 0:
                        w.wait(f"x_gsilu_{sgi - 1}_7")
                    for dc in range(NDC):
                        w.matmul(PB[dc // 4][:, (dc % 4) * 128:(dc % 4 + 1) * 128],
                                 xgf32[:, dc * 128:(dc + 1) * 128],
                                 ident_s[:, :], start=True, stop=True,
                                 is_transpose=True).then_inc("pe", 1)
                    w.cp(f"x_xgtr_{sgi}_{j}", "pe")
                for ic in range(8):
                    w.wait(f"x_w_{sgi}_{ic}")
                    w.wait(f"x_xgf_{sgi}_1")
                    if sgi == 0 and ic == 0:
                        w.wait("kk_rrs_7_1")
                        w.wait("kk_s_7_1")
                    if ic >= 1:
                        w.wait(f"x_gsilu_{sgi}_{ic - 1}")
                    for sub in range(2):
                        for dcK in range(NDC):
                            w.matmul(PB[0][:, sub * 256:(sub + 1) * 256],
                                     wka[:, dcK, (ic % 2) * 512 + sub * 128:(ic % 2) * 512 + (sub + 1) * 128],
                                     z1h[:, dcK, 0:256],
                                     start=(dcK == 0), stop=(dcK == NDC - 1)).then_inc("pe", 1)
                    for sub in range(2):
                        for dcK in range(NDC):
                            w.matmul(PB[1][:, sub * 256:(sub + 1) * 256],
                                     wka[:, dcK, (ic % 2) * 512 + (2 + sub) * 128:(ic % 2) * 512 + (3 + sub) * 128],
                                     z1h[:, dcK, 0:256],
                                     start=(dcK == 0), stop=(dcK == NDC - 1)).then_inc("pe", 1)
                    w.cp(f"x_gmm_{sgi}_{ic}", "pe")
                    if ic >= 1:
                        w.wait(f"x_upev_{sgi}_{ic - 1}")
                    for sub in range(2):
                        for dcK in range(NDC):
                            w.matmul(PB[2][:, sub * 256:(sub + 1) * 256],
                                     wkb[:, dcK, (ic % 2) * 512 + sub * 128:(ic % 2) * 512 + (sub + 1) * 128],
                                     z1h[:, dcK, 0:256],
                                     start=(dcK == 0), stop=(dcK == NDC - 1)).then_inc("pe", 1)
                    for sub in range(2):
                        for dcK in range(NDC):
                            w.matmul(PB[3][:, sub * 256:(sub + 1) * 256],
                                     wkb[:, dcK, (ic % 2) * 512 + (2 + sub) * 128:(ic % 2) * 512 + (3 + sub) * 128],
                                     z1h[:, dcK, 0:256],
                                     start=(dcK == 0), stop=(dcK == NDC - 1)).then_inc("pe", 1)
                    w.cp(f"x_upmm_{sgi}_{ic}", "pe")
                    w.wait(f"x_hp_{sgi}_{ic}")
                    if ic == 0:
                        if sgi >= 1:
                            w.wait(f"x_eoev_{sgi - 1}_1")
                        else:
                            w.wait("kk_s_7_1")
                    for j in range(2):
                        for n in range(2):
                            for sub in range(4):
                                w.matmul(PB[4 + j * 2 + n][:, :],
                                         mixa[:, sub, j * 128:(j + 1) * 128],
                                         wkc[:, (ic % 2) * 4 + sub, n * 512:(n + 1) * 512],
                                         start=(ic == 0 and sub == 0),
                                         stop=(ic == 7 and sub == 3)).then_inc("pe", 1)
                    w.cp(f"x_eo_{sgi}_{ic}", "pe")
                    if ic == 7:
                        w.cp(f"x_eomm_{sgi}", "pe")
            w.cp("x_pe_done", "pe")

    def ph5(en, w):
        if en == "gpsimd":
            w.wait("kk_s_all_a")
            w.wait("kk_s_all_b")
            w.memset(gs1[:, :], 0.0).then_inc("po", 1)
            w.cp("f_zrow_ms", "po")
            w.wait("f_zrow_ms")
            w.dma_start(out=ageo_out.ap()[NCORE * CAP:NCORE * CAP + 1, :],
                        in_=gs1[0:1, :]).then_inc("gp", 16)
            w.cp("f_zrow", "gp")
            for g in range(8):
                for ic in range(16):
                    n_slot = g * 16 + ic
                    gsem = "ga" if ic % 2 == 0 else "gb"
                    if n_slot == 0:
                        w.wait("x_pe_done")
                    if n_slot >= 2:
                        pg, pic = (n_slot - 2) // 16, (n_slot - 2) % 16
                        w.wait(f"f_h2_{pg}_{pic}")
                        w.wait(f"f_sd_{pg}_{pic}")
                    w.dma_start(out=wka[:, :, (ic % 2) * 512:(ic % 2) * 512 + 512],
                                in_=sg_d.ap().rearrange("(a p) n -> p a n", p=128)[:, :, ic * 512:(ic + 1) * 512],
                                ).then_inc(gsem, 16)
                    w.dma_start(out=wkb[:, :, (ic % 2) * 512:(ic % 2) * 512 + 512],
                                in_=su_d.ap().rearrange("(a p) n -> p a n", p=128)[:, :, ic * 512:(ic + 1) * 512],
                                ).then_inc(gsem, 16)
                    w.dma_start(out=wkc[:, (ic % 2) * 4:(ic % 2) * 4 + 4, :],
                                in_=sd_d.ap().rearrange("(a p) n -> p a n", p=128)[:, ic * 4:(ic + 1) * 4, :],
                                ).then_inc(gsem, 16)
                    w.cp(f"f_w_{g}_{ic}", gsem)
                for j in range(2):
                    ti = g * 2 + j
                    gsem = "gr" if ti % 2 == 0 else "gs"
                    if ti == 0:
                        w.wait("ageo_done")
                        w.wait("f_zrow")
                    if ti >= 2:
                        w.wait(f"f_rt_{ti - 2}")
                        w.wait(f"f_out_{ti - 2}")
                    w.indirect_dma_start(
                        out=g1t[:, j, :], out_offset=None, in_=ageo_out.ap(),
                        in_offset=IndirectOffsetOnAxis(gidx1[:, ti:ti + 1], axis=0),
                        ).then_inc(gsem, 16)
                    w.indirect_dma_start(
                        out=g2t[:, j, :], out_offset=None, in_=ageo_out.ap(),
                        in_offset=IndirectOffsetOnAxis(gidx2[:, ti:ti + 1], axis=0),
                        ).then_inc(gsem, 16)
                    w.cp(f"f_g_{ti}", gsem)
        if en == "sync":
            for g in range(8):
                if g == 0:
                    w.wait("x_eomm_9")
                if g >= 2:
                    w.wait(f"f_sdmm_{g - 2}")
                w.dma_start(out=xn1b[:, :, 0:256],
                            in_=xnfm_d.ap().rearrange("(a p) t -> p a t", p=128)[:, :, g * 256:(g + 1) * 256],
                            ).then_inc("ld", 16)
                w.cp(f"f_xn_{g}", "ld")
                for j in range(2):
                    ti = g * 2 + j
                    lsem = "la" if j == 0 else "lb"
                    if ti >= 2:
                        w.wait(f"f_o_{ti - 2}")
                    w.dma_start(out=xt[:, j, :],
                                in_=xs.ap()[ti * 128:(ti + 1) * 128, :]).then_inc(lsem, 16)
                    w.dma_start(out=xnt[:, j, :],
                                in_=s32_d.ap()[ti * 128:(ti + 1) * 128, :]).then_inc(lsem, 16)
                    w.cp(f"f_xs_{ti}", lsem)
                for j in range(2):
                    ti = g * 2 + j
                    ssem = "sa" if j == 0 else "sb"
                    w.wait(f"f_o_{ti}")
                    w.dma_start(out=out_d.ap()[ti * 128:(ti + 1) * 128, :],
                                in_=g2t[:, j, :]).then_inc(ssem, 16)
                    w.cp(f"f_out_{ti}", ssem)
            w.cp("f_out_all_a", "sa")
            w.cp("f_out_all_b", "sb")
            w.wait("f_out_all_a")
            w.wait("f_out_all_b")
        if en == "tensor":
            for g in range(8):
                for ic in range(16):
                    w.wait(f"f_w_{g}_{ic}")
                    w.wait(f"f_xn_{g}")
                    if g == 0 and ic == 0:
                        w.wait("x_eoev_9_1")
                        w.wait("x_hp_9_7")
                        w.wait("x_gsilu_9_7")
                        w.wait("x_upev_9_7")
                    if ic >= 1:
                        w.wait(f"f_hsilu_{g}_{ic - 1}")
                    elif g >= 1:
                        w.wait(f"f_hsilu_{g - 1}_15")
                    for sub in range(2):
                        for dcK in range(NDC):
                            w.matmul(PB[0][:, sub * 256:(sub + 1) * 256],
                                     wka[:, dcK, (ic % 2) * 512 + sub * 128:(ic % 2) * 512 + (sub + 1) * 128],
                                     xn1b[:, dcK, 0:256],
                                     start=(dcK == 0), stop=(dcK == NDC - 1)).then_inc("pe", 1)
                    for sub in range(2):
                        for dcK in range(NDC):
                            w.matmul(PB[1][:, sub * 256:(sub + 1) * 256],
                                     wka[:, dcK, (ic % 2) * 512 + (2 + sub) * 128:(ic % 2) * 512 + (3 + sub) * 128],
                                     xn1b[:, dcK, 0:256],
                                     start=(dcK == 0), stop=(dcK == NDC - 1)).then_inc("pe", 1)
                    w.cp(f"f_h1_{g}_{ic}", "pe")
                    if ic >= 1:
                        w.wait(f"f_h2ev_{g}_{ic - 1}")
                    elif g >= 1:
                        w.wait(f"f_h2ev_{g - 1}_15")
                    for sub in range(2):
                        for dcK in range(NDC):
                            w.matmul(PB[2][:, sub * 256:(sub + 1) * 256],
                                     wkb[:, dcK, (ic % 2) * 512 + sub * 128:(ic % 2) * 512 + (sub + 1) * 128],
                                     xn1b[:, dcK, 0:256],
                                     start=(dcK == 0), stop=(dcK == NDC - 1)).then_inc("pe", 1)
                    for sub in range(2):
                        for dcK in range(NDC):
                            w.matmul(PB[3][:, sub * 256:(sub + 1) * 256],
                                     wkb[:, dcK, (ic % 2) * 512 + (2 + sub) * 128:(ic % 2) * 512 + (3 + sub) * 128],
                                     xn1b[:, dcK, 0:256],
                                     start=(dcK == 0), stop=(dcK == NDC - 1)).then_inc("pe", 1)
                    w.cp(f"f_h2_{g}_{ic}", "pe")
                    w.wait(f"f_hp_{g}_{ic}")
                    if ic == 0 and g >= 1:
                        w.wait(f"f_gsum_{g * 2 - 1}")
                    for j in range(2):
                        for n in range(2):
                            for sub in range(4):
                                w.matmul(PB[4 + j * 2 + n][:, :],
                                         mixa[:, sub, j * 128:(j + 1) * 128],
                                         wkc[:, (ic % 2) * 4 + sub, n * 512:(n + 1) * 512],
                                         start=(ic == 0 and sub == 0),
                                         stop=(ic == 15 and sub == 3)).then_inc("pe", 1)
                    w.cp(f"f_sd_{g}_{ic}", "pe")
                    if ic == 15:
                        w.cp(f"f_sdmm_{g}", "pe")
        if en == "scalar":
            for g in range(8):
                for ic in range(16):
                    w.wait(f"f_h1_{g}_{ic}")
                    w.activation(mixb[:, 0:2, 0:256], PB[0][:, :], AF.Silu).then_inc("ac", 1)
                    w.activation(mixb[:, 2:4, 0:256], PB[1][:, :], AF.Silu).then_inc("ac", 1)
                    w.cp(f"f_hsilu_{g}_{ic}", "ac")
                for j in range(2):
                    ti = g * 2 + j
                    w.wait(f"f_gsum_{ti}")
                    w.activation(g1t[:, j, :], g1t[:, j, :], AF.Silu).then_inc("ac", 1)
                    w.cp(f"f_silu_{ti}", "ac")
        if en == "vector":
            for g in range(8):
                for ic in range(16):
                    w.wait(f"f_h2_{g}_{ic}")
                    w.tensor_copy(mixa[:, 0:2, 0:256], p3(PB[2], 256)).then_inc("ve", 1)
                    w.tensor_copy(mixa[:, 2:4, 0:256], p3(PB[3], 256)).then_inc("ve", 1)
                    w.cp(f"f_h2ev_{g}_{ic}", "ve")
                    w.wait(f"f_hsilu_{g}_{ic}")
                    w.tensor_tensor(mixa[:, 0:4, 0:256], mixb[:, 0:4, 0:256],
                                    mixa[:, 0:4, 0:256], ALU.mult).then_inc("ve", 1)
                    w.cp(f"f_hp_{g}_{ic}", "ve")
                for j in range(2):
                    ti = g * 2 + j
                    w.wait(f"f_g_{ti}")
                    w.wait(f"f_sdmm_{g}")
                    w.tensor_scalar(g1t[:, j, :], g1t[:, j, :], w1c[:, ti:ti + 1],
                                    None, ALU.mult).then_inc("ve", 1)
                    w.scalar_tensor_tensor(g1t[:, j, :], g2t[:, j, :],
                                           w2c[:, ti:ti + 1], g1t[:, j, :],
                                           ALU.mult, ALU.add).then_inc("ve", 1)
                    w.tensor_tensor(g1t[:, j, 0:512], g1t[:, j, 0:512],
                                    PB[4 + j * 2][:, :], ALU.add).then_inc("ve", 1)
                    w.tensor_tensor(g1t[:, j, 512:1024], g1t[:, j, 512:1024],
                                    PB[5 + j * 2][:, :], ALU.add).then_inc("ve", 1)
                    w.cp(f"f_gsum_{ti}", "ve")
                    w.cp(f"f_rt_{ti}", "ve")
                    w.wait(f"f_silu_{ti}")
                    w.wait(f"f_xs_{ti}")
                    if ti >= 2:
                        w.wait(f"f_out_{ti - 2}")
                    w.tensor_tensor(g2t[:, j, :], xnt[:, j, :], g1t[:, j, :],
                                    ALU.mult).then_inc("ve", 1)
                    w.tensor_tensor(g2t[:, j, :], g2t[:, j, :], xt[:, j, :],
                                    ALU.add).then_inc("ve", 1)
                    w.cp(f"f_o_{ti}", "ve")

    def ph3(en, w):
        for q in range(NQ):
            ph3q(en, w, q)
        ph3fix(en, w)
        ph3yh(en, w)
        if en == "tensor":
            w.cp("kk_rrwka_free_mark", "pe")
        if en == "gpsimd":
            pass
        ph3kk(en, w)

    def alias(en, w):
        if en == "tensor":
            pass
        if en == "gpsimd":
            pass

    phases = [ph0, ph0rw, ph0b, ph0c, ph1, ph2, ph3, ph4, ph5]

    for en in ENGINES:
        w = EW(sched, sems, en, None)
        for ph in phases:
            ph(en, w)
    sched.cur["kk_rrwka_free"] = sched.cur["a_kvmm_3_7"]
    sched.start_real()

    with nc.Block() as block:
        def mk(en):
            def emit(real_engine):
                w = EW(sched, sems, en, real_engine)
                for ph in phases:
                    ph(en, w)
            return emit
        block.sync(mk("sync"))
        block.scalar(mk("scalar"))
        block.vector(mk("vector"))
        block.tensor(mk("tensor"))
        block.gpsimd(mk("gpsimd"))

    sched.cur["kk_rrwka_free"] = sched.cur["a_kvmm_3_7"]
    sched.check()
    return nc


_CACHE = {}


def kernel(**inputs):
    x = np.asarray(inputs["L0_attn_x"], np.float32)
    B, L, _ = x.shape
    xf = np.ascontiguousarray(x.reshape(B * L, D))

    vecs = np.ascontiguousarray(np.stack([
        np.asarray(inputs["L0_attn_ln1_g"], np.float32),
        np.asarray(inputs["L0_attn_ln1_b"], np.float32),
        np.asarray(inputs["L0_attn_ln2_g"], np.float32),
        np.asarray(inputs["L0_attn_ln2_b"], np.float32),
        np.asarray(inputs["L0_attn_mix_k"], np.float32),
        np.asarray(inputs["L0_attn_mix_v"], np.float32),
        np.asarray(inputs["L0_attn_mix_r"], np.float32),
        np.asarray(inputs["L0_attn_time_decay"], np.float32),
        np.asarray(inputs["L0_attn_time_first"], np.float32),
    ], axis=0))

    if "nc" not in _CACHE:
        _CACHE["nc"] = build_nc()
    nc = _CACHE["nc"]

    shared = {
        "wk": np.ascontiguousarray(np.asarray(inputs["L0_attn_Wk"], np.float32)),
        "wv": np.ascontiguousarray(np.asarray(inputs["L0_attn_Wv"], np.float32)),
        "wr": np.ascontiguousarray(np.asarray(inputs["L0_attn_Wr"], np.float32)),
        "wo": np.ascontiguousarray(np.asarray(inputs["L0_attn_Wo"], np.float32)),
        "wfr": np.ascontiguousarray(np.asarray(inputs["L0_attn_Wfr"], np.float32)),
        "wfk": np.ascontiguousarray(np.asarray(inputs["L0_attn_Wfk"], np.float32)),
        "wfv": np.ascontiguousarray(np.asarray(inputs["L0_attn_Wfv"], np.float32)),
        "sgw": np.ascontiguousarray(np.asarray(inputs["L0_ffn_sg"], np.float32)),
        "suw": np.ascontiguousarray(np.asarray(inputs["L0_ffn_su"], np.float32)),
        "sdw": np.ascontiguousarray(np.asarray(inputs["L0_ffn_sd"], np.float32)),
        "rww": np.ascontiguousarray(np.asarray(inputs["L0_ffn_router_w"], np.float32)),
        "vecs": vecs,
    }
    rg_a = np.asarray(inputs["L0_ffn_rg"], np.float32)
    ru_a = np.asarray(inputs["L0_ffn_ru"], np.float32)
    rd_a = np.asarray(inputs["L0_ffn_rd"], np.float32)

    in_maps = []
    for c in range(NCORE):
        h = c % 2
        im = dict(shared)
        im["xs"] = np.ascontiguousarray(xf[c * NTOK:(c + 1) * NTOK])
        im["xhalo"] = (np.ascontiguousarray(xf[c * NTOK - 1:c * NTOK])
                       if h == 1 else np.zeros((1, D), np.float32))
        im["rgw"] = np.ascontiguousarray(rg_a[c])
        im["ruw"] = np.ascontiguousarray(ru_a[c])
        im["rdw"] = np.ascontiguousarray(rd_a[c])
        sp = np.zeros((E, 1), np.float32)
        sp[:c, 0] = 1.0
        im["m_selprev"] = sp
        im["m_myid"] = np.full((128, 1), float(c), np.float32)
        im["m_flags"] = np.array([[1.0 if h == 0 else 0.0,
                                   1.0 if h == 1 else 0.0, 0.0, 0.0]], np.float32)
        poh = np.zeros((E, 1), np.float32)
        if h == 1:
            im["m_caridx"] = ((c - 1) * 128 + np.arange(128, dtype=np.int32)
                              ).reshape(128, 1)
            im["m_rowidx"] = np.array([[c - 1]], np.int32)
            poh[c - 1, 0] = 1.0
        else:
            im["m_caridx"] = np.full((128, 1), NCORE * 128, np.int32)
            im["m_rowidx"] = np.array([[OOB]], np.int32)
        im["m_predoh"] = poh
        in_maps.append(im)

    res = run_bass_kernel_spmd(nc, in_maps, core_ids=list(range(NCORE)))
    out = np.concatenate([res.results[c]["out"] for c in range(NCORE)], axis=0)
    return np.ascontiguousarray(out.reshape(B, L, D).astype(np.float32))


# version 05
# speedup vs baseline: 1.1170x; 1.1170x over previous
import os
import numpy as np
import concourse.bass as bass
import concourse.mybir as mybir
from concourse.bass import IndirectOffsetOnAxis
from concourse.bass_utils import run_bass_kernel_spmd

F32 = mybir.dt.float32
BF16 = mybir.dt.bfloat16
I32 = mybir.dt.int32
AF = mybir.ActivationFunctionType
ALU = mybir.AluOpType
AX = mybir.AxisListType

NCORE = 8
D = 1024
NDC = 8
NTOK = 2048
NT = 16
Q = 512
NQ = 4
E = 8
CAP = 2560
IR = 4096
IS = 8192
DFF = 4096
LN_EPS = 1e-5
WKV_EPS = 1e-8
EPS32 = float(np.finfo(np.float32).eps)
BIG = float(2 ** 28)
OOB = (1 << 30)
CORRW = 256

ENGINES = ["sync", "scalar", "vector", "tensor", "gpsimd"]
RG = [list(range(NCORE))]


class Sched:
    def __init__(self):
        self.dry = True
        self.counts = {}
        self.cur = {}
        self.final = {}
        self.ssn = {}

    def bump(self, sem, n):
        self.counts[sem] = self.counts.get(sem, 0) + n

    def cp(self, label, sem):
        self.cur[label] = (sem, self.counts.get(sem, 0))

    def start_real(self):
        self.final = dict(self.cur)
        self.cur = {}
        self.counts = {}
        self.ssn = {}
        self.dry = False

    def check(self):
        assert self.cur == self.final, "dry/real pass divergence"


class IW:
    __slots__ = ("s", "sems", "inst")

    def __init__(self, s, sems, inst):
        self.s, self.sems, self.inst = s, sems, inst

    def then_inc(self, semname, n=1):
        self.s.bump(semname, n)
        if self.inst is not None:
            self.inst.then_inc(self.sems[semname], n)
        return self


class EW:
    def __init__(self, sched, sems, name, real):
        self.s = sched
        self.sems = sems
        self.name = name
        self.real = real

    def cp(self, label, sem):
        self.s.cp(label, sem)

    def wait(self, label):
        if self.real is not None:
            sem, v = self.s.final[label]
            if v > 0:
                self.real.wait_ge(self.sems[sem], v)

    def __getattr__(self, attr):
        def wrap(*a, **k):
            inst = getattr(self.real, attr)(*a, **k) if self.real is not None else None
            return IW(self.s, self.sems, inst)
        return wrap


def build_nc():
    nc = bass.Bass("TRN2", target_bir_lowering=False, detect_race_conditions=False)

    def inp(name, shape, dt=F32):
        return nc.dram_tensor(name, shape, dt, kind="ExternalInput")

    xs = inp("xs", [NTOK, D])
    xhalo = inp("xhalo", [1, D])
    wk_d = inp("wk", [D, D]); wv_d = inp("wv", [D, D]); wr_d = inp("wr", [D, D])
    wo_d = inp("wo", [D, D]); wfr_d = inp("wfr", [D, D])
    wfk_d = inp("wfk", [D, DFF]); wfv_d = inp("wfv", [DFF, D])
    sg_d = inp("sgw", [D, IS]); su_d = inp("suw", [D, IS]); sd_d = inp("sdw", [IS, D])
    rg_d = inp("rgw", [D, IR]); ru_d = inp("ruw", [D, IR]); rd_d = inp("rdw", [IR, D])
    rw_d = inp("rww", [E, D])
    vecs_d = inp("vecs", [9, D])
    m_selprev = inp("m_selprev", [E, 1])
    m_myid = inp("m_myid", [128, 1])
    m_flags = inp("m_flags", [1, 4])
    m_caridx = inp("m_caridx", [128, 1], I32)
    m_rowidx = inp("m_rowidx", [1, 1], I32)
    m_predoh = inp("m_predoh", [E, 1])

    out_d = nc.dram_tensor("out", [NTOK, D], F32, kind="ExternalOutput")

    _dbg = os.environ.get("BASS_KERNEL_DEBUG_DUMPS") == "1"
    _ik = dict(kind="ExternalOutput") if _dbg else {}
    xn32_d = nc.dram_tensor("xn32", [NTOK, D], F32, **_ik)
    x2_d = nc.dram_tensor("x2d", [NTOK, D], F32, **_ik)
    s32_d = nc.dram_tensor("s32d", [NTOK, D], F32, **_ik)
    xnfm_d = nc.dram_tensor("xnfm", [D, NTOK], BF16, **_ik)
    z1fm_d = nc.dram_tensor("z1fm", [D, NTOK], BF16, **_ik)
    z2fm_d = nc.dram_tensor("z2fm", [D, NTOK], BF16, **_ik)
    agx_in = nc.dram_tensor("agx_in", [NTOK, D], BF16)
    agx_out = nc.dram_tensor("agx_out", [NCORE * NTOK, D], BF16)
    agcnt_in = nc.dram_tensor("agcnt_in", [1, E], F32)
    agcnt_out = nc.dram_tensor("agcnt_out", [NCORE, E], F32)
    agrec_in = nc.dram_tensor("agrec_in", [128, 64], F32)
    agrec_out = nc.dram_tensor("agrec_out", [NCORE * 128, 64], F32)
    agcar_in = nc.dram_tensor("agcar_in", [128, 16], F32)
    agcar_out = nc.dram_tensor("agcar_out", [NCORE * 128 + 1, 16], F32)
    agx2_in = nc.dram_tensor("agx2_in", [1, D], F32)
    agx2_out = nc.dram_tensor("agx2_out", [NCORE, D], F32)
    ageo_in = nc.dram_tensor("ageo_in", [CAP, D], F32)
    ageo_out = nc.dram_tensor("ageo_out", [NCORE * CAP + 1, D], F32)
    xg_d = nc.dram_tensor("xg16", [CAP + 1, D], BF16, **_ik)
    dbg_rt = nc.dram_tensor("dbg_rt", [128, 1024], F32, kind="ExternalOutput") if _dbg else None
    dbg_sc = nc.dram_tensor("dbg_sc", [128, 20480], F32, kind="ExternalOutput") if _dbg else None
    dbg_mx = nc.dram_tensor("dbg_mx", [128, 8192], BF16, kind="ExternalOutput") if _dbg else None
    dbg_wk = nc.dram_tensor("dbg_wk", [128, 4096], BF16, kind="ExternalOutput") if _dbg else None
    dbg_h = nc.dram_tensor("dbg_h", [128, 64], F32, kind="ExternalOutput") if _dbg else None
    dbg_h16 = nc.dram_tensor("dbg_h16", [128, 8], BF16, kind="ExternalOutput") if _dbg else None
    dbg_hr = nc.dram_tensor("dbg_hr", [4, D], F32, kind="ExternalOutput") if _dbg else None

    ident32 = nc.inline_tensor(np.eye(128, dtype=np.float32), name="c_ident32")
    u_np = np.zeros((128, 128), np.float32)
    for p in range(128):
        u_np[p, p + 1:] = 1.0
    ustrict = nc.inline_tensor(u_np, name="c_ustrict")
    c_np = np.zeros((128, 128), np.float32)
    for p in range(128):
        for qq in range(128):
            if (p % 8) == (qq % 8) and (p // 8) < (qq // 8):
                c_np[p, qq] = 1.0
    ctile = nc.inline_tensor(c_np, name="c_ctile")
    ones_row = nc.inline_tensor(np.ones((1, 128), np.float32), name="c_ones_row")
    ones_col = nc.inline_tensor(np.ones((128, 1), np.float32), name="c_ones_col")
    cvals_np = np.zeros((128, 8), np.float32)
    cvals_np[:, 0] = LN_EPS
    cvals_np[:, 1] = EPS32
    cvals_np[:, 2] = WKV_EPS
    cvals_np[:, 3] = -0.5
    cvals_np[:, 4] = -1.0
    cvals_d = nc.inline_tensor(cvals_np, name="c_cvals")

    def sb(name, shape, dt):
        return nc.alloc_sbuf_tensor(name, shape, dt)

    vecr = sb("vecr", [9, D], F32)
    vcol = sb("vcol", [128, 9, 8], F32)
    ewcol = sb("ewcol", [128, 8], F32)
    netd = sb("netd", [128, 8], F32)
    tfm = sb("tfm", [128, 8], F32)
    rw16 = sb("rw16", [128, 8, 8], BF16)
    rwrow16 = sb("rwrow16", [8, D], BF16)
    haloc = sb("haloc", [128, 8], BF16)
    yhaloc = sb("yhaloc", [128, 8], BF16)
    ynprev = sb("ynprev", [128, 8], BF16)
    selprev = sb("selprev", [8, 1], F32)
    myid = sb("myid", [128, 1], F32)
    flags = sb("flags", [1, 4], F32)
    caridx = sb("caridx", [128, 1], I32)
    rowidx = sb("rowidx", [1, 1], I32)
    predoh = sb("predoh", [8, 1], F32)
    hrow = sb("hrow", [1, D], F32)
    hrow2 = sb("hrow2", [1, D], F32)
    hrow16 = sb("hrow16", [1, D], BF16)
    hstat = sb("hstat", [1, 8], F32)
    iota256 = sb("iota256", [128, CORRW], F32)
    iota8t = sb("iota8t", [128, 128], F32)
    ident_s = sb("ident_s", [128, 128], F32)
    ustr_s = sb("ustr_s", [128, 128], F32)
    ctile_s = sb("ctile_s", [128, 128], F32)
    onesr_s = sb("onesr_s", [1, 128], F32)
    onesc_s = sb("onesc_s", [128, 1], F32)
    cvals = sb("cvals", [128, 8], F32)
    flagc = sb("flagc", [128, 4], F32)

    stat = sb("stat", [128, NT, 8], F32)
    w1c = sb("w1c", [128, NT], F32)
    w2c = sb("w2c", [128, NT], F32)
    gidx1 = sb("gidx1", [128, NT], I32)
    gidx2 = sb("gidx2", [128, NT], I32)
    sidx = sb("sidx", [128, 128], I32)
    scst = sb("scst", [128, 16], F32)
    carg = sb("carg", [128, 16], F32)
    lgbuf = sb("lgbuf", [128, NT, 8], F32)
    routc = sb("routc", [128, NT, 8], F32)
    routc2 = sb("routc2", [128, NT, 8], F32)
    prefx = sb("prefx", [128, 128], F32)
    tsum = sb("tsum", [128, 1], F32)
    offsc = sb("offsc", [128, 1], F32)
    offsr = sb("offsr", [1, 128], F32)
    offsb = sb("offsb", [128, 128], F32)
    offm = sb("offm", [1, 8], F32)
    offmb = sb("offmb", [128, 8], F32)
    cntrow = sb("cntrow", [1, 128], F32)
    cnts = sb("cnts", [8, 8], F32)
    r16a = sb("r16a", [128, NT], F32)
    r16b = sb("r16b", [128, NT], F32)
    i16a = sb("i16a", [128, NT], F32)
    i16b = sb("i16b", [128, NT], F32)
    tmpc = sb("tmpc", [128, NT], F32)
    tmpc2 = sb("tmpc2", [128, NT], F32)
    recs = sb("recs", [128, 64], F32)
    rec3 = sb("rec3", [128, 16], F32)
    rec4 = sb("rec4", [128, 16], F32)
    rec5 = sb("rec5", [128, 16], F32)

    xt = sb("xt", [128, 2, D], F32)
    xnt = sb("xnt", [128, 2, D], F32)
    z1t = sb("z1t", [128, D], F32)
    sqj = sb("sqj", [128, D], BF16)
    xn16t = sb("xn16t", [128, 2, D], BF16)
    trs = sb("trs", [128, 2, 8, 128], BF16)
    g1t = sb("g1t", [128, 2, D], F32)
    g2t = sb("g2t", [128, 2, D], F32)
    xg16t = sb("xg16t", [128, 2, D], BF16)
    pool3 = sb("pool3", [128, 4096], mybir.dt.uint8)
    _b3 = nc.lookup_mloc(pool3).addr
    xgf32 = nc.alloc_sbuf_tensor_at("xgf32", [128, D], F32, offset=_b3)
    gs1 = nc.alloc_sbuf_tensor_at("gs1", [128, D], F32, offset=_b3)

    wka = sb("wka", [128, 8, D], BF16)
    wkb = sb("wkb", [128, 8, D], BF16)
    wkc = sb("wkc", [128, 8, D], BF16)

    kch = sb("kch", [128, Q], F32)
    vch = sb("vch", [128, Q], F32)
    ekb = sb("ekb", [128, Q], F32)
    ekv = sb("ekv", [128, Q], F32)
    ewb = sb("ewb", [128, Q], F32)
    wkvh = sb("wkvh", [128, 8, Q], BF16)
    corra = sb("corra", [128, 8, CORRW], F32)
    corrb = sb("corrb", [128, 8, CORRW], F32)
    pcoef = sb("pcoef", [128, CORRW], F32)
    msb = sb("msb", [128, 8, 256], BF16)

    pool1 = sb("pool1", [128, 32800], mybir.dt.uint8)
    base1 = nc.lookup_mloc(pool1).addr

    def at1(name, shape, dt, off):
        return nc.alloc_sbuf_tensor_at(name, shape, dt, offset=base1 + off)

    mixa = at1("mixa", [128, 8, Q], BF16, 0)
    mixb = at1("mixb", [128, 8, Q], BF16, 8192)
    z1h = at1("z1h", [128, 8, Q], BF16, 16384)
    xn1b = at1("xn1b", [128, 8, Q + 1], BF16, 24576)
    yn = at1("yn", [128, 8, NTOK + 1], BF16, 0)

    pool2 = sb("pool2", [128, 8192], mybir.dt.uint8)
    base2 = nc.lookup_mloc(pool2).addr

    def at2(name, shape, dt, off):
        return nc.alloc_sbuf_tensor_at(name, shape, dt, offset=base2 + off)

    rcorr = at2("rcorr", [128, 8, CORRW], BF16, 0)
    cw = at2("cw", [128, 8, CORRW], BF16, 4096)
    kkc = at2("kkc", [128, 2, 4, 256], BF16, 0)
    rrg = at2("rrg", [128, 2, D], BF16, 4096)

    xnf = nc.alloc_sbuf_tensor_at("xnf", [128, 8, 1024], BF16, offset=base1)
    hpa = nc.alloc_sbuf_tensor_at("hpa", [128, 4, 256], BF16, offset=base2)
    hpb = nc.alloc_sbuf_tensor_at("hpb", [128, 4, 256], BF16, offset=base2 + 2048)
    _bk = nc.lookup_mloc(kch).addr
    assert base1 - _bk >= 8 * 1024 * 4, (base1, _bk)
    sacc = nc.alloc_sbuf_tensor_at("sacc", [128, 8, 1024], F32, offset=_bk)

    PB = [nc.alloc_psum_tensor(f"pb{i}", [128, 512], F32) for i in range(8)]

    sems = {}
    for s in ["ld", "la", "lb", "sa", "sb", "sf", "sc", "st",
              "gp", "ga", "gb", "gc", "gr", "gs",
              "po", "cc", "cd", "ce", "pe", "ve", "ac"]:
        sems[s] = nc.alloc_semaphore(s)

    sched = Sched()

    def ssync(w, sem):
        n = sched.ssn.get(sem, 0)
        sched.ssn[sem] = n + 1
        lbl = f"zz_{sem}_{n}"
        w.cp(lbl, sem)
        w.wait(lbl)

    def vc(nm, dc):
        j = {"g1": 0, "b1": 1, "g2": 2, "b2": 3, "mk": 4, "mv": 5, "mr": 6,
             "td": 7, "tf": 8}[nm]
        return vcol[:, j, dc:dc + 1]

    def p3(pb, w_):
        return pb[:, :].rearrange("p (c t) -> p c t", t=w_)

    C_LN = lambda: cvals[:, 0:1]
    C_E32 = lambda: cvals[:, 1:2]
    C_WKV = lambda: cvals[:, 2:3]
    C_NH = lambda: cvals[:, 3:4]
    C_N1 = lambda: cvals[:, 4:5]
    C1_LN = lambda: cvals[0:1, 0:1]
    C1_NH = lambda: cvals[0:1, 3:4]

    def ph0(en, w):
        if en == "sync":
            for j in range(9):
                w.dma_start(out=vecr[j:j + 1, :], in_=vecs_d.ap()[j:j + 1, :]).then_inc("ld", 16)
            w.dma_start(out=hrow[:, :], in_=xhalo.ap()).then_inc("ld", 16)
            w.dma_start(out=selprev[:, :], in_=m_selprev.ap()).then_inc("ld", 16)
            w.dma_start(out=myid[:, :], in_=m_myid.ap()).then_inc("ld", 16)
            w.dma_start(out=flags[:, :], in_=m_flags.ap()).then_inc("ld", 16)
            w.dma_start(out=caridx[:, :], in_=m_caridx.ap()).then_inc("ld", 16)
            w.dma_start(out=rowidx[:, :], in_=m_rowidx.ap()).then_inc("ld", 16)
            w.dma_start(out=predoh[:, :], in_=m_predoh.ap()).then_inc("ld", 16)
            w.dma_start(out=ident_s[:, :], in_=ident32.ap()).then_inc("ld", 16)
            w.dma_start(out=ustr_s[:, :], in_=ustrict.ap()).then_inc("ld", 16)
            w.dma_start(out=ctile_s[:, :], in_=ctile.ap()).then_inc("ld", 16)
            w.dma_start(out=onesr_s[:, :], in_=ones_row.ap()).then_inc("ld", 16)
            w.dma_start(out=onesc_s[:, :], in_=ones_col.ap()).then_inc("ld", 16)
            w.dma_start(out=cvals[:, :], in_=cvals_d.ap()).then_inc("ld", 16)
            w.cp("ph0_ld", "ld")
        if en == "gpsimd":
            w.iota(iota256[:, :], [[1, CORRW]], base=1, channel_multiplier=0,
                   allow_small_or_imprecise_dtypes=True).then_inc("po", 1)
            w.iota(iota8t[:, :], [[0, 16], [1, 8]], base=0, channel_multiplier=0,
                   allow_small_or_imprecise_dtypes=True).then_inc("po", 1)
            w.cp("ph0_iota", "po")
            w.dma_start(out=rwrow16[:, :], in_=rw_d.ap()).then_inc("gp", 16)
            w.cp("ph0_rw16", "gp")
            w.dma_start(out=wka[:, :, :],
                        in_=wk_d.ap().rearrange("(a p) n -> p a n", p=128)).then_inc("ga", 16)
            w.cp("w_wk", "ga")
            w.dma_start(out=wkb[:, :, :],
                        in_=wv_d.ap().rearrange("(a p) n -> p a n", p=128)).then_inc("gb", 16)
            w.cp("w_wv", "gb")
        if en == "tensor":
            w.wait("ph0_ld")
            for dc in range(NDC):
                w.matmul(PB[0][:, dc * 16: dc * 16 + 9],
                         vecr[0:9, dc * 128:(dc + 1) * 128],
                         ident_s[0:9, 0:9], start=True, stop=True,
                         is_transpose=True).then_inc("pe", 1)
            w.cp("ph0_vtr", "pe")
            w.matmul(PB[3][:, 0:4], onesr_s[0:1, :], flags[0:1, :],
                     start=True, stop=True).then_inc("pe", 1)
            w.cp("ph0_flagb", "pe")
        if en == "vector":
            w.wait("ph0_vtr")
            for dc in range(NDC):
                w.tensor_copy(vcol[:, :, dc], PB[0][:, dc * 16:dc * 16 + 9]).then_inc("ve", 1)
            w.cp("ph0_vcol", "ve")

    def ph0rw(en, w):
        if en == "vector":
            w.wait("ph0_rw16")
            w.tensor_copy(xgf32[0:8, :], rwrow16[0:8, :]).then_inc("ve", 1)
            w.cp("ph0_rw32", "ve")
            w.wait("ph0_rwtr2")
            for dc in range(NDC):
                w.tensor_copy(rw16[:, dc, :], PB[1][:, dc * 16:dc * 16 + 8]).then_inc("ve", 1)
            w.cp("ph0_rw16c", "ve")
        if en == "tensor":
            w.wait("ph0_rw32")
            for dc in range(NDC):
                w.matmul(PB[1][:, dc * 16: dc * 16 + 8],
                         xgf32[0:8, dc * 128:(dc + 1) * 128],
                         ident_s[0:8, 0:8], start=True, stop=True,
                         is_transpose=True).then_inc("pe", 1)
            w.cp("ph0_rwtr2", "pe")

    def ph0b(en, w):
        if en == "scalar":
            w.wait("ph0_vcol")
            w.activation(netd[:, :], vcol[:, 7, :], AF.Exp).then_inc("ac", 1)
            w.cp("ph0_expt", "ac")
            w.wait("ph0_negt")
            w.activation(ewcol[:, :], netd[:, :], AF.Exp).then_inc("ac", 1)
            w.cp("ph0_ew", "ac")
        if en == "vector":
            w.wait("ph0_expt")
            w.tensor_scalar(netd[:, :], netd[:, :], -1.0, None, ALU.mult).then_inc("ve", 1)
            w.cp("ph0_negt", "ve")
            w.wait("ph0_flagb")
            w.wait("ph0_vcol")
            w.tensor_copy(flagc[:, :], PB[3][:, 0:4]).then_inc("ve", 1)
            w.cp("ph0_flagc", "ve")
            for dc in range(NDC):
                w.scalar_tensor_tensor(tfm[:, dc:dc + 1], vcol[:, 8, dc:dc + 1],
                                       1.0, PB[3][:, 0:1], ALU.mult,
                                       ALU.mult).then_inc("ve", 1)
            w.cp("ph0_tfm", "ve")

    def lnrow_chain(en, w, tag, src_row, dst_row):
        if en == "vector":
            w.wait(f"{tag}_go")
            w.tensor_reduce(hstat[0:1, 0:1], src_row, AX.X, ALU.add,
                            negate=True).then_inc("ve", 1)
            w.cp(f"{tag}_c0", "ve")
            w.wait(f"{tag}_c1")
            w.tensor_scalar(hstat[0:1, 2:3], hstat[0:1, 0:1], 1.0 / D, None,
                            ALU.mult).then_inc("ve", 1)
            w.tensor_scalar(hstat[0:1, 3:4], hstat[0:1, 1:2], 1.0 / D, None,
                            ALU.mult).then_inc("ve", 1)
            ssync(w, "ve")
            w.tensor_tensor(hstat[0:1, 4:5], hstat[0:1, 2:3], hstat[0:1, 2:3],
                            ALU.mult).then_inc("ve", 1)
            ssync(w, "ve")
            w.tensor_tensor(hstat[0:1, 4:5], hstat[0:1, 3:4], hstat[0:1, 4:5],
                            ALU.subtract).then_inc("ve", 1)
            w.cp(f"{tag}_var", "ve")
            w.wait(f"{tag}_rstd")
            w.tensor_tensor(hstat[0:1, 6:7], hstat[0:1, 2:3], hstat[0:1, 5:6],
                            ALU.mult).then_inc("ve", 1)
            w.cp(f"{tag}_bias", "ve")
            ssync(w, "ve")
            w.tensor_scalar(dst_row, src_row, hstat[0:1, 5:6], hstat[0:1, 6:7],
                            ALU.mult, ALU.add).then_inc("ve", 1)
            w.cp(f"{tag}_z", "ve")
            w.cp(f"{tag}_h16", "ve")
        if en == "scalar":
            w.wait(f"{tag}_c0")
            w.activation(sqj[0:1, :], src_row, AF.Square,
                         accum_out=hstat[0:1, 1:2]).then_inc("ac", 1)
            w.cp(f"{tag}_c1", "ac")
            w.wait(f"{tag}_var")
            w.activation(hstat[0:1, 5:6], hstat[0:1, 4:5], AF.Ln,
                         bias=C1_LN()).then_inc("ac", 1)
            ssync(w, "ac")
            w.activation(hstat[0:1, 5:6], hstat[0:1, 5:6], AF.Exp,
                         scale=C1_NH()).then_inc("ac", 1)
            w.cp(f"{tag}_rstd", "ac")
        if en == "tensor":
            w.wait(f"{tag}_h16")
            for dc in range(NDC):
                w.matmul(PB[2][:, dc * 2:dc * 2 + 1],
                         dst_row[:, dc * 128:(dc + 1) * 128] if False else
                         bass.AP(dst_row.tensor, dst_row.offset + dc * 128,
                                 [[dst_row.ap[0][0], 1], [1, 128]]),
                         ident_s[0:1, 0:1], start=True, stop=True,
                         is_transpose=True).then_inc("pe", 1)
            w.cp(f"{tag}_htr", "pe")

    def ph0c(en, w):
        if en == "sync" and dbg_hr is not None:
            w.wait("ph0_ld")
            w.dma_start(out=dbg_hr.ap()[3:4, :], in_=hrow[0:1, :]).then_inc("st", 16)
            w.cp("dbg_pre", "st")
            w.wait("hx_xn")
            w.dma_start(out=dbg_h.ap()[0:1, 44:51], in_=hstat[0:1, 0:7]).then_inc("st", 16)
            w.cp("dbg_hx", "st")
        if en == "vector":
            w.wait("ph0_ld")
            if dbg_hr is not None:
                w.wait("dbg_pre")
            w.tensor_reduce(hstat[0:1, 0:1], hrow[0:1, :], AX.X, ALU.add,
                            negate=True).then_inc("ve", 1)
            w.cp("hx_c0", "ve")
            w.wait("hx_c1")
            w.tensor_scalar(hstat[0:1, 2:3], hstat[0:1, 0:1], 1.0 / D, None,
                            ALU.mult).then_inc("ve", 1)
            w.tensor_scalar(hstat[0:1, 3:4], hstat[0:1, 1:2], 1.0 / D, None,
                            ALU.mult).then_inc("ve", 1)
            ssync(w, "ve")
            w.tensor_tensor(hstat[0:1, 4:5], hstat[0:1, 2:3], hstat[0:1, 2:3],
                            ALU.mult).then_inc("ve", 1)
            ssync(w, "ve")
            w.tensor_tensor(hstat[0:1, 4:5], hstat[0:1, 3:4], hstat[0:1, 4:5],
                            ALU.subtract).then_inc("ve", 1)
            w.cp("hx_var", "ve")
            w.wait("hx_rstd")
            w.tensor_tensor(hstat[0:1, 6:7], hstat[0:1, 2:3], hstat[0:1, 5:6],
                            ALU.mult).then_inc("ve", 1)
            w.cp("hx_bias", "ve")
            ssync(w, "ve")
            w.tensor_scalar(hrow[0:1, :], hrow[0:1, :], hstat[0:1, 5:6],
                            hstat[0:1, 6:7], ALU.mult, ALU.add).then_inc("ve", 1)
            w.cp("hx_xn", "ve")
            if dbg_hr is not None:
                w.wait("dbg_hx")
            w.cp("h1_go", "ve")
        if en == "scalar":
            w.wait("hx_c0")
            w.activation(sqj[0:1, :], hrow[0:1, :], AF.Square,
                         accum_out=hstat[0:1, 1:2]).then_inc("ac", 1)
            w.cp("hx_c1", "ac")
            w.wait("hx_var")
            w.activation(hstat[0:1, 5:6], hstat[0:1, 4:5], AF.Ln,
                         bias=C1_LN()).then_inc("ac", 1)
            ssync(w, "ac")
            w.activation(hstat[0:1, 5:6], hstat[0:1, 5:6], AF.Exp,
                         scale=C1_NH()).then_inc("ac", 1)
            w.cp("hx_rstd", "ac")
        lnrow_chain(en, w, "h1", hrow[0:1, :], hrow2[0:1, :])
        if en == "vector":
            w.wait("h1_htr")
            w.wait("ph0_flagc")
            for dc in range(NDC):
                w.tensor_scalar(haloc[:, dc:dc + 1], PB[2][:, dc * 2:dc * 2 + 1],
                                vc("g1", dc), vc("b1", dc), ALU.mult,
                                ALU.add).then_inc("ve", 1)
            ssync(w, "ve")
            w.tensor_scalar(haloc[:, :], haloc[:, :], flagc[:, 1:2], None,
                            ALU.mult).then_inc("ve", 1)
            w.cp("ph0_haloc", "ve")

    def ph1(en, w):
        if en == "sync":
            def ph1_store(i):
                ssem = "sa" if i % 2 == 0 else "sb"
                w.wait(f"e_xn_{i}")
                w.dma_start(out=xn32_d.ap()[i * 128:(i + 1) * 128, :],
                            in_=xnt[:, i % 2, :]).then_inc(ssem, 16)
                w.cp(f"e_xnst_{i}", ssem)
                w.wait(f"e_xn16_{i}")
                w.dma_start(out=agx_in.ap()[i * 128:(i + 1) * 128, :],
                            in_=xn16t[:, i % 2, :]).then_inc(ssem, 16)
                w.cp(f"e_agst_{i}", ssem)
                w.wait(f"e_tre_{i}")
                w.dma_start(
                    out=xnfm_d.ap().rearrange("(a p) t -> p a t", p=128)[:, :, i * 128:(i + 1) * 128],
                    in_=trs[:, 0, :, :]).then_inc("sf", 16)
                w.dma_start(
                    out=z1fm_d.ap().rearrange("(a p) t -> p a t", p=128)[:, :, i * 128:(i + 1) * 128],
                    in_=trs[:, 1, :, :]).then_inc("sf", 16)
                w.cp(f"e_st_{i}", "sf")
            for i in range(NT):
                lsem = "la" if i % 2 == 0 else "lb"
                if i >= 2:
                    w.wait(f"e_tr_{i - 2}")
                w.dma_start(out=xt[:, i % 2, :],
                            in_=xs.ap()[i * 128:(i + 1) * 128, :]).then_inc(lsem, 16)
                w.cp(f"e_x_{i}", lsem)
                if i >= 1:
                    ph1_store(i - 1)
            ph1_store(NT - 1)
            w.cp("e_sa_done", "sa")
            w.cp("e_sb_done", "sb")
            w.cp("e_sf_done", "sf")
        if en == "vector":
            for i in range(NT):
                st8 = stat[:, i, :]
                w.wait(f"e_x_{i}")
                w.tensor_reduce(st8[:, 0:1], xt[:, i % 2, :], AX.X, ALU.add,
                                negate=True).then_inc("ve", 1)
                w.cp(f"e_c0_{i}", "ve")
                w.wait(f"e_sq_{i}")
                w.tensor_scalar(st8[:, 0:1], st8[:, 0:1], 1.0 / D, None,
                                ALU.mult).then_inc("ve", 1)
                w.tensor_scalar(st8[:, 1:2], st8[:, 1:2], 1.0 / D, None,
                                ALU.mult).then_inc("ve", 1)
                ssync(w, "ve")
                w.tensor_tensor(st8[:, 2:3], st8[:, 0:1], st8[:, 0:1],
                                ALU.mult).then_inc("ve", 1)
                ssync(w, "ve")
                w.tensor_tensor(st8[:, 2:3], st8[:, 1:2], st8[:, 2:3],
                                ALU.subtract).then_inc("ve", 1)
                w.cp(f"e_var_{i}", "ve")
                w.wait(f"e_rstd_{i}")
                w.tensor_tensor(st8[:, 3:4], st8[:, 0:1], st8[:, 2:3],
                                ALU.mult).then_inc("ve", 1)
                w.cp(f"e_bias_{i}", "ve")
                if i >= 2:
                    w.wait(f"e_xnst_{i - 2}")
                    w.wait(f"e_petr_{i - 2}")
                ssync(w, "ve")
                w.tensor_scalar(xnt[:, i % 2, :], xt[:, i % 2, :], st8[:, 2:3],
                                st8[:, 3:4], ALU.mult, ALU.add).then_inc("ve", 1)
                w.cp(f"e_xn_{i}", "ve")
                w.tensor_reduce(st8[:, 4:5], xnt[:, i % 2, :], AX.X, ALU.add,
                                negate=True).then_inc("ve", 1)
                if i >= 2:
                    w.wait(f"e_agst_{i - 2}")
                w.tensor_copy(xn16t[:, i % 2, :], xnt[:, i % 2, :]).then_inc("ve", 1)
                w.cp(f"e_xn16_{i}", "ve")
                w.wait(f"e_sq1_{i}")
                w.tensor_scalar(st8[:, 4:5], st8[:, 4:5], 1.0 / D, None,
                                ALU.mult).then_inc("ve", 1)
                w.tensor_scalar(st8[:, 5:6], st8[:, 5:6], 1.0 / D, None,
                                ALU.mult).then_inc("ve", 1)
                ssync(w, "ve")
                w.tensor_tensor(st8[:, 6:7], st8[:, 4:5], st8[:, 4:5],
                                ALU.mult).then_inc("ve", 1)
                ssync(w, "ve")
                w.tensor_tensor(st8[:, 6:7], st8[:, 5:6], st8[:, 6:7],
                                ALU.subtract).then_inc("ve", 1)
                w.cp(f"e_var1_{i}", "ve")
                w.wait(f"e_rstd1_{i}")
                w.tensor_tensor(st8[:, 7:8], st8[:, 4:5], st8[:, 6:7],
                                ALU.mult).then_inc("ve", 1)
                w.cp(f"e_bias1_{i}", "ve")
                if i >= 1:
                    w.wait(f"e_petr_{i - 1}")
                ssync(w, "ve")
                w.tensor_scalar(z1t[:, :], xnt[:, i % 2, :], st8[:, 6:7],
                                st8[:, 7:8], ALU.mult, ALU.add).then_inc("ve", 1)
                w.cp(f"e_z1_{i}", "ve")
                w.wait(f"e_petr_{i}")
                if i >= 1:
                    w.wait(f"e_st_{i - 1}")
                w.tensor_copy(trs[:, 0, 0:4, :], p3(PB[0], 128)).then_inc("ve", 1)
                w.tensor_copy(trs[:, 0, 4:8, :], p3(PB[1], 128)).then_inc("ve", 1)
                w.tensor_copy(trs[:, 1, 0:4, :], p3(PB[2], 128)).then_inc("ve", 1)
                w.tensor_copy(trs[:, 1, 4:8, :], p3(PB[3], 128)).then_inc("ve", 1)
                w.cp(f"e_tre_{i}", "ve")
                w.cp(f"e_tr_{i}", "ve")
        if en == "scalar":
            for i in range(NT):
                st8 = stat[:, i, :]
                w.wait(f"e_c0_{i}")
                w.activation(sqj[:, :], xt[:, i % 2, :], AF.Square,
                             accum_out=st8[:, 1:2]).then_inc("ac", 1)
                w.cp(f"e_sq_{i}", "ac")
                w.wait(f"e_var_{i}")
                w.activation(st8[:, 2:3], st8[:, 2:3], AF.Ln, bias=C_LN()).then_inc("ac", 1)
                ssync(w, "ac")
                w.activation(st8[:, 2:3], st8[:, 2:3], AF.Exp, scale=C_NH()).then_inc("ac", 1)
                w.cp(f"e_rstd_{i}", "ac")
                w.wait(f"e_xn_{i}")
                w.activation(sqj[:, :], xnt[:, i % 2, :], AF.Square,
                             accum_out=st8[:, 5:6]).then_inc("ac", 1)
                w.cp(f"e_sq1_{i}", "ac")
                w.wait(f"e_var1_{i}")
                w.activation(st8[:, 5:6], st8[:, 5:6], AF.Ln, bias=C_E32()).then_inc("ac", 1)
                ssync(w, "ac")
                w.activation(st8[:, 5:6], st8[:, 5:6], AF.Exp, scale=C_NH()).then_inc("ac", 1)
                w.activation(st8[:, 6:7], st8[:, 6:7], AF.Ln, bias=C_LN()).then_inc("ac", 1)
                ssync(w, "ac")
                w.activation(st8[:, 6:7], st8[:, 6:7], AF.Exp, scale=C_NH()).then_inc("ac", 1)
                w.cp(f"e_rstd1_{i}", "ac")
        if en == "tensor":
            for i in range(NT):
                w.wait(f"e_z1_{i}")
                if i == 0:
                    w.wait("ph0_vcol")
                    w.wait("ph0_rw16c")
                    w.wait("h1_htr")
                    w.wait("ph0_haloc")
                    w.wait("ph0_tfm")
                if i >= 1:
                    w.wait(f"e_tre_{i - 1}")
                for dc in range(NDC):
                    w.matmul(PB[dc // 4][:, (dc % 4) * 128:(dc % 4 + 1) * 128],
                             xnt[:, i % 2, dc * 128:(dc + 1) * 128],
                             ident_s[:, :], start=True, stop=True,
                             is_transpose=True).then_inc("pe", 1)
                for dc in range(NDC):
                    w.matmul(PB[2 + dc // 4][:, (dc % 4) * 128:(dc % 4 + 1) * 128],
                             z1t[:, dc * 128:(dc + 1) * 128],
                             ident_s[:, :], start=True, stop=True,
                             is_transpose=True).then_inc("pe", 1)
                w.cp(f"e_petr_{i}", "pe")
        if en == "gpsimd":
            w.wait(f"e_agst_{NT - 1}")
            w.collective_compute("AllGather", ALU.bypass, replica_groups=RG,
                                 ins=[agx_in.ap()], outs=[agx_out.ap()]).then_inc("cc", 1)
            w.cp("agx_done", "cc")

    def ph2(en, w):
        if en == "sync":
            w.wait("e_sa_done")
            w.wait("e_sb_done")
            w.wait("e_sf_done")
            for i in range(NT):
                lsem = "la" if i % 2 == 0 else "lb"
                if i >= 2:
                    w.wait(f"r_lg_{i - 2}")
                w.dma_start(
                    out=trs[:, i % 2, :, :],
                    in_=xnfm_d.ap().rearrange("(a p) t -> p a t", p=128)[:, :, i * 128:(i + 1) * 128],
                ).then_inc(lsem, 16)
                w.cp(f"r_fm_{i}", lsem)
            if dbg_rt is not None:
                w.wait(f"r_lg_{NT - 1}")
                w.dma_start(out=dbg_rt.ap()[:, 370:498],
                            in_=lgbuf[:, :, :].rearrange("p a b -> p (a b)")).then_inc("st", 16)
                w.cp("dbg_lg1", "st")
                w.wait("r_sub")
                w.dma_start(out=dbg_rt.ap()[:, 498:626],
                            in_=routc[:, :, :].rearrange("p a b -> p (a b)")).then_inc("st", 16)
                w.cp("dbg_lg2", "st")
                w.wait("r_exp")
                w.dma_start(out=dbg_rt.ap()[:, 626:754],
                            in_=routc2[:, :, :].rearrange("p a b -> p (a b)")).then_inc("st", 16)
                w.cp("dbg_lg3", "st")
                w.wait("r_gidx")
                w.dma_start(out=dbg_rt.ap()[:, 0:64], in_=recs[:, :]).then_inc("st", 16)
                w.dma_start(out=dbg_rt.ap()[:, 64:192], in_=prefx[:, :]).then_inc("st", 16)
                w.dma_start(out=dbg_rt.ap()[:, 192:320], in_=offsb[:, :]).then_inc("st", 16)
                w.dma_start(out=dbg_rt.ap()[:, 320:328], in_=offmb[:, :]).then_inc("st", 16)
                w.dma_start(out=dbg_rt.ap()[:, 328:329], in_=tsum[:, :], allow_slow_non_contiguous=True).then_inc("st", 16)
                w.dma_start(out=dbg_rt.ap()[:, 329:330], in_=offsc[:, :], allow_slow_non_contiguous=True).then_inc("st", 16)
                w.dma_start(out=dbg_rt.ap()[0:8, 330:338], in_=cnts[:, :]).then_inc("st", 16)
                w.dma_start(out=dbg_rt.ap()[:, 338:354], in_=r16a[:, :]).then_inc("st", 16)
                w.dma_start(out=dbg_rt.ap()[:, 354:370], in_=i16a[:, :]).then_inc("st", 16)
                w.cp("dbg_rt_st", "st")
        if en == "tensor":
            for i in range(NT):
                w.wait(f"r_fm_{i}")
                if i >= 4:
                    w.wait(f"r_lg_{i - 4}")
                for dc in range(NDC):
                    w.matmul(PB[4 + i % 4][:, 0:8],
                             trs[:, i % 2, dc, :], rw16[:, dc, :],
                             start=(dc == 0), stop=(dc == NDC - 1)).then_inc("pe", 1)
                w.cp(f"r_mm_{i}", "pe")
            w.wait("r_mask")
            w.matmul(PB[5][:, 0:128], ustr_s[:, :],
                     routc[:, :, :].rearrange("p a b -> p (a b)"),
                     start=True, stop=True).then_inc("pe", 1)
            w.matmul(PB[5][:, 128:129],
                     routc[:, :, :].rearrange("p a b -> p (a b)"),
                     onesc_s[:, :], start=True, stop=True).then_inc("pe", 1)
            w.cp("r_pfx", "pe")
            w.wait("r_pfxe")
            w.matmul(PB[5][:, 129:130], ctile_s[:, :], tsum[:, :],
                     start=True, stop=True).then_inc("pe", 1)
            w.cp("r_offs", "pe")
            w.wait("r_offse")
            w.matmul(PB[5][0:1, 130:258], offsc[:, :], ident_s[:, :],
                     start=True, stop=True, is_transpose=True).then_inc("pe", 1)
            w.cp("r_offtr", "pe")
            w.wait("r_offre")
            w.matmul(PB[6][:, 0:128], onesr_s[0:1, :], offsr[0:1, :],
                     start=True, stop=True).then_inc("pe", 1)
            w.matmul(PB[6][0:1, 128:256], onesc_s[:, 0:1],
                     routc[:, :, :].rearrange("p a b -> p (a b)"),
                     start=True, stop=True).then_inc("pe", 1)
            w.cp("r_offb", "pe")
            w.wait("r_cnts_ld")
            w.wait("r_cnt8")
            w.matmul(PB[6][0:1, 256:264], selprev[:, :], cnts[:, :],
                     start=True, stop=True).then_inc("pe", 1)
            w.cp("r_offm_mm", "pe")
            w.wait("r_offm")
            w.matmul(PB[6][:, 264:272], onesr_s[0:1, :], offm[0:1, :],
                     start=True, stop=True).then_inc("pe", 1)
            w.cp("r_offmb_mm", "pe")
        if en == "vector":
            for i in range(NT):
                w.wait(f"r_mm_{i}")
                w.tensor_scalar(lgbuf[:, i, :], PB[4 + i % 4][:, 0:8],
                                stat[:, i, 5:6], None, ALU.mult).then_inc("ve", 1)
                w.cp(f"r_lg_{i}", "ve")
            L3 = lgbuf[:, :, :]
            i8 = iota8t[:, :].rearrange("p (a b) -> p a b", b=8)
            w.wait("ph0_iota")
            ssync(w, "ve")
            w.tensor_reduce(tmpc[:, :], L3, AX.X, ALU.max, negate=True).then_inc("ve", 1)
            ssync(w, "ve")
            a_, b_ = bass.broadcast_tensor_aps(L3, tmpc[:, :, None])
            w.tensor_tensor(routc[:, :, :], a_, b_, ALU.add).then_inc("ve", 1)
            w.cp("r_sub", "ve")
            w.wait("r_exp")
            w.tensor_reduce(tmpc[:, :], routc2[:, :, :], AX.X, ALU.add).then_inc("ve", 1)
            ssync(w, "ve")
            w.reciprocal(tmpc2[:, :], tmpc[:, :]).then_inc("ve", 1)
            ssync(w, "ve")
            if dbg_rt is not None:
                w.wait("dbg_lg1")
            a_, b_ = bass.broadcast_tensor_aps(routc2[:, :, :], tmpc2[:, :, None])
            w.tensor_tensor(lgbuf[:, :, :], a_, b_, ALU.mult).then_inc("ve", 1)
            w.tensor_reduce(w1c[:, :], lgbuf[:, :, :], AX.X, ALU.max).then_inc("ve", 1)
            ssync(w, "ve")
            if dbg_rt is not None:
                w.wait("dbg_lg2")
            a_, b_ = bass.broadcast_tensor_aps(lgbuf[:, :, :], w1c[:, :, None])
            w.tensor_tensor(routc[:, :, :], a_, b_, ALU.is_ge).then_inc("ve", 1)
            if dbg_rt is not None:
                w.wait("dbg_lg3")
            w.tensor_scalar(routc2[:, :, :], routc[:, :, :], -BIG, BIG, ALU.mult,
                            ALU.add).then_inc("ve", 1)
            w.tensor_tensor(routc2[:, :, :], i8, routc2[:, :, :],
                            ALU.add).then_inc("ve", 1)
            w.tensor_reduce(i16a[:, :], routc2[:, :, :], AX.X, ALU.min).then_inc("ve", 1)
            ssync(w, "ve")
            a_, b_ = bass.broadcast_tensor_aps(i8, i16a[:, :, None])
            w.tensor_tensor(routc[:, :, :], a_, b_, ALU.is_equal).then_inc("ve", 1)
            w.tensor_scalar(routc2[:, :, :], routc[:, :, :], -BIG, None,
                            ALU.mult).then_inc("ve", 1)
            w.tensor_tensor(routc2[:, :, :], lgbuf[:, :, :], routc2[:, :, :],
                            ALU.add).then_inc("ve", 1)
            w.tensor_reduce(w2c[:, :], routc2[:, :, :], AX.X, ALU.max).then_inc("ve", 1)
            ssync(w, "ve")
            a_, b_ = bass.broadcast_tensor_aps(routc2[:, :, :], w2c[:, :, None])
            w.tensor_tensor(routc2[:, :, :], a_, b_, ALU.is_ge).then_inc("ve", 1)
            w.tensor_scalar(routc2[:, :, :], routc2[:, :, :], -BIG, BIG, ALU.mult,
                            ALU.add).then_inc("ve", 1)
            w.tensor_tensor(routc2[:, :, :], i8, routc2[:, :, :],
                            ALU.add).then_inc("ve", 1)
            w.tensor_reduce(i16b[:, :], routc2[:, :, :], AX.X, ALU.min).then_inc("ve", 1)
            ssync(w, "ve")
            a_, b_ = bass.broadcast_tensor_aps(i8, i16b[:, :, None])
            w.tensor_tensor(routc2[:, :, :], a_, b_, ALU.is_equal).then_inc("ve", 1)
            w.tensor_tensor(routc[:, :, :], routc[:, :, :], routc2[:, :, :],
                            ALU.add).then_inc("ve", 1)
            w.cp("r_mask", "ve")
            w.tensor_tensor(tmpc[:, :], w1c[:, :], w2c[:, :], ALU.add).then_inc("ve", 1)
            ssync(w, "ve")
            w.reciprocal(tmpc[:, :], tmpc[:, :]).then_inc("ve", 1)
            ssync(w, "ve")
            w.tensor_tensor(w1c[:, :], w1c[:, :], tmpc[:, :], ALU.mult).then_inc("ve", 1)
            w.tensor_tensor(w2c[:, :], w2c[:, :], tmpc[:, :], ALU.mult).then_inc("ve", 1)
            w.cp("r_w", "ve")
            w.wait("r_pfx")
            w.tensor_copy(prefx[:, :], PB[5][:, 0:128]).then_inc("ve", 1)
            w.tensor_copy(tsum[:, :], PB[5][:, 128:129]).then_inc("ve", 1)
            w.cp("r_pfxe", "ve")
            w.wait("r_offs")
            w.tensor_copy(offsc[:, :], PB[5][:, 129:130]).then_inc("ve", 1)
            w.cp("r_offse", "ve")
            w.wait("r_offtr")
            w.tensor_copy(offsr[0:1, :], PB[5][0:1, 130:258]).then_inc("ve", 1)
            w.cp("r_offre", "ve")
            w.wait("r_offb")
            w.tensor_copy(offsb[:, :], PB[6][:, 0:128]).then_inc("ve", 1)
            w.tensor_copy(cntrow[0:1, :], PB[6][0:1, 128:256]).then_inc("ve", 1)
            w.tensor_reduce(offm[0:1, 0:8],
                            bass.AP(cntrow, 0, [[128, 1], [1, 8], [8, 16]]),
                            AX.X, ALU.add).then_inc("ve", 1)
            w.cp("r_cnt8", "ve")
            w.wait("r_offm_mm")
            w.tensor_copy(offm[0:1, :], PB[6][0:1, 256:264]).then_inc("ve", 1)
            w.cp("r_offm", "ve")
            w.wait("r_offmb_mm")
            w.tensor_copy(offmb[:, :], PB[6][:, 264:272]).then_inc("ve", 1)
            pf3 = prefx[:, :].rearrange("p (a b) -> p a b", b=8)
            ob3 = offsb[:, :].rearrange("p (a b) -> p a b", b=8)
            w.tensor_tensor(routc2[:, :, :], pf3, ob3, ALU.add).then_inc("ve", 1)
            w.tensor_tensor(routc2[:, :, :], routc2[:, :, :],
                            bass.AP(offmb, 0, [[8, 128], [0, 16], [1, 8]]),
                            ALU.add).then_inc("ve", 1)
            w.cp("r_rank3", "ve")
            for (idxs, ranks) in [(i16a, r16a), (i16b, r16b)]:
                a_, b_ = bass.broadcast_tensor_aps(i8, idxs[:, :, None])
                w.tensor_tensor(lgbuf[:, :, :], a_, b_, ALU.is_equal).then_inc("ve", 1)
                w.tensor_tensor(lgbuf[:, :, :], lgbuf[:, :, :], routc2[:, :, :],
                                ALU.mult).then_inc("ve", 1)
                w.tensor_reduce(ranks[:, :], lgbuf[:, :, :], AX.X,
                                ALU.add).then_inc("ve", 1)
            w.cp("r_rank", "ve")
            ssync(w, "ve")
            w.tensor_copy(recs[:, 0:16], i16a[:, :]).then_inc("ve", 1)
            w.tensor_copy(recs[:, 16:32], r16a[:, :]).then_inc("ve", 1)
            w.tensor_copy(recs[:, 32:48], i16b[:, :]).then_inc("ve", 1)
            w.tensor_copy(recs[:, 48:64], r16b[:, :]).then_inc("ve", 1)
            w.cp("r_recs", "ve")
            ZROW = float(NCORE * CAP)
            for (idxs, ranks, gout) in [(i16a, r16a, gidx1), (i16b, r16b, gidx2)]:
                w.tensor_scalar(tmpc[:, :], ranks[:, :], float(CAP), None,
                                ALU.is_lt).then_inc("ve", 1)
                w.tensor_scalar(tmpc2[:, :], idxs[:, :], float(CAP), None,
                                ALU.mult).then_inc("ve", 1)
                ssync(w, "ve")
                w.tensor_tensor(tmpc2[:, :], tmpc2[:, :], ranks[:, :],
                                ALU.add).then_inc("ve", 1)
                ssync(w, "ve")
                w.tensor_scalar(tmpc2[:, :], tmpc2[:, :], -ZROW, None,
                                ALU.add).then_inc("ve", 1)
                ssync(w, "ve")
                w.tensor_tensor(tmpc2[:, :], tmpc2[:, :], tmpc[:, :],
                                ALU.mult).then_inc("ve", 1)
                ssync(w, "ve")
                w.tensor_scalar(tmpc2[:, :], tmpc2[:, :], ZROW, None,
                                ALU.add).then_inc("ve", 1)
                ssync(w, "ve")
                w.tensor_copy(gout[:, :], tmpc2[:, :]).then_inc("ve", 1)
            w.cp("r_gidx", "ve")
        if en == "scalar":
            w.wait("r_sub")
            w.activation(routc2[:, :, :], routc[:, :, :], AF.Exp).then_inc("ac", 1)
            w.cp("r_exp", "ac")
        if en == "gpsimd":
            w.wait("r_cnt8")
            w.dma_start(out=agcnt_in.ap(), in_=offm[0:1, 0:8]).then_inc("gr", 16)
            w.cp("r_cnt_st", "gr")
            w.wait("r_cnt_st")
            w.collective_compute("AllGather", ALU.bypass, replica_groups=RG,
                                 ins=[agcnt_in.ap()], outs=[agcnt_out.ap()]).then_inc("ce", 1)
            w.cp("agcnt_done", "ce")
            w.wait("agcnt_done")
            w.dma_start(out=cnts[:, :], in_=agcnt_out.ap()).then_inc("gr", 16)
            w.cp("r_cnts_ld", "gr")
            w.wait("r_recs")
            w.wait("r_offm_mm")
            w.dma_start(out=agrec_in.ap(), in_=recs[:, :]).then_inc("gr", 16)
            w.cp("r_rec_st", "gr")
            w.wait("r_rec_st")
            w.collective_compute("AllGather", ALU.bypass, replica_groups=RG,
                                 ins=[agrec_in.ap()], outs=[agrec_out.ap()]).then_inc("cd", 1)
            w.cp("agrec_done", "cd")

    def ph3q(en, w, q):
        t0 = q * Q
        if en == "sync":
            if q == 0:
                w.wait("e_sa_done")
                w.wait("e_sb_done")
                w.wait("e_sf_done")
            if q >= 1:
                w.wait(f"a_rwkv_{q - 1}_7")
            w.dma_start(out=z1h[:, :, :],
                        in_=z1fm_d.ap().rearrange("(a p) t -> p a t", p=128)[:, :, t0:t0 + Q],
                        ).then_inc("ld", 16)
            w.cp(f"a_z1_{q}", "ld")
            if dbg_sc is not None and q == 0:
                w.wait("ph0_haloc")
                w.wait("ph0_tfm")
                w.dma_start(out=dbg_h16.ap(), in_=haloc[:, :]).then_inc("st", 16)
                w.dma_start(out=dbg_h.ap()[:, 8:12], in_=flagc[:, :], allow_slow_non_contiguous=True).then_inc("st", 16)
                w.dma_start(out=dbg_h.ap()[:, 12:20], in_=vcol[:, 0, :], allow_slow_non_contiguous=True).then_inc("st", 16)
                w.dma_start(out=dbg_h.ap()[:, 20:28], in_=vcol[:, 1, :], allow_slow_non_contiguous=True).then_inc("st", 16)
                w.dma_start(out=dbg_h.ap()[:, 28:36], in_=tfm[:, :], allow_slow_non_contiguous=True).then_inc("st", 16)
                w.dma_start(out=dbg_h.ap()[:, 36:44], in_=ewcol[:, :], allow_slow_non_contiguous=True).then_inc("st", 16)
                w.wait("h1_z")
                w.dma_start(out=dbg_hr.ap()[0:1, :], in_=hrow[0:1, :]).then_inc("st", 16)
                w.dma_start(out=dbg_hr.ap()[1:2, :], in_=hrow2[0:1, :]).then_inc("st", 16)
                w.dma_start(out=dbg_hr.ap()[2:3, 0:7], in_=hstat[0:1, 0:7]).then_inc("st", 16)
                w.cp("dbg_h_st", "st")
                w.wait(f"a_mixk_{q}")
                w.dma_start(out=dbg_mx.ap()[:, 0:4096],
                            in_=mixa[:, :, :]).then_inc("st", 16)
                w.wait(f"a_mixv_{q}")
                w.dma_start(out=dbg_mx.ap()[:, 4096:8192],
                            in_=mixb[:, :, :]).then_inc("st", 16)
                w.cp("dbg_mx_st", "st")
                for dc2 in range(NDC):
                    w.wait(f"a_kvev_{q}_{dc2}")
                    w.dma_start(out=dbg_sc.ap()[:, 12288 + dc2 * 512:12288 + (dc2 + 1) * 512],
                                in_=kch[:, :]).then_inc("st", 16)
                    w.dma_start(out=dbg_sc.ap()[:, 16384 + dc2 * 512:16384 + (dc2 + 1) * 512],
                                in_=vch[:, :]).then_inc("st", 16)
                    w.cp(f"dbg_kv_{dc2}", "st")
                    w.wait(f"a_scan_{q}_{dc2}")
                    w.dma_start(out=dbg_sc.ap()[:, dc2 * 512:(dc2 + 1) * 512],
                                in_=vch[:, :]).then_inc("st", 16)
                    w.dma_start(out=dbg_sc.ap()[:, 4096 + dc2 * 512:4096 + (dc2 + 1) * 512],
                                in_=kch[:, :]).then_inc("st", 16)
                    w.wait(f"a_recip_{q}_{dc2}")
                    w.dma_start(out=dbg_sc.ap()[:, 8192 + dc2 * 512:8192 + (dc2 + 1) * 512],
                                in_=ekb[:, :]).then_inc("st", 16)
                    w.cp(f"dbg_sc_{dc2}", "st")
                w.wait("a_rwkv_0_7")
                w.dma_start(out=dbg_wk.ap(),
                            in_=wkvh[:, :, :]).then_inc("st", 16)
                w.cp("dbg_wk_st", "st")
            for j in range(4):
                m = q * 4 + j
                lsem = "la" if j % 2 == 0 else "lb"
                if m < 2:
                    w.wait(f"r_lg_{NT - 1}")
                if m >= 2:
                    w.wait(f"a_x2_{(m - 2) // 4}_{(m - 2) % 4}")
                w.dma_start(out=xt[:, j % 2, :],
                            in_=xn32_d.ap()[t0 + j * 128: t0 + j * 128 + 128, :],
                            ).then_inc(lsem, 16)
                w.cp(f"a_xnrow_{q}_{j}", lsem)
            for j in range(4):
                ssem = "sa" if j % 2 == 0 else "sb"
                w.wait(f"a_x2_{q}_{j}")
                if q == 0 and j == 1:
                    w.wait("agx_done")
                w.dma_start(out=x2_d.ap()[t0 + j * 128: t0 + j * 128 + 128, :],
                            in_=xnt[:, j % 2, :]).then_inc(ssem, 16)
                w.cp(f"a_x2st_{q}_{j}", ssem)
                w.wait(f"a_ztre_{q}_{j}")
                w.dma_start(
                    out=z2fm_d.ap().rearrange("(a p) t -> p a t", p=128)[:, :, t0 + j * 128:t0 + (j + 1) * 128],
                    in_=trs[:, j % 2, :, :]).then_inc(ssem, 16)
                w.cp(f"a_z2st_{q}_{j}", ssem)
            if q == NQ - 1:
                w.dma_start(out=agx2_in.ap(), in_=xnt[127:128, 1, :]).then_inc("sc", 16)
                w.cp("x2row_pub", "sc")
                w.wait("a_scstw_3")
                w.dma_start(out=agcar_in.ap(), in_=scst[:, :]).then_inc("sc", 16)
                w.cp("car_pub", "sc")
        if en == "gpsimd":
            if q >= 1:
                w.wait(f"a_womm_{q - 1}_3")
            w.dma_start(out=wkc[:, :, :],
                        in_=wr_d.ap().rearrange("(a p) n -> p a n", p=128)).then_inc("gc", 16)
            w.cp(f"w_wr_{q}", "gc")
            w.wait(f"a_rmm_{q}_7")
            w.dma_start(out=wkc[:, :, :],
                        in_=wo_d.ap().rearrange("(a p) n -> p a n", p=128)).then_inc("gc", 16)
            w.cp(f"w_wo_{q}", "gc")
        if en == "vector":
            w.wait(f"a_z1_{q}")
            if q == 0:
                w.wait("ph0_vcol")
                w.wait("ph0_haloc")
            for dc in range(NDC):
                w.tensor_scalar(xn1b[:, dc, 1:Q + 1], z1h[:, dc, :],
                                vc("g1", dc), vc("b1", dc), ALU.mult,
                                ALU.add).then_inc("ve", 1)
            if q == 0:
                w.tensor_copy(xn1b[:, :, 0:1], haloc[:, :, None]).then_inc("ve", 1)
            else:
                w.tensor_copy(xn1b[:, :, 0:1], ynprev[:, :, None]).then_inc("ve", 1)
            ssync(w, "ve")
            w.tensor_copy(ynprev[:, :, None], xn1b[:, :, Q:Q + 1]).then_inc("ve", 1)
            ssync(w, "ve")
            for dc in range(NDC):
                w.tensor_tensor(z1h[:, dc, :], xn1b[:, dc, 1:Q + 1],
                                xn1b[:, dc, 0:Q], ALU.subtract).then_inc("ve", 1)
            w.cp(f"a_diff_{q}", "ve")
            for dc in range(NDC):
                w.scalar_tensor_tensor(mixa[:, dc, :], z1h[:, dc, :], vc("mk", dc),
                                       xn1b[:, dc, 0:Q], ALU.mult,
                                       ALU.add).then_inc("ve", 1)
            w.cp(f"a_mixk_{q}", "ve")
            for dc in range(NDC):
                w.scalar_tensor_tensor(mixb[:, dc, :], z1h[:, dc, :], vc("mv", dc),
                                       xn1b[:, dc, 0:Q], ALU.mult,
                                       ALU.add).then_inc("ve", 1)
            w.cp(f"a_mixv_{q}", "ve")
            for dc2 in range(NDC):
                w.wait(f"a_kvmm_{q}_{dc2}")
                if dc2 >= 1:
                    w.wait(f"a_recip_{q}_{dc2 - 1}")
                    if dbg_sc is not None and q == 0:
                        w.wait(f"dbg_sc_{dc2 - 1}")
                w.tensor_copy(kch[:, :], PB[0 + (dc2 % 2) * 2][:, :]).then_inc("ve", 1)
                w.tensor_copy(vch[:, :], PB[1 + (dc2 % 2) * 2][:, :]).then_inc("ve", 1)
                w.cp(f"a_kvev_{q}_{dc2}", "ve")
                w.wait(f"a_ek_{q}_{dc2}")
                w.tensor_tensor(ekv[:, :], ekb[:, :], vch[:, :], ALU.mult).then_inc("ve", 1)
                w.tensor_scalar(ewb[:, :], kch[:, :], 0.0, ewcol[:, dc2:dc2 + 1],
                                ALU.mult, ALU.add).then_inc("ve", 1)
                ini_a = scst[:, dc2:dc2 + 1] if q > 0 else 0.0
                ini_b = scst[:, 8 + dc2:8 + dc2 + 1] if q > 0 else 0.0
                if dbg_sc is not None and q == 0:
                    w.wait(f"dbg_kv_{dc2}")
                w.tensor_tensor_scan(vch[:, :], ewb[:, :], ekv[:, :], ini_a,
                                     ALU.mult, ALU.add).then_inc("ve", 1)
                w.tensor_tensor_scan(kch[:, :], ewb[:, :], ekb[:, :], ini_b,
                                     ALU.mult, ALU.add).then_inc("ve", 1)
                w.tensor_copy(scst[:, dc2:dc2 + 1], vch[:, Q - 1:Q]).then_inc("ve", 1)
                ssync(w, "ve")
                w.tensor_copy(scst[:, 8 + dc2:8 + dc2 + 1],
                              kch[:, Q - 1:Q]).then_inc("ve", 1)
                if q == 0:
                    w.tensor_copy(corra[:, dc2, :], vch[:, 0:CORRW]).then_inc("ve", 1)
                    w.tensor_copy(corrb[:, dc2, :], kch[:, 0:CORRW]).then_inc("ve", 1)
                w.cp(f"a_scan_{q}_{dc2}", "ve")
                if q == NQ - 1 and dc2 == NDC - 1:
                    w.cp("a_scstw_3", "ve")
                w.wait(f"a_recip_{q}_{dc2}")
                w.tensor_tensor(wkvh[:, dc2, :], vch[:, :], ekb[:, :],
                                ALU.mult).then_inc("ve", 1)
                w.cp(f"a_wkv_{q}_{dc2}", "ve")
            w.wait(f"a_kvmm_{q}_7")
            if dbg_sc is not None and q == 0:
                w.wait("dbg_mx_st")
            for dc in range(NDC):
                w.scalar_tensor_tensor(mixa[:, dc, :], z1h[:, dc, :], vc("mr", dc),
                                       xn1b[:, dc, 0:Q], ALU.mult,
                                       ALU.add).then_inc("ve", 1)
            w.cp(f"a_rmix_{q}", "ve")
            for dc2 in range(NDC):
                w.wait(f"a_rsig_{q}_{dc2}")
                if q == 0:
                    w.tensor_copy(rcorr[:, dc2, :], z1h[:, dc2, 0:CORRW]).then_inc("ve", 1)
                w.tensor_tensor(wkvh[:, dc2, :], wkvh[:, dc2, :], z1h[:, dc2, :],
                                ALU.mult).then_inc("ve", 1)
                w.cp(f"a_rwkv_{q}_{dc2}", "ve")
            for j in range(4):
                w.wait(f"a_womm_{q}_{j}")
                w.wait(f"a_xnrow_{q}_{j}")
                w.tensor_tensor(xnt[:, j % 2, 0:512], PB[4 + (j % 2) * 2][:, :],
                                xt[:, j % 2, 0:512], ALU.add).then_inc("ve", 1)
                w.tensor_tensor(xnt[:, j % 2, 512:1024], PB[5 + (j % 2) * 2][:, :],
                                xt[:, j % 2, 512:1024], ALU.add).then_inc("ve", 1)
                w.cp(f"a_x2_{q}_{j}", "ve")
                w.tensor_reduce(tmpc[:, 0:1], xnt[:, j % 2, :], AX.X, ALU.add,
                                negate=True).then_inc("ve", 1)
                w.cp(f"a_c2_{q}_{j}", "ve")
                w.wait(f"a_sq2_{q}_{j}")
                w.tensor_scalar(tmpc[:, 0:1], tmpc[:, 0:1], 1.0 / D, None,
                                ALU.mult).then_inc("ve", 1)
                w.tensor_scalar(tmpc[:, 1:2], tmpc[:, 1:2], 1.0 / D, None,
                                ALU.mult).then_inc("ve", 1)
                ssync(w, "ve")
                w.tensor_tensor(tmpc[:, 2:3], tmpc[:, 0:1], tmpc[:, 0:1],
                                ALU.mult).then_inc("ve", 1)
                ssync(w, "ve")
                w.tensor_tensor(tmpc[:, 2:3], tmpc[:, 1:2], tmpc[:, 2:3],
                                ALU.subtract).then_inc("ve", 1)
                w.cp(f"a_var2_{q}_{j}", "ve")
                w.wait(f"a_rstd2_{q}_{j}")
                w.tensor_tensor(tmpc[:, 3:4], tmpc[:, 0:1], tmpc[:, 2:3],
                                ALU.mult).then_inc("ve", 1)
                w.cp(f"a_bias2_{q}_{j}", "ve")
                if j >= 1:
                    w.wait(f"a_ztr_{q}_{j - 1}")
                elif q >= 1:
                    w.wait(f"a_ztr_{q - 1}_3")
                ssync(w, "ve")
                w.tensor_scalar(z1t[:, :], xnt[:, j % 2, :], tmpc[:, 2:3],
                                tmpc[:, 3:4], ALU.mult, ALU.add).then_inc("ve", 1)
                w.cp(f"a_z2_{q}_{j}", "ve")
                w.wait(f"a_ztr_{q}_{j}")
                if q == 0 and j == 0:
                    w.wait("r_offmb_mm")
                else:
                    pj = q * 4 + j - 1
                    w.wait(f"a_z2st_{pj // 4}_{pj % 4}")
                w.tensor_copy(trs[:, j % 2, 0:4, :], p3(PB[0], 128)).then_inc("ve", 1)
                w.tensor_copy(trs[:, j % 2, 4:8, :], p3(PB[1], 128)).then_inc("ve", 1)
                w.cp(f"a_ztre_{q}_{j}", "ve")
        if en == "scalar":
            for dc2 in range(NDC):
                w.wait(f"a_kvev_{q}_{dc2}")
                w.activation(ekb[:, :], kch[:, :], AF.Exp).then_inc("ac", 1)
                if q == 0:
                    w.wait("ph0_tfm")
                    w.activation(ekb[:, 0:1], kch[:, 0:1], AF.Exp,
                                 bias=tfm[:, dc2:dc2 + 1]).then_inc("ac", 1)
                w.cp(f"a_ek_{q}_{dc2}", "ac")
                w.wait(f"a_scan_{q}_{dc2}")
                w.activation(ekb[:, :], kch[:, :], AF.Ln, bias=C_WKV()).then_inc("ac", 1)
                w.activation(ekb[:, :], ekb[:, :], AF.Exp, scale=C_N1()).then_inc("ac", 1)
                w.cp(f"a_recip_{q}_{dc2}", "ac")
            for dc2 in range(NDC):
                w.wait(f"a_rmm_{q}_{dc2}")
                w.activation(z1h[:, dc2, :], PB[4 + dc2 % 4][:, :],
                             AF.Sigmoid).then_inc("ac", 1)
                w.cp(f"a_rsig_{q}_{dc2}", "ac")
            for j in range(4):
                w.wait(f"a_c2_{q}_{j}")
                w.activation(sqj[:, :], xnt[:, j % 2, :], AF.Square,
                             accum_out=tmpc[:, 1:2]).then_inc("ac", 1)
                w.cp(f"a_sq2_{q}_{j}", "ac")
                w.wait(f"a_var2_{q}_{j}")
                w.activation(tmpc[:, 2:3], tmpc[:, 2:3], AF.Ln,
                             bias=C_LN()).then_inc("ac", 1)
                ssync(w, "ac")
                w.activation(tmpc[:, 2:3], tmpc[:, 2:3], AF.Exp,
                             scale=C_NH()).then_inc("ac", 1)
                w.cp(f"a_rstd2_{q}_{j}", "ac")
        if en == "tensor":
            for dc2 in range(NDC):
                if dc2 == 0:
                    w.wait(f"a_mixk_{q}")
                    w.wait(f"a_mixv_{q}")
                    w.wait("w_wk")
                    w.wait("w_wv")
                    if q == 0:
                        w.wait("r_gidx")
                        w.wait(f"e_tr_{NT - 1}")
                    else:
                        w.wait(f"a_ztre_{q - 1}_3")
                if dc2 >= 2:
                    w.wait(f"a_kvev_{q}_{dc2 - 2}")
                for dcK in range(NDC):
                    w.matmul(PB[0 + (dc2 % 2) * 2][:, :],
                             wka[:, dcK, dc2 * 128:(dc2 + 1) * 128],
                             mixa[:, dcK, :],
                             start=(dcK == 0), stop=(dcK == NDC - 1)).then_inc("pe", 1)
                for dcK in range(NDC):
                    w.matmul(PB[1 + (dc2 % 2) * 2][:, :],
                             wkb[:, dcK, dc2 * 128:(dc2 + 1) * 128],
                             mixb[:, dcK, :],
                             start=(dcK == 0), stop=(dcK == NDC - 1)).then_inc("pe", 1)
                w.cp(f"a_kvmm_{q}_{dc2}", "pe")
            for dc2 in range(NDC):
                if dc2 == 0:
                    w.wait(f"a_rmix_{q}")
                    w.wait(f"w_wr_{q}")
                    if q == 0:
                        w.wait("r_gidx")
                    else:
                        w.wait(f"a_x2_{q - 1}_3")
                if dc2 >= 4:
                    w.wait(f"a_rsig_{q}_{dc2 - 4}")
                for dcK in range(NDC):
                    w.matmul(PB[4 + dc2 % 4][:, :],
                             wkc[:, dcK, dc2 * 128:(dc2 + 1) * 128],
                             mixa[:, dcK, :],
                             start=(dcK == 0), stop=(dcK == NDC - 1)).then_inc("pe", 1)
                w.cp(f"a_rmm_{q}_{dc2}", "pe")
            for j in range(4):
                if j == 0:
                    w.wait(f"w_wo_{q}")
                    w.wait(f"a_rwkv_{q}_7")
                    w.wait(f"a_rsig_{q}_7")
                if j >= 2:
                    w.wait(f"a_x2_{q}_{j - 2}")
                for n in range(2):
                    for dcK in range(NDC):
                        w.matmul(PB[4 + (j % 2) * 2 + n][:, :],
                                 wkvh[:, dcK, j * 128:(j + 1) * 128],
                                 wkc[:, dcK, n * 512:(n + 1) * 512],
                                 start=(dcK == 0), stop=(dcK == NDC - 1)).then_inc("pe", 1)
                w.cp(f"a_womm_{q}_{j}", "pe")
                w.wait(f"a_z2_{q}_{j}")
                if not (q == 0 and j == 0):
                    pj = q * 4 + j - 1
                    w.wait(f"a_ztre_{pj // 4}_{pj % 4}")
                for dc in range(NDC):
                    w.matmul(PB[dc // 4][:, (dc % 4) * 128:(dc % 4 + 1) * 128],
                             z1t[:, dc * 128:(dc + 1) * 128],
                             ident_s[:, :], start=True, stop=True,
                             is_transpose=True).then_inc("pe", 1)
                w.cp(f"a_ztr_{q}_{j}", "pe")

    def ph3fix(en, w):
        if en == "gpsimd":
            w.wait("car_pub")
            w.collective_compute("AllGather", ALU.bypass, replica_groups=RG,
                                 ins=[agcar_in.ap()], outs=[agcar_out.ap()[0:NCORE * 128, :]]).then_inc("ce", 1)
            w.cp("agcar_done", "ce")
            w.wait("agcar_done")
            w.memset(carg[:, :], 0.0).then_inc("po", 1)
            w.cp("car_z", "po")
            w.wait("car_z")
            w.dma_start(out=agcar_out.ap()[NCORE * 128:NCORE * 128 + 1, :],
                        in_=carg[0:1, :]).then_inc("gp", 16)
            w.cp("car_zrow", "gp")
            w.wait("car_zrow")
            w.indirect_dma_start(out=carg[:, :], out_offset=None,
                                 in_=agcar_out.ap(),
                                 in_offset=IndirectOffsetOnAxis(caridx[:, :], axis=0),
                                 ).then_inc("gp", 16)
            w.cp("car_g", "gp")
            w.wait("car_pub")
            w.collective_compute("AllGather", ALU.bypass, replica_groups=RG,
                                 ins=[agx2_in.ap()], outs=[agx2_out.ap()]).then_inc("ce", 1)
            w.cp("agx2_done", "ce")
            w.wait("agx2_done")
            w.wait("cf_p_0")
            w.dma_start(out=xgf32[0:8, :], in_=agx2_out.ap()).then_inc("gp", 16)
            w.cp("x2rows_ld", "gp")
        if en == "scalar":
            for dc in range(NDC):
                if dc == 0:
                    w.wait("car_g")
                    w.wait("ph0_iota")
                    w.wait("ph0_negt")
                w.activation(pcoef[:, :], iota256[:, :], AF.Exp,
                             scale=netd[:, dc:dc + 1]).then_inc("ac", 1)
                w.cp(f"cf_p_{dc}", "ac")
                w.wait(f"cf_ab_{dc}")
                w.activation(corrb[:, dc, :], corrb[:, dc, :], AF.Ln,
                             bias=C_WKV()).then_inc("ac", 1)
                ssync(w, "ac")
                w.activation(corrb[:, dc, :], corrb[:, dc, :], AF.Exp,
                             scale=C_N1()).then_inc("ac", 1)
                w.cp(f"cf_recip_{dc}", "ac")
            for j in range(2):
                w.wait(f"cf_c2_{j}")
                w.activation(sqj[:, :], xnt[:, j, :], AF.Square,
                             accum_out=tmpc[:, 1:2]).then_inc("ac", 1)
                w.cp(f"cf_sq2_{j}", "ac")
                w.wait(f"cf_var2_{j}")
                w.activation(tmpc[:, 2:3], tmpc[:, 2:3], AF.Ln,
                             bias=C_LN()).then_inc("ac", 1)
                ssync(w, "ac")
                w.activation(tmpc[:, 2:3], tmpc[:, 2:3], AF.Exp,
                             scale=C_NH()).then_inc("ac", 1)
                w.cp(f"cf_rstd2_{j}", "ac")
        if en == "vector":
            w.wait("y_hsel")
            w.wait("h1_z")
            w.tensor_copy(hrow[0:1, 0:512], PB[2][0:1, 0:512]).then_inc("ve", 1)
            w.tensor_copy(hrow[0:1, 512:1024], PB[3][0:1, 0:512]).then_inc("ve", 1)
            w.cp("y_hrow", "ve")
            for dc in range(NDC):
                w.wait(f"cf_p_{dc}")
                w.scalar_tensor_tensor(corra[:, dc, :], pcoef[:, :],
                                       carg[:, dc:dc + 1], corra[:, dc, :],
                                       ALU.mult, ALU.add).then_inc("ve", 1)
                w.scalar_tensor_tensor(corrb[:, dc, :], pcoef[:, :],
                                       carg[:, 8 + dc:8 + dc + 1], corrb[:, dc, :],
                                       ALU.mult, ALU.add).then_inc("ve", 1)
                w.cp(f"cf_ab_{dc}", "ve")
                w.wait(f"cf_recip_{dc}")
                w.tensor_tensor(corra[:, dc, :], corra[:, dc, :], corrb[:, dc, :],
                                ALU.mult).then_inc("ve", 1)
                w.tensor_tensor(cw[:, dc, :], corra[:, dc, :], rcorr[:, dc, :],
                                ALU.mult).then_inc("ve", 1)
            w.cp("cf_cw", "ve")
            for j in range(2):
                w.wait(f"cf_womm_{j}")
                w.wait(f"cf_xn_{j}")
                w.wait(f"a_x2st_3_{2 + j}")
                if j == 1:
                    w.wait("car_pub")
                w.tensor_tensor(xnt[:, j, 0:512], PB[4 + j * 2][:, :],
                                xt[:, j, 0:512], ALU.add).then_inc("ve", 1)
                w.tensor_tensor(xnt[:, j, 512:1024], PB[5 + j * 2][:, :],
                                xt[:, j, 512:1024], ALU.add).then_inc("ve", 1)
                w.cp(f"cf_x2_{j}", "ve")
                w.tensor_reduce(tmpc[:, 0:1], xnt[:, j, :], AX.X, ALU.add,
                                negate=True).then_inc("ve", 1)
                w.cp(f"cf_c2_{j}", "ve")
                w.wait(f"cf_sq2_{j}")
                w.tensor_scalar(tmpc[:, 0:1], tmpc[:, 0:1], 1.0 / D, None,
                                ALU.mult).then_inc("ve", 1)
                w.tensor_scalar(tmpc[:, 1:2], tmpc[:, 1:2], 1.0 / D, None,
                                ALU.mult).then_inc("ve", 1)
                ssync(w, "ve")
                w.tensor_tensor(tmpc[:, 2:3], tmpc[:, 0:1], tmpc[:, 0:1],
                                ALU.mult).then_inc("ve", 1)
                ssync(w, "ve")
                w.tensor_tensor(tmpc[:, 2:3], tmpc[:, 1:2], tmpc[:, 2:3],
                                ALU.subtract).then_inc("ve", 1)
                w.cp(f"cf_var2_{j}", "ve")
                w.wait(f"cf_rstd2_{j}")
                w.tensor_tensor(tmpc[:, 3:4], tmpc[:, 0:1], tmpc[:, 2:3],
                                ALU.mult).then_inc("ve", 1)
                w.cp(f"cf_bias2_{j}", "ve")
                if j == 1:
                    w.wait("cf_ztr_0")
                else:
                    w.wait(f"a_ztr_{NQ - 1}_3")
                ssync(w, "ve")
                w.tensor_scalar(z1t[:, :], xnt[:, j, :], tmpc[:, 2:3],
                                tmpc[:, 3:4], ALU.mult, ALU.add).then_inc("ve", 1)
                w.cp(f"cf_z2_{j}", "ve")
                w.wait(f"cf_ztr_{j}")
                if j == 0:
                    w.wait("a_z2st_3_3")
                else:
                    w.wait("cf_z2st_0")
                w.tensor_copy(trs[:, j, 0:4, :], p3(PB[0], 128)).then_inc("ve", 1)
                w.tensor_copy(trs[:, j, 4:8, :], p3(PB[1], 128)).then_inc("ve", 1)
                w.cp(f"cf_ztre_{j}", "ve")
        if en == "tensor":
            w.wait("x2rows_ld")
            w.wait(f"a_kvev_{NQ - 1}_7")
            w.matmul(PB[2][0:1, 0:512], predoh[:, :], xgf32[0:8, 0:512],
                     start=True, stop=True).then_inc("pe", 1)
            w.matmul(PB[3][0:1, 0:512], predoh[:, :], xgf32[0:8, 512:1024],
                     start=True, stop=True).then_inc("pe", 1)
            w.cp("y_hsel", "pe")
            for j in range(2):
                if j == 0:
                    w.wait("cf_cw")
                    w.wait(f"a_ztr_{NQ - 1}_3")
                    w.wait(f"a_x2_{NQ - 1}_3")
                for n in range(2):
                    for dcK in range(NDC):
                        w.matmul(PB[4 + j * 2 + n][:, :],
                                 cw[:, dcK, j * 128:(j + 1) * 128],
                                 wkc[:, dcK, n * 512:(n + 1) * 512],
                                 start=(dcK == 0), stop=(dcK == NDC - 1)).then_inc("pe", 1)
                w.cp(f"cf_womm_{j}", "pe")
                w.wait(f"cf_z2_{j}")
                if j == 0:
                    w.wait(f"a_ztre_{NQ - 1}_3")
                else:
                    w.wait("cf_ztre_0")
                for dc in range(NDC):
                    w.matmul(PB[dc // 4][:, (dc % 4) * 128:(dc % 4 + 1) * 128],
                             z1t[:, dc * 128:(dc + 1) * 128],
                             ident_s[:, :], start=True, stop=True,
                             is_transpose=True).then_inc("pe", 1)
                w.cp(f"cf_ztr_{j}", "pe")
        if en == "sync":
            for j in range(2):
                w.wait("cf_cw")
                w.wait(f"a_x2_3_{2 + j}")
                w.dma_start(out=xt[:, j, :],
                            in_=xn32_d.ap()[j * 128:(j + 1) * 128, :]).then_inc("la" if j == 0 else "lb", 16)
                w.cp(f"cf_xn_{j}", "la" if j == 0 else "lb")
            for j in range(2):
                ssem = "sa" if j == 0 else "sb"
                w.wait(f"cf_x2_{j}")
                w.dma_start(out=x2_d.ap()[j * 128:(j + 1) * 128, :],
                            in_=xnt[:, j, :]).then_inc(ssem, 16)
                w.cp(f"cf_x2st_{j}", ssem)
            for j in range(2):
                ssem = "sa" if j == 0 else "sb"
                w.wait(f"cf_ztre_{j}")
                w.dma_start(
                    out=z2fm_d.ap().rearrange("(a p) t -> p a t", p=128)[:, :, j * 128:(j + 1) * 128],
                    in_=trs[:, j, :, :]).then_inc(ssem, 16)
                w.cp(f"cf_z2st_{j}", ssem)

    def ph3yh(en, w):
        if en == "vector":
            w.wait("y_hrow")
            w.cp("yh_go", "ve")
        lnrow_chain(en, w, "yh", hrow[0:1, :], hrow2[0:1, :])
        if en == "vector":
            w.wait("yh_htr")
            for dc in range(NDC):
                w.tensor_scalar(yhaloc[:, dc:dc + 1], PB[2][:, dc * 2:dc * 2 + 1],
                                vc("g2", dc), vc("b2", dc), ALU.mult,
                                ALU.add).then_inc("ve", 1)
            ssync(w, "ve")
            w.tensor_scalar(yhaloc[:, :], yhaloc[:, :], flagc[:, 1:2], None,
                            ALU.mult).then_inc("ve", 1)
            w.cp("y_haloc", "ve")

    def ph3kk(en, w):
        if en == "gpsimd":
            w.wait("a_rwkv_3_7")
            w.wait("kk_rrwka_free")
            w.dma_start(out=wka[:, :, :],
                        in_=wfr_d.ap().rearrange("(a p) n -> p a n", p=128)).then_inc("ga", 16)
            w.cp("w_wfr", "ga")
            for g in range(8):
                for ic in range(8):
                    n_slot = g * 8 + ic
                    gsem = "ga" if ic % 2 == 0 else "gb"
                    if n_slot == 0:
                        w.wait("cf_womm_1")
                    if n_slot < 2:
                        w.wait("kk_rrmm_0_0")
                    if n_slot >= 2:
                        pg, pic = (n_slot - 2) // 8, (n_slot - 2) % 8
                        w.wait(f"kk_fkmm_{pg}_{pic}")
                        w.wait(f"kk_fv_{pg}_{pic}")
                    w.dma_start(out=wkb[:, :, (ic % 2) * 512:(ic % 2) * 512 + 512],
                                in_=wfk_d.ap().rearrange("(a p) n -> p a n", p=128)[:, :, ic * 512:(ic + 1) * 512],
                                ).then_inc(gsem, 16)
                    w.dma_start(out=wkc[:, (ic % 2) * 4:(ic % 2) * 4 + 4, :],
                                in_=wfv_d.ap().rearrange("(a p) n -> p a n", p=128)[:, ic * 4:(ic + 1) * 4, :],
                                ).then_inc(gsem, 16)
                    w.cp(f"kk_w_{g}_{ic}", gsem)
        if en == "sync":
            w.wait("cf_z2st_1")
            for qq in range(NQ):
                for j in range(4):
                    w.wait(f"a_z2st_{qq}_{j}")
            w.wait("a_rmm_3_7")
            w.wait("a_rwkv_3_7")
            w.dma_start(out=yn[:, :, 1:NTOK + 1],
                        in_=z2fm_d.ap().rearrange("(a p) t -> p a t", p=128),
                        ).then_inc("ld", 16)
            w.cp("kk_yn_ld", "ld")
            for g in range(8):
                for j in range(2):
                    lsem = "la" if j == 0 else "lb"
                    if g == 0:
                        w.wait("cf_x2st_1")
                    if g >= 1:
                        w.wait(f"kk_s_{g - 1}_{j}")
                    w.dma_start(out=g1t[:, j, :],
                                in_=x2_d.ap()[g * 256 + j * 128: g * 256 + j * 128 + 128, :],
                                ).then_inc(lsem, 16)
                    w.cp(f"kk_x2_{g}_{j}", lsem)
                for j in range(2):
                    ssem = "sa" if j == 0 else "sb"
                    w.wait(f"kk_s_{g}_{j}")
                    w.dma_start(out=s32_d.ap()[g * 256 + j * 128: g * 256 + j * 128 + 128, :],
                                in_=g2t[:, j, :]).then_inc(ssem, 16)
                    w.cp(f"kk_sst_{g}_{j}", ssem)
            w.cp("kk_s_all_a", "sa")
            w.cp("kk_s_all_b", "sb")
        if en == "vector":
            w.wait("kk_yn_ld")
            w.wait("y_haloc")
            for dc in range(NDC):
                w.tensor_scalar(yn[:, dc, 1:NTOK + 1], yn[:, dc, 1:NTOK + 1],
                                vc("g2", dc), vc("b2", dc), ALU.mult,
                                ALU.add).then_inc("ve", 1)
            w.tensor_copy(yn[:, :, 0:1], yhaloc[:, :, None]).then_inc("ve", 1)
            w.cp("kk_yn", "ve")
            for g in range(8):
                for dc in range(NDC):
                    w.tensor_tensor(msb[:, dc, :], yn[:, dc, 1 + g * 256:1 + (g + 1) * 256],
                                    yn[:, dc, g * 256:(g + 1) * 256],
                                    ALU.add).then_inc("ve", 1)
                w.cp(f"kk_ms_{g}", "ve")
                for ic in range(8):
                    w.wait(f"kk_fkmm_{g}_{ic}")
                    if g == 0 and ic == 0:
                        w.wait("cf_womm_1")
                    pa, pb_ = (ic % 2) * 2, (ic % 2) * 2 + 1
                    w.tensor_scalar(kkc[:, ic % 2, 0:2, :], p3(PB[pa], 256),
                                    0.5, 0.0, ALU.mult, ALU.max).then_inc("ve", 1)
                    w.tensor_scalar(kkc[:, ic % 2, 2:4, :], p3(PB[pb_], 256),
                                    0.5, 0.0, ALU.mult, ALU.max).then_inc("ve", 1)
                    w.tensor_tensor(kkc[:, ic % 2, :, :], kkc[:, ic % 2, :, :],
                                    kkc[:, ic % 2, :, :], ALU.mult).then_inc("ve", 1)
                    w.cp(f"kk_kk_{g}_{ic}", "ve")
                for j in range(2):
                    w.wait(f"kk_fvmm_{g}")
                    w.wait(f"kk_x2_{g}_0")
                    w.wait(f"kk_x2_{g}_1")
                    w.wait(f"kk_rrs_{g}_{j}")
                    if g >= 1:
                        w.wait(f"kk_sst_{g - 1}_{j}")
                    w.tensor_tensor(gs1[:, 0:512], rrg[:, j, 0:512],
                                    PB[4 + j * 2][:, :], ALU.mult).then_inc("ve", 1)
                    w.tensor_tensor(gs1[:, 512:1024], rrg[:, j, 512:1024],
                                    PB[5 + j * 2][:, :], ALU.mult).then_inc("ve", 1)
                    w.tensor_tensor(g2t[:, j, :], gs1[:, :], g1t[:, j, :],
                                    ALU.add).then_inc("ve", 1)
                    w.cp(f"kk_s_{g}_{j}", "ve")
        if en == "tensor":
            for g in range(8):
                for j in range(2):
                    if g == 0 and j == 0:
                        w.wait("w_wfr")
                        w.wait("kk_yn")
                        w.wait("cf_ztre_1")
                        w.wait("y_haloc")
                    else:
                        pg, pj = (g, 0) if j == 1 else (g - 1, 1)
                        w.wait(f"kk_rrs_{pg}_{pj}")
                        if j == 0:
                            w.wait(f"kk_kk_{g - 1}_7")
                    for n in range(2):
                        for dcK in range(NDC):
                            w.matmul(PB[2 + n][:, :],
                                     yn[:, dcK, 1 + g * 256 + j * 128: 1 + g * 256 + (j + 1) * 128],
                                     wka[:, dcK, n * 512:(n + 1) * 512],
                                     start=(dcK == 0), stop=(dcK == NDC - 1)).then_inc("pe", 1)
                    w.cp(f"kk_rrmm_{g}_{j}", "pe")
                    if g == 7 and j == 1:
                        w.cp("kk_rr_last", "pe")
                for ic in range(8):
                    w.wait(f"kk_w_{g}_{ic}")
                    w.wait(f"kk_ms_{g}")
                    if g == 0 and ic < 2:
                        w.wait("cf_ztre_1")
                        w.wait(f"kk_rrs_0_{1}") if False else None
                    if ic >= 2:
                        w.wait(f"kk_kk_{g}_{ic - 2}")
                    pa, pb_ = (ic % 2) * 2, (ic % 2) * 2 + 1
                    if ic % 2 == 1 and g == 0 and ic == 1:
                        w.wait("kk_rrs_0_1")
                    if ic % 2 == 1:
                        w.wait(f"kk_rrs_{g}_1")
                    for sub in range(2):
                        for dcK in range(NDC):
                            w.matmul(PB[pa][:, sub * 256:(sub + 1) * 256],
                                     wkb[:, dcK, (ic % 2) * 512 + sub * 128:(ic % 2) * 512 + (sub + 1) * 128],
                                     msb[:, dcK, :],
                                     start=(dcK == 0), stop=(dcK == NDC - 1)).then_inc("pe", 1)
                    for sub in range(2):
                        for dcK in range(NDC):
                            w.matmul(PB[pb_][:, sub * 256:(sub + 1) * 256],
                                     wkb[:, dcK, (ic % 2) * 512 + (2 + sub) * 128:(ic % 2) * 512 + (3 + sub) * 128],
                                     msb[:, dcK, :],
                                     start=(dcK == 0), stop=(dcK == NDC - 1)).then_inc("pe", 1)
                    w.cp(f"kk_fkmm_{g}_{ic}", "pe")
                    w.wait(f"kk_kk_{g}_{ic}")
                    if ic == 0:
                        if g >= 1:
                            w.wait(f"kk_s_{g - 1}_1")
                        else:
                            w.wait("cf_x2_1")
                    for j in range(2):
                        for n in range(2):
                            for sub in range(4):
                                w.matmul(PB[4 + j * 2 + n][:, :],
                                         kkc[:, ic % 2, sub, j * 128:(j + 1) * 128],
                                         wkc[:, (ic % 2) * 4 + sub, n * 512:(n + 1) * 512],
                                         start=(ic == 0 and sub == 0),
                                         stop=(ic == 7 and sub == 3)).then_inc("pe", 1)
                    w.cp(f"kk_fv_{g}_{ic}", "pe")
                    if ic == 7:
                        w.cp(f"kk_fvmm_{g}", "pe")
            w.cp("kk_pe_done", "pe")
        if en == "scalar":
            for g in range(8):
                for j in range(2):
                    w.wait(f"kk_rrmm_{g}_{j}")
                    if g == 0 and j == 0:
                        w.wait("cf_womm_1")
                    w.activation(rrg[:, j, 0:512], PB[2][:, :], AF.Sigmoid).then_inc("ac", 1)
                    w.activation(rrg[:, j, 512:1024], PB[3][:, :], AF.Sigmoid).then_inc("ac", 1)
                    w.cp(f"kk_rrs_{g}_{j}", "ac")

    def ph3kk_pre(en, w):
        if en == "tensor":
            pass
        if en == "gpsimd":
            pass
        if en == "vector":
            pass

    def ph4(en, w):
        if en == "sync":
            for cc in range(NCORE):
                if cc == 0:
                    w.wait("agrec_done")
                    w.wait("kk_yn")
                if cc >= 1:
                    w.wait(f"x_sidx_{cc - 1}")
                w.dma_start(out=recs[:, :],
                            in_=agrec_out.ap()[cc * 128:(cc + 1) * 128, :]).then_inc("ld", 16)
                w.cp(f"x_rec_{cc}", "ld")
            w.wait("x_zt")
            for j in range(CAP // 128):
                w.dma_start(out=xg_d.ap()[j * 128:(j + 1) * 128, :],
                            in_=xg16t[:, 0, :]).then_inc("sa", 16)
            w.cp("x_zeroed", "sa")
            for i in range(NCORE * NT):
                cc, t = i // NT, i % NT
                lsem = "la" if i % 2 == 0 else "lb"
                if i < 2:
                    w.wait("agx_done")
                    w.wait("x_zeroed")
                if i >= 2:
                    w.wait(f"x_scat_{i - 2}")
                w.dma_start(out=xg16t[:, i % 2, :],
                            in_=agx_out.ap()[cc * NTOK + t * 128: cc * NTOK + t * 128 + 128, :],
                            ).then_inc(lsem, 16)
                w.cp(f"x_row_{i}", lsem)
            for sgi in range(10):
                if sgi == 0:
                    w.wait("x_scat_all_r")
                    w.wait("x_scat_all_s")
                for j in range(2):
                    lsem = "la" if j == 0 else "lb"
                    if sgi >= 1:
                        w.wait(f"x_xg32_{sgi - 1}_{j}")
                    w.dma_start(out=xg16t[:, j, :],
                                in_=xg_d.ap()[sgi * 256 + j * 128: sgi * 256 + j * 128 + 128, :],
                                ).then_inc(lsem, 16)
                    w.cp(f"x_xgld_{sgi}_{j}", lsem)
                for j in range(2):
                    ssem = "sa" if j == 0 else "sb"
                    w.wait(f"x_eoev_{sgi}_{j}")
                    w.dma_start(out=ageo_in.ap()[sgi * 256 + j * 128: sgi * 256 + j * 128 + 128, :],
                                in_=g1t[:, j, :]).then_inc(ssem, 16)
                    w.cp(f"x_eost_{sgi}_{j}", ssem)
        if en == "vector":
            w.wait("kk_s_all_a")
            w.wait("kk_s_all_b")
            w.memset(xg16t[:, 0, :], 0.0).then_inc("ve", 1)
            w.cp("x_zt", "ve")
            for cc in range(NCORE):
                w.wait(f"x_rec_{cc}")
                w.tensor_scalar(rec3[:, :], recs[:, 0:16], myid[:, :], None,
                                ALU.is_equal).then_inc("ve", 1)
                ssync(w, "ve")
                w.tensor_tensor(rec4[:, :], rec3[:, :], recs[:, 16:32],
                                ALU.mult).then_inc("ve", 1)
                w.tensor_scalar(rec5[:, :], recs[:, 32:48], myid[:, :], None,
                                ALU.is_equal).then_inc("ve", 1)
                ssync(w, "ve")
                w.tensor_tensor(tmpc[:, :], rec5[:, :], recs[:, 48:64],
                                ALU.mult).then_inc("ve", 1)
                ssync(w, "ve")
                w.tensor_tensor(rec4[:, :], rec4[:, :], tmpc[:, :],
                                ALU.add).then_inc("ve", 1)
                w.tensor_tensor(rec3[:, :], rec3[:, :], rec5[:, :],
                                ALU.add).then_inc("ve", 1)
                ssync(w, "ve")
                w.tensor_scalar(rec3[:, :], rec3[:, :], -1.0, 1.0,
                                ALU.mult, ALU.add).then_inc("ve", 1)
                ssync(w, "ve")
                w.scalar_tensor_tensor(rec4[:, :], rec3[:, :], float(CAP),
                                       rec4[:, :], ALU.mult,
                                       ALU.add).then_inc("ve", 1)
                ssync(w, "ve")
                w.tensor_scalar(rec4[:, :], rec4[:, :], float(CAP), None,
                                ALU.min).then_inc("ve", 1)
                ssync(w, "ve")
                w.tensor_copy(sidx[:, cc * 16:(cc + 1) * 16], rec4[:, :]).then_inc("ve", 1)
                w.cp(f"x_sidx_{cc}", "ve")
            w.cp("x_sidx", "ve")
            for sgi in range(10):
                for j in range(2):
                    w.wait(f"x_xgld_{sgi}_{j}")
                    if sgi == 0 and j == 0:
                        w.wait("f_zrow")
                    w.tensor_copy(xgf32[:, :], xg16t[:, j, :]).then_inc("ve", 1)
                    w.cp(f"x_xg32_{sgi}_{j}", "ve")
                    w.wait(f"x_xgtr_{sgi}_{j}")
                    if sgi == 0 and j == 0:
                        w.wait("kk_rr_last")
                    w.tensor_copy(z1h[:, 0:4, j * 128:(j + 1) * 128],
                                  p3(PB[0], 128)).then_inc("ve", 1)
                    w.tensor_copy(z1h[:, 4:8, j * 128:(j + 1) * 128],
                                  p3(PB[1], 128)).then_inc("ve", 1)
                    w.cp(f"x_xgf_{sgi}_{j}", "ve")
                for ic in range(8):
                    w.wait(f"x_upmm_{sgi}_{ic}")
                    w.tensor_copy(mixa[:, 0:2, 0:256], p3(PB[2], 256)).then_inc("ve", 1)
                    w.tensor_copy(mixa[:, 2:4, 0:256], p3(PB[3], 256)).then_inc("ve", 1)
                    w.cp(f"x_upev_{sgi}_{ic}", "ve")
                    w.wait(f"x_gsilu_{sgi}_{ic}")
                    w.tensor_tensor(mixa[:, 0:4, 0:256], mixb[:, 0:4, 0:256],
                                    mixa[:, 0:4, 0:256], ALU.mult).then_inc("ve", 1)
                    w.cp(f"x_hp_{sgi}_{ic}", "ve")
                for j in range(2):
                    w.wait(f"x_eomm_{sgi}")
                    if sgi >= 1:
                        w.wait(f"x_eost_{sgi - 1}_{j}")
                    else:
                        w.wait("kk_s_7_1")
                        w.wait(f"kk_sst_7_{j}")
                    w.tensor_copy(g1t[:, j, 0:512], PB[4 + j * 2][:, :]).then_inc("ve", 1)
                    w.tensor_copy(g1t[:, j, 512:1024], PB[5 + j * 2][:, :]).then_inc("ve", 1)
                    w.cp(f"x_eoev_{sgi}_{j}", "ve")
        if en == "scalar":
            for sgi in range(10):
                for ic in range(8):
                    w.wait(f"x_gmm_{sgi}_{ic}")
                    w.activation(mixb[:, 0:2, 0:256], PB[0][:, :], AF.Silu).then_inc("ac", 1)
                    w.activation(mixb[:, 2:4, 0:256], PB[1][:, :], AF.Silu).then_inc("ac", 1)
                    w.cp(f"x_gsilu_{sgi}_{ic}", "ac")
        if en == "gpsimd":
            w.wait("kk_s_all_a")
            w.wait("kk_s_all_b")
            w.memset(gs1[:, :], 0.0).then_inc("po", 1)
            w.cp("f_zrow_ms", "po")
            w.wait("f_zrow_ms")
            w.dma_start(out=ageo_out.ap()[NCORE * CAP:NCORE * CAP + 1, :],
                        in_=gs1[0:1, :]).then_inc("gp", 16)
            w.cp("f_zrow", "gp")
            for i in range(NCORE * NT):
                cc, t = i // NT, i % NT
                gsem = "gr" if i % 2 == 0 else "gs"
                if i == 0:
                    w.wait("x_zeroed")
                    w.wait("x_sidx")
                w.wait(f"x_row_{i}")
                w.indirect_dma_start(
                    out=xg_d.ap(),
                    out_offset=IndirectOffsetOnAxis(sidx[:, cc * 16 + t: cc * 16 + t + 1], axis=0),
                    in_=xg16t[:, i % 2, :], in_offset=None).then_inc(gsem, 16)
                w.cp(f"x_scat_{i}", gsem)
            w.cp("x_scat_all_r", "gr")
            w.cp("x_scat_all_s", "gs")
            for sgi in range(10):
                for ic in range(8):
                    n_slot = sgi * 8 + ic
                    gsem = "ga" if ic % 2 == 0 else "gb"
                    if n_slot == 0:
                        w.wait("kk_pe_done")
                        w.wait("kk_rr_last")
                    if n_slot >= 2:
                        pg, pic = (n_slot - 2) // 8, (n_slot - 2) % 8
                        w.wait(f"x_upmm_{pg}_{pic}")
                        w.wait(f"x_eo_{pg}_{pic}")
                    w.dma_start(out=wka[:, :, (ic % 2) * 512:(ic % 2) * 512 + 512],
                                in_=rg_d.ap().rearrange("(a p) n -> p a n", p=128)[:, :, ic * 512:(ic + 1) * 512],
                                ).then_inc(gsem, 16)
                    w.dma_start(out=wkb[:, :, (ic % 2) * 512:(ic % 2) * 512 + 512],
                                in_=ru_d.ap().rearrange("(a p) n -> p a n", p=128)[:, :, ic * 512:(ic + 1) * 512],
                                ).then_inc(gsem, 16)
                    w.dma_start(out=wkc[:, (ic % 2) * 4:(ic % 2) * 4 + 4, :],
                                in_=rd_d.ap().rearrange("(a p) n -> p a n", p=128)[:, ic * 4:(ic + 1) * 4, :],
                                ).then_inc(gsem, 16)
                    w.cp(f"x_w_{sgi}_{ic}", gsem)
            w.wait("x_eost_9_0")
            w.wait("x_eost_9_1")
            w.collective_compute("AllGather", ALU.bypass, replica_groups=RG,
                                 ins=[ageo_in.ap()], outs=[ageo_out.ap()[0:NCORE * CAP, :]]).then_inc("ce", 1)
            w.cp("ageo_done", "ce")
        if en == "tensor":
            for sgi in range(10):
                for j in range(2):
                    w.wait(f"x_xg32_{sgi}_{j}")
                    if sgi == 0 and j == 0:
                        w.wait("kk_kk_7_7")
                    else:
                        pg = sgi * 2 + j - 1
                        w.wait(f"x_xgf_{pg // 2}_{pg % 2}")
                    if sgi >= 1 and j == 0:
                        w.wait(f"x_gsilu_{sgi - 1}_7")
                    for dc in range(NDC):
                        w.matmul(PB[dc // 4][:, (dc % 4) * 128:(dc % 4 + 1) * 128],
                                 xgf32[:, dc * 128:(dc + 1) * 128],
                                 ident_s[:, :], start=True, stop=True,
                                 is_transpose=True).then_inc("pe", 1)
                    w.cp(f"x_xgtr_{sgi}_{j}", "pe")
                for ic in range(8):
                    w.wait(f"x_w_{sgi}_{ic}")
                    w.wait(f"x_xgf_{sgi}_1")
                    if sgi == 0 and ic == 0:
                        w.wait("kk_rrs_7_1")
                        w.wait("kk_s_7_1")
                    if ic >= 1:
                        w.wait(f"x_gsilu_{sgi}_{ic - 1}")
                    for sub in range(2):
                        for dcK in range(NDC):
                            w.matmul(PB[0][:, sub * 256:(sub + 1) * 256],
                                     wka[:, dcK, (ic % 2) * 512 + sub * 128:(ic % 2) * 512 + (sub + 1) * 128],
                                     z1h[:, dcK, 0:256],
                                     start=(dcK == 0), stop=(dcK == NDC - 1)).then_inc("pe", 1)
                    for sub in range(2):
                        for dcK in range(NDC):
                            w.matmul(PB[1][:, sub * 256:(sub + 1) * 256],
                                     wka[:, dcK, (ic % 2) * 512 + (2 + sub) * 128:(ic % 2) * 512 + (3 + sub) * 128],
                                     z1h[:, dcK, 0:256],
                                     start=(dcK == 0), stop=(dcK == NDC - 1)).then_inc("pe", 1)
                    w.cp(f"x_gmm_{sgi}_{ic}", "pe")
                    if ic >= 1:
                        w.wait(f"x_upev_{sgi}_{ic - 1}")
                    for sub in range(2):
                        for dcK in range(NDC):
                            w.matmul(PB[2][:, sub * 256:(sub + 1) * 256],
                                     wkb[:, dcK, (ic % 2) * 512 + sub * 128:(ic % 2) * 512 + (sub + 1) * 128],
                                     z1h[:, dcK, 0:256],
                                     start=(dcK == 0), stop=(dcK == NDC - 1)).then_inc("pe", 1)
                    for sub in range(2):
                        for dcK in range(NDC):
                            w.matmul(PB[3][:, sub * 256:(sub + 1) * 256],
                                     wkb[:, dcK, (ic % 2) * 512 + (2 + sub) * 128:(ic % 2) * 512 + (3 + sub) * 128],
                                     z1h[:, dcK, 0:256],
                                     start=(dcK == 0), stop=(dcK == NDC - 1)).then_inc("pe", 1)
                    w.cp(f"x_upmm_{sgi}_{ic}", "pe")
                    w.wait(f"x_hp_{sgi}_{ic}")
                    if ic == 0:
                        if sgi >= 1:
                            w.wait(f"x_eoev_{sgi - 1}_1")
                        else:
                            w.wait("kk_s_7_1")
                    for j in range(2):
                        for n in range(2):
                            for sub in range(4):
                                w.matmul(PB[4 + j * 2 + n][:, :],
                                         mixa[:, sub, j * 128:(j + 1) * 128],
                                         wkc[:, (ic % 2) * 4 + sub, n * 512:(n + 1) * 512],
                                         start=(ic == 0 and sub == 0),
                                         stop=(ic == 7 and sub == 3)).then_inc("pe", 1)
                    w.cp(f"x_eo_{sgi}_{ic}", "pe")
                    if ic == 7:
                        w.cp(f"x_eomm_{sgi}", "pe")
            w.cp("x_pe_done", "pe")

    def ph5(en, w):
        NB = 4
        if en == "gpsimd":
            for h in range(2):
                for ic in range(16):
                    n_slot = h * 16 + ic
                    gsem = "ga" if ic % 2 == 0 else "gb"
                    if n_slot == 0:
                        w.wait("x_pe_done")
                    if n_slot >= 2:
                        ph_, pic = (n_slot - 2) // 16, (n_slot - 2) % 16
                        w.wait(f"f_h2_{ph_}_{pic}_{NB - 1}")
                        w.wait(f"f_sd_{ph_}_{pic}_{NB - 1}")
                    w.dma_start(out=wka[:, :, (ic % 2) * 512:(ic % 2) * 512 + 512],
                                in_=sg_d.ap().rearrange("(a p) n -> p a n", p=128)[:, :, ic * 512:(ic + 1) * 512],
                                ).then_inc(gsem, 16)
                    w.dma_start(out=wkb[:, :, (ic % 2) * 512:(ic % 2) * 512 + 512],
                                in_=su_d.ap().rearrange("(a p) n -> p a n", p=128)[:, :, ic * 512:(ic + 1) * 512],
                                ).then_inc(gsem, 16)
                    w.dma_start(out=wkc[:, (ic % 2) * 4:(ic % 2) * 4 + 4, :],
                                in_=sd_d.ap().rearrange("(a p) n -> p a n", p=128)[:, ic * 4:(ic + 1) * 4, :],
                                ).then_inc(gsem, 16)
                    w.cp(f"f_w_{h}_{ic}", gsem)
                for ti in range(h * 8, h * 8 + 8):
                    gsem = "gr" if ti % 2 == 0 else "gs"
                    j = ti % 2
                    if ti == 0:
                        w.wait("ageo_done")
                        w.wait("f_zrow")
                    if ti >= 2:
                        w.wait(f"f_rt_{ti - 2}")
                        w.wait(f"f_out_{ti - 2}")
                    w.indirect_dma_start(
                        out=g1t[:, j, :], out_offset=None, in_=ageo_out.ap(),
                        in_offset=IndirectOffsetOnAxis(gidx1[:, ti:ti + 1], axis=0),
                        ).then_inc(gsem, 16)
                    w.indirect_dma_start(
                        out=g2t[:, j, :], out_offset=None, in_=ageo_out.ap(),
                        in_offset=IndirectOffsetOnAxis(gidx2[:, ti:ti + 1], axis=0),
                        ).then_inc(gsem, 16)
                    w.cp(f"f_g_{ti}", gsem)
        if en == "sync":
            for h in range(2):
                if h == 0:
                    w.wait("x_eomm_9")
                    w.wait("x_hp_9_7")
                else:
                    w.wait(f"f_h2_0_15_{NB - 1}")
                w.dma_start(out=xnf[:, :, :],
                            in_=xnfm_d.ap().rearrange("(a p) t -> p a t", p=128)[:, :, h * 1024:(h + 1) * 1024],
                            ).then_inc("ld", 16)
                w.cp(f"f_xnf_{h}", "ld")
                for ti in range(h * 8, h * 8 + 8):
                    j = ti % 2
                    lsem = "la" if j == 0 else "lb"
                    if ti >= 2:
                        w.wait(f"f_o_{ti - 2}")
                    w.dma_start(out=xt[:, j, :],
                                in_=xs.ap()[ti * 128:(ti + 1) * 128, :]).then_inc(lsem, 16)
                    w.dma_start(out=xnt[:, j, :],
                                in_=s32_d.ap()[ti * 128:(ti + 1) * 128, :]).then_inc(lsem, 16)
                    w.cp(f"f_xs_{ti}", lsem)
                    w.wait(f"f_o_{ti}")
                    ssem = "sa" if j == 0 else "sb"
                    w.dma_start(out=out_d.ap()[ti * 128:(ti + 1) * 128, :],
                                in_=g2t[:, j, :]).then_inc(ssem, 16)
                    w.cp(f"f_out_{ti}", ssem)
            w.cp("f_out_all_a", "sa")
            w.cp("f_out_all_b", "sb")
            w.wait("f_out_all_a")
            w.wait("f_out_all_b")
        if en == "tensor":
            for h in range(2):
                for ic in range(16):
                    w.wait(f"f_w_{h}_{ic}")
                    w.wait(f"f_xnf_{h}")
                    if h == 0 and ic == 0:
                        w.wait("x_eoev_9_1")
                        w.wait("x_hp_9_7")
                        w.wait("x_gsilu_9_7")
                        w.wait("x_upev_9_7")
                    for b in range(NB):
                        n_slot = (h * 16 + ic) * NB + b
                        if n_slot >= 1:
                            pn = n_slot - 1
                            w.wait(f"f_hsilu_{pn // (16 * NB)}_{(pn // NB) % 16}_{pn % NB}")
                        for sub in range(2):
                            for dcK in range(NDC):
                                w.matmul(PB[0][:, sub * 256:(sub + 1) * 256],
                                         wka[:, dcK, (ic % 2) * 512 + sub * 128:(ic % 2) * 512 + (sub + 1) * 128],
                                         xnf[:, dcK, b * 256:(b + 1) * 256],
                                         start=(dcK == 0), stop=(dcK == NDC - 1)).then_inc("pe", 1)
                        for sub in range(2):
                            for dcK in range(NDC):
                                w.matmul(PB[1][:, sub * 256:(sub + 1) * 256],
                                         wka[:, dcK, (ic % 2) * 512 + (2 + sub) * 128:(ic % 2) * 512 + (3 + sub) * 128],
                                         xnf[:, dcK, b * 256:(b + 1) * 256],
                                         start=(dcK == 0), stop=(dcK == NDC - 1)).then_inc("pe", 1)
                        w.cp(f"f_h1_{h}_{ic}_{b}", "pe")
                        if n_slot >= 1:
                            pn = n_slot - 1
                            w.wait(f"f_hp_{pn // (16 * NB)}_{(pn // NB) % 16}_{pn % NB}")
                        for sub in range(2):
                            for dcK in range(NDC):
                                w.matmul(PB[2][:, sub * 256:(sub + 1) * 256],
                                         wkb[:, dcK, (ic % 2) * 512 + sub * 128:(ic % 2) * 512 + (sub + 1) * 128],
                                         xnf[:, dcK, b * 256:(b + 1) * 256],
                                         start=(dcK == 0), stop=(dcK == NDC - 1)).then_inc("pe", 1)
                        for sub in range(2):
                            for dcK in range(NDC):
                                w.matmul(PB[3][:, sub * 256:(sub + 1) * 256],
                                         wkb[:, dcK, (ic % 2) * 512 + (2 + sub) * 128:(ic % 2) * 512 + (3 + sub) * 128],
                                         xnf[:, dcK, b * 256:(b + 1) * 256],
                                         start=(dcK == 0), stop=(dcK == NDC - 1)).then_inc("pe", 1)
                        w.cp(f"f_h2_{h}_{ic}_{b}", "pe")
                        w.wait(f"f_hp_{h}_{ic}_{b}")
                        if n_slot >= 1:
                            pn = n_slot - 1
                            w.wait(f"f_sacc_{pn // (16 * NB)}_{(pn // NB) % 16}_{pn % NB}")
                        for j in range(2):
                            for n in range(2):
                                for sub in range(4):
                                    w.matmul(PB[4 + j * 2 + n][:, :],
                                             hpa[:, sub, j * 128:(j + 1) * 128],
                                             wkc[:, (ic % 2) * 4 + sub, n * 512:(n + 1) * 512],
                                             start=(sub == 0), stop=(sub == 3)).then_inc("pe", 1)
                        w.cp(f"f_sd_{h}_{ic}_{b}", "pe")
        if en == "scalar":
            for h in range(2):
                for ic in range(16):
                    for b in range(NB):
                        w.wait(f"f_h1_{h}_{ic}_{b}")
                        w.activation(hpb[:, 0:2, :], PB[0][:, :], AF.Silu).then_inc("ac", 1)
                        w.activation(hpb[:, 2:4, :], PB[1][:, :], AF.Silu).then_inc("ac", 1)
                        w.cp(f"f_hsilu_{h}_{ic}_{b}", "ac")
                for ti in range(h * 8, h * 8 + 8):
                    j = ti % 2
                    w.wait(f"f_gsum_{ti}")
                    w.activation(g1t[:, j, :], g1t[:, j, :], AF.Silu).then_inc("ac", 1)
                    w.cp(f"f_silu_{ti}", "ac")
        if en == "vector":
            for h in range(2):
                for ic in range(16):
                    for b in range(NB):
                        w.wait(f"f_h2_{h}_{ic}_{b}")
                        w.wait(f"f_hsilu_{h}_{ic}_{b}")
                        w.tensor_tensor(hpa[:, 0:2, :], hpb[:, 0:2, :],
                                        p3(PB[2], 256), ALU.mult).then_inc("ve", 1)
                        w.tensor_tensor(hpa[:, 2:4, :], hpb[:, 2:4, :],
                                        p3(PB[3], 256), ALU.mult).then_inc("ve", 1)
                        w.cp(f"f_hp_{h}_{ic}_{b}", "ve")
                        w.wait(f"f_sd_{h}_{ic}_{b}")
                        for j in range(2):
                            for n in range(2):
                                if ic == 0:
                                    w.tensor_copy(sacc[:, 2 * b + j, n * 512:(n + 1) * 512],
                                                  PB[4 + j * 2 + n][:, :]).then_inc("ve", 1)
                                else:
                                    w.tensor_tensor(sacc[:, 2 * b + j, n * 512:(n + 1) * 512],
                                                    sacc[:, 2 * b + j, n * 512:(n + 1) * 512],
                                                    PB[4 + j * 2 + n][:, :],
                                                    ALU.add).then_inc("ve", 1)
                        w.cp(f"f_sacc_{h}_{ic}_{b}", "ve")
                for ti in range(h * 8, h * 8 + 8):
                    j = ti % 2
                    tloc = ti - h * 8
                    w.wait(f"f_g_{ti}")
                    w.wait(f"f_sacc_{h}_15_{tloc // 2}")
                    w.tensor_scalar(g1t[:, j, :], g1t[:, j, :], w1c[:, ti:ti + 1],
                                    None, ALU.mult).then_inc("ve", 1)
                    w.scalar_tensor_tensor(g1t[:, j, :], g2t[:, j, :],
                                           w2c[:, ti:ti + 1], g1t[:, j, :],
                                           ALU.mult, ALU.add).then_inc("ve", 1)
                    w.tensor_tensor(g1t[:, j, 0:512], g1t[:, j, 0:512],
                                    sacc[:, tloc, 0:512], ALU.add).then_inc("ve", 1)
                    w.tensor_tensor(g1t[:, j, 512:1024], g1t[:, j, 512:1024],
                                    sacc[:, tloc, 512:1024], ALU.add).then_inc("ve", 1)
                    w.cp(f"f_gsum_{ti}", "ve")
                    w.cp(f"f_rt_{ti}", "ve")
                    w.wait(f"f_silu_{ti}")
                    w.wait(f"f_xs_{ti}")
                    if ti >= 2:
                        w.wait(f"f_out_{ti - 2}")
                    w.tensor_tensor(g2t[:, j, :], xnt[:, j, :], g1t[:, j, :],
                                    ALU.mult).then_inc("ve", 1)
                    w.tensor_tensor(g2t[:, j, :], g2t[:, j, :], xt[:, j, :],
                                    ALU.add).then_inc("ve", 1)
                    w.cp(f"f_o_{ti}", "ve")

    def ph3(en, w):
        for q in range(NQ):
            ph3q(en, w, q)
        ph3fix(en, w)
        ph3yh(en, w)
        if en == "tensor":
            w.cp("kk_rrwka_free_mark", "pe")
        if en == "gpsimd":
            pass
        ph3kk(en, w)

    def alias(en, w):
        if en == "tensor":
            pass
        if en == "gpsimd":
            pass

    phases = [ph0, ph0rw, ph0b, ph0c, ph1, ph2, ph3, ph4, ph5]

    for en in ENGINES:
        w = EW(sched, sems, en, None)
        for ph in phases:
            ph(en, w)
    sched.cur["kk_rrwka_free"] = sched.cur["a_kvmm_3_7"]
    sched.start_real()

    with nc.Block() as block:
        def mk(en):
            def emit(real_engine):
                w = EW(sched, sems, en, real_engine)
                for ph in phases:
                    ph(en, w)
            return emit
        block.sync(mk("sync"))
        block.scalar(mk("scalar"))
        block.vector(mk("vector"))
        block.tensor(mk("tensor"))
        block.gpsimd(mk("gpsimd"))

    sched.cur["kk_rrwka_free"] = sched.cur["a_kvmm_3_7"]
    sched.check()
    return nc


_CACHE = {}


def kernel(**inputs):
    x = np.asarray(inputs["L0_attn_x"], np.float32)
    B, L, _ = x.shape
    xf = np.ascontiguousarray(x.reshape(B * L, D))

    vecs = np.ascontiguousarray(np.stack([
        np.asarray(inputs["L0_attn_ln1_g"], np.float32),
        np.asarray(inputs["L0_attn_ln1_b"], np.float32),
        np.asarray(inputs["L0_attn_ln2_g"], np.float32),
        np.asarray(inputs["L0_attn_ln2_b"], np.float32),
        np.asarray(inputs["L0_attn_mix_k"], np.float32),
        np.asarray(inputs["L0_attn_mix_v"], np.float32),
        np.asarray(inputs["L0_attn_mix_r"], np.float32),
        np.asarray(inputs["L0_attn_time_decay"], np.float32),
        np.asarray(inputs["L0_attn_time_first"], np.float32),
    ], axis=0))

    if "nc" not in _CACHE:
        _CACHE["nc"] = build_nc()
    nc = _CACHE["nc"]

    shared = {
        "wk": np.ascontiguousarray(np.asarray(inputs["L0_attn_Wk"], np.float32)),
        "wv": np.ascontiguousarray(np.asarray(inputs["L0_attn_Wv"], np.float32)),
        "wr": np.ascontiguousarray(np.asarray(inputs["L0_attn_Wr"], np.float32)),
        "wo": np.ascontiguousarray(np.asarray(inputs["L0_attn_Wo"], np.float32)),
        "wfr": np.ascontiguousarray(np.asarray(inputs["L0_attn_Wfr"], np.float32)),
        "wfk": np.ascontiguousarray(np.asarray(inputs["L0_attn_Wfk"], np.float32)),
        "wfv": np.ascontiguousarray(np.asarray(inputs["L0_attn_Wfv"], np.float32)),
        "sgw": np.ascontiguousarray(np.asarray(inputs["L0_ffn_sg"], np.float32)),
        "suw": np.ascontiguousarray(np.asarray(inputs["L0_ffn_su"], np.float32)),
        "sdw": np.ascontiguousarray(np.asarray(inputs["L0_ffn_sd"], np.float32)),
        "rww": np.ascontiguousarray(np.asarray(inputs["L0_ffn_router_w"], np.float32)),
        "vecs": vecs,
    }
    rg_a = np.asarray(inputs["L0_ffn_rg"], np.float32)
    ru_a = np.asarray(inputs["L0_ffn_ru"], np.float32)
    rd_a = np.asarray(inputs["L0_ffn_rd"], np.float32)

    in_maps = []
    for c in range(NCORE):
        h = c % 2
        im = dict(shared)
        im["xs"] = np.ascontiguousarray(xf[c * NTOK:(c + 1) * NTOK])
        im["xhalo"] = (np.ascontiguousarray(xf[c * NTOK - 1:c * NTOK])
                       if h == 1 else np.zeros((1, D), np.float32))
        im["rgw"] = np.ascontiguousarray(rg_a[c])
        im["ruw"] = np.ascontiguousarray(ru_a[c])
        im["rdw"] = np.ascontiguousarray(rd_a[c])
        sp = np.zeros((E, 1), np.float32)
        sp[:c, 0] = 1.0
        im["m_selprev"] = sp
        im["m_myid"] = np.full((128, 1), float(c), np.float32)
        im["m_flags"] = np.array([[1.0 if h == 0 else 0.0,
                                   1.0 if h == 1 else 0.0, 0.0, 0.0]], np.float32)
        poh = np.zeros((E, 1), np.float32)
        if h == 1:
            im["m_caridx"] = ((c - 1) * 128 + np.arange(128, dtype=np.int32)
                              ).reshape(128, 1)
            im["m_rowidx"] = np.array([[c - 1]], np.int32)
            poh[c - 1, 0] = 1.0
        else:
            im["m_caridx"] = np.full((128, 1), NCORE * 128, np.int32)
            im["m_rowidx"] = np.array([[OOB]], np.int32)
        im["m_predoh"] = poh
        in_maps.append(im)

    res = run_bass_kernel_spmd(nc, in_maps, core_ids=list(range(NCORE)))
    out = np.concatenate([res.results[c]["out"] for c in range(NCORE)], axis=0)
    return np.ascontiguousarray(out.reshape(B, L, D).astype(np.float32))


# version 06
# speedup vs baseline: 1.2359x; 1.1064x over previous
import os
import numpy as np
import concourse.bass as bass
import concourse.mybir as mybir
from concourse.bass import IndirectOffsetOnAxis
from concourse.bass_utils import run_bass_kernel_spmd

F32 = mybir.dt.float32
BF16 = mybir.dt.bfloat16
I32 = mybir.dt.int32
AF = mybir.ActivationFunctionType
ALU = mybir.AluOpType
AX = mybir.AxisListType

NCORE = 8
D = 1024
NDC = 8
NTOK = 2048
NT = 16
Q = 512
NQ = 4
E = 8
CAP = 2560
IR = 4096
IS = 8192
DFF = 4096
LN_EPS = 1e-5
WKV_EPS = 1e-8
EPS32 = float(np.finfo(np.float32).eps)
BIG = float(2 ** 28)
OOB = (1 << 30)
CORRW = 256

ENGINES = ["sync", "scalar", "vector", "tensor", "gpsimd"]
RG = [list(range(NCORE))]


class Sched:
    def __init__(self):
        self.dry = True
        self.counts = {}
        self.cur = {}
        self.final = {}
        self.ssn = {}

    def bump(self, sem, n):
        self.counts[sem] = self.counts.get(sem, 0) + n

    def cp(self, label, sem):
        self.cur[label] = (sem, self.counts.get(sem, 0))

    def start_real(self):
        self.final = dict(self.cur)
        self.cur = {}
        self.counts = {}
        self.ssn = {}
        self.dry = False

    def check(self):
        assert self.cur == self.final, "dry/real pass divergence"


class IW:
    __slots__ = ("s", "sems", "inst")

    def __init__(self, s, sems, inst):
        self.s, self.sems, self.inst = s, sems, inst

    def then_inc(self, semname, n=1):
        self.s.bump(semname, n)
        if self.inst is not None:
            self.inst.then_inc(self.sems[semname], n)
        return self


class EW:
    def __init__(self, sched, sems, name, real):
        self.s = sched
        self.sems = sems
        self.name = name
        self.real = real

    def cp(self, label, sem):
        self.s.cp(label, sem)

    def wait(self, label):
        if self.real is not None:
            sem, v = self.s.final[label]
            if v > 0:
                self.real.wait_ge(self.sems[sem], v)

    def __getattr__(self, attr):
        def wrap(*a, **k):
            inst = getattr(self.real, attr)(*a, **k) if self.real is not None else None
            return IW(self.s, self.sems, inst)
        return wrap


def build_nc():
    nc = bass.Bass("TRN2", target_bir_lowering=False, detect_race_conditions=False)

    def inp(name, shape, dt=F32):
        return nc.dram_tensor(name, shape, dt, kind="ExternalInput")

    xs = inp("xs", [NTOK, D])
    xhalo = inp("xhalo", [1, D])
    wk_d = inp("wk", [D, D]); wv_d = inp("wv", [D, D]); wr_d = inp("wr", [D, D])
    wo_d = inp("wo", [D, D]); wfr_d = inp("wfr", [D, D])
    wfk_d = inp("wfk", [D, DFF]); wfv_d = inp("wfv", [DFF, D])
    sg_d = inp("sgw", [D, IS]); su_d = inp("suw", [D, IS]); sd_d = inp("sdw", [IS, D])
    rg_d = inp("rgw", [D, IR]); ru_d = inp("ruw", [D, IR]); rd_d = inp("rdw", [IR, D])
    rw_d = inp("rww", [E, D])
    vecs_d = inp("vecs", [9, D])
    m_selprev = inp("m_selprev", [E, 1])
    m_myid = inp("m_myid", [128, 1])
    m_flags = inp("m_flags", [1, 4])
    m_caridx = inp("m_caridx", [128, 1], I32)
    m_rowidx = inp("m_rowidx", [1, 1], I32)
    m_predoh = inp("m_predoh", [E, 1])

    out_d = nc.dram_tensor("out", [NTOK, D], F32, kind="ExternalOutput")

    _dbg = os.environ.get("BASS_KERNEL_DEBUG_DUMPS") == "1"
    _ik = dict(kind="ExternalOutput") if _dbg else {}
    xn32_d = nc.dram_tensor("xn32", [NTOK, D], F32, **_ik)
    x2_d = nc.dram_tensor("x2d", [NTOK, D], F32, **_ik)
    s32_d = nc.dram_tensor("s32d", [NTOK, D], F32, **_ik)
    xnfm_d = nc.dram_tensor("xnfm", [D, NTOK], BF16, **_ik)
    z1fm_d = nc.dram_tensor("z1fm", [D, NTOK], BF16, **_ik)
    z2fm_d = nc.dram_tensor("z2fm", [D, NTOK], BF16, **_ik)
    agx_in = nc.dram_tensor("agx_in", [NTOK, D], BF16)
    agx_out = nc.dram_tensor("agx_out", [NCORE * NTOK, D], BF16)
    agcnt_in = nc.dram_tensor("agcnt_in", [1, E], F32)
    agcnt_out = nc.dram_tensor("agcnt_out", [NCORE, E], F32)
    agrec_in = nc.dram_tensor("agrec_in", [128, 64], F32)
    agrec_out = nc.dram_tensor("agrec_out", [NCORE * 128, 64], F32)
    agcar_in = nc.dram_tensor("agcar_in", [128, 16], F32)
    agcar_out = nc.dram_tensor("agcar_out", [NCORE * 128 + 1, 16], F32)
    agx2_in = nc.dram_tensor("agx2_in", [1, D], F32)
    agx2_out = nc.dram_tensor("agx2_out", [NCORE, D], F32)
    ageo_in = nc.dram_tensor("ageo_in", [CAP, D], F32)
    ageo_out = nc.dram_tensor("ageo_out", [NCORE * CAP + 1, D], F32)
    xg_d = nc.dram_tensor("xg16", [CAP + 1, D], BF16, **_ik)
    dbg_rt = nc.dram_tensor("dbg_rt", [128, 1024], F32, kind="ExternalOutput") if _dbg else None
    dbg_sc = nc.dram_tensor("dbg_sc", [128, 20480], F32, kind="ExternalOutput") if _dbg else None
    dbg_mx = nc.dram_tensor("dbg_mx", [128, 8192], BF16, kind="ExternalOutput") if _dbg else None
    dbg_wk = nc.dram_tensor("dbg_wk", [128, 4096], BF16, kind="ExternalOutput") if _dbg else None
    dbg_h = nc.dram_tensor("dbg_h", [128, 64], F32, kind="ExternalOutput") if _dbg else None
    dbg_h16 = nc.dram_tensor("dbg_h16", [128, 8], BF16, kind="ExternalOutput") if _dbg else None
    dbg_hr = nc.dram_tensor("dbg_hr", [4, D], F32, kind="ExternalOutput") if _dbg else None

    ident32 = nc.inline_tensor(np.eye(128, dtype=np.float32), name="c_ident32")
    u_np = np.zeros((128, 128), np.float32)
    for p in range(128):
        u_np[p, p + 1:] = 1.0
    ustrict = nc.inline_tensor(u_np, name="c_ustrict")
    c_np = np.zeros((128, 128), np.float32)
    for p in range(128):
        for qq in range(128):
            if (p % 8) == (qq % 8) and (p // 8) < (qq // 8):
                c_np[p, qq] = 1.0
    ctile = nc.inline_tensor(c_np, name="c_ctile")
    ones_row = nc.inline_tensor(np.ones((1, 128), np.float32), name="c_ones_row")
    ones_col = nc.inline_tensor(np.ones((128, 1), np.float32), name="c_ones_col")
    cvals_np = np.zeros((128, 8), np.float32)
    cvals_np[:, 0] = LN_EPS
    cvals_np[:, 1] = EPS32
    cvals_np[:, 2] = WKV_EPS
    cvals_np[:, 3] = -0.5
    cvals_np[:, 4] = -1.0
    cvals_d = nc.inline_tensor(cvals_np, name="c_cvals")

    def sb(name, shape, dt):
        return nc.alloc_sbuf_tensor(name, shape, dt)

    vecr = sb("vecr", [9, D], F32)
    vcol = sb("vcol", [128, 9, 8], F32)
    ewcol = sb("ewcol", [128, 8], F32)
    netd = sb("netd", [128, 8], F32)
    tfm = sb("tfm", [128, 8], F32)
    rw16 = sb("rw16", [128, 8, 8], BF16)
    rwrow16 = sb("rwrow16", [8, D], BF16)
    haloc = sb("haloc", [128, 8], BF16)
    yhaloc = sb("yhaloc", [128, 8], BF16)
    ynprev = sb("ynprev", [128, 8], BF16)
    selprev = sb("selprev", [8, 1], F32)
    myid = sb("myid", [128, 1], F32)
    flags = sb("flags", [1, 4], F32)
    caridx = sb("caridx", [128, 1], I32)
    rowidx = sb("rowidx", [1, 1], I32)
    predoh = sb("predoh", [8, 1], F32)
    hrow = sb("hrow", [1, D], F32)
    hrow2 = sb("hrow2", [1, D], F32)
    hrow16 = sb("hrow16", [1, D], BF16)
    hstat = sb("hstat", [1, 8], F32)
    ident16 = sb("ident16", [128, 128], BF16)
    iota256 = sb("iota256", [128, CORRW], F32)
    iota8t = sb("iota8t", [128, 128], F32)
    ident_s = sb("ident_s", [128, 128], F32)
    ustr_s = sb("ustr_s", [128, 128], F32)
    ctile_s = sb("ctile_s", [128, 128], F32)
    onesr_s = sb("onesr_s", [1, 128], F32)
    onesc_s = sb("onesc_s", [128, 1], F32)
    cvals = sb("cvals", [128, 8], F32)
    flagc = sb("flagc", [128, 4], F32)

    stat = sb("stat", [128, NT, 8], F32)
    w1c = sb("w1c", [128, NT], F32)
    w2c = sb("w2c", [128, NT], F32)
    gidx1 = sb("gidx1", [128, NT], I32)
    gidx2 = sb("gidx2", [128, NT], I32)
    sidx = sb("sidx", [128, 128], I32)
    scst = sb("scst", [128, 16], F32)
    carg = sb("carg", [128, 16], F32)
    lgbuf = sb("lgbuf", [128, NT, 8], F32)
    routc = sb("routc", [128, NT, 8], F32)
    routc2 = sb("routc2", [128, NT, 8], F32)
    prefx = sb("prefx", [128, 128], F32)
    tsum = sb("tsum", [128, 1], F32)
    offsc = sb("offsc", [128, 1], F32)
    offsr = sb("offsr", [1, 128], F32)
    offsb = sb("offsb", [128, 128], F32)
    offm = sb("offm", [1, 8], F32)
    offmb = sb("offmb", [128, 8], F32)
    cntrow = sb("cntrow", [1, 128], F32)
    cnts = sb("cnts", [8, 8], F32)
    r16a = sb("r16a", [128, NT], F32)
    r16b = sb("r16b", [128, NT], F32)
    i16a = sb("i16a", [128, NT], F32)
    i16b = sb("i16b", [128, NT], F32)
    tmpc = sb("tmpc", [128, NT], F32)
    tmpc2 = sb("tmpc2", [128, NT], F32)
    recs = sb("recs", [128, 64], F32)
    rec3 = sb("rec3", [128, 16], F32)
    rec4 = sb("rec4", [128, 16], F32)
    rec5 = sb("rec5", [128, 16], F32)

    xt = sb("xt", [128, 2, D], F32)
    xnt = sb("xnt", [128, 2, D], F32)
    z1t = sb("z1t", [128, D], F32)
    sqj = sb("sqj", [128, D], BF16)
    xn16t = sb("xn16t", [128, 2, D], BF16)
    trs = sb("trs", [128, 2, 8, 128], BF16)
    g1t = sb("g1t", [128, 2, D], F32)
    g2t = sb("g2t", [128, 2, D], F32)
    xg16t = sb("xg16t", [128, 2, D], BF16)
    pool3 = sb("pool3", [128, 4096], mybir.dt.uint8)
    _b3 = nc.lookup_mloc(pool3).addr
    xgf32 = nc.alloc_sbuf_tensor_at("xgf32", [128, D], F32, offset=_b3)
    gs1 = nc.alloc_sbuf_tensor_at("gs1", [128, D], F32, offset=_b3)

    wka = sb("wka", [128, 8, D], BF16)
    wkb = sb("wkb", [128, 8, D], BF16)
    wkc = sb("wkc", [128, 8, D], BF16)

    kch = sb("kch", [128, Q], F32)
    vch = sb("vch", [128, Q], F32)
    ekb = sb("ekb", [128, Q], F32)
    ekv = sb("ekv", [128, Q], F32)
    ewb = sb("ewb", [128, Q], F32)
    wkvh = sb("wkvh", [128, 8, Q], BF16)
    corra = sb("corra", [128, 8, CORRW], F32)
    corrb = sb("corrb", [128, 8, CORRW], F32)
    pcoef = sb("pcoef", [128, CORRW], F32)
    msb = sb("msb", [128, 8, 256], BF16)

    pool1 = sb("pool1", [128, 32800], mybir.dt.uint8)
    base1 = nc.lookup_mloc(pool1).addr

    def at1(name, shape, dt, off):
        return nc.alloc_sbuf_tensor_at(name, shape, dt, offset=base1 + off)

    mixa = at1("mixa", [128, 8, Q], BF16, 0)
    mixb = at1("mixb", [128, 8, Q], BF16, 8192)
    z1h = at1("z1h", [128, 8, Q], BF16, 16384)
    xn1b = at1("xn1b", [128, 8, Q + 1], BF16, 24576)
    yn = at1("yn", [128, 8, NTOK + 1], BF16, 0)

    pool2 = sb("pool2", [128, 8192], mybir.dt.uint8)
    base2 = nc.lookup_mloc(pool2).addr

    def at2(name, shape, dt, off):
        return nc.alloc_sbuf_tensor_at(name, shape, dt, offset=base2 + off)

    rcorr = at2("rcorr", [128, 8, CORRW], BF16, 0)
    cw = at2("cw", [128, 8, CORRW], BF16, 4096)
    kkc = at2("kkc", [128, 2, 4, 256], BF16, 0)
    rrg = at2("rrg", [128, 2, D], BF16, 4096)

    xnf = nc.alloc_sbuf_tensor_at("xnf", [128, 8, 1024], BF16, offset=base1)
    hpa = nc.alloc_sbuf_tensor_at("hpa", [128, 4, 256], BF16, offset=base2)
    hpb = nc.alloc_sbuf_tensor_at("hpb", [128, 4, 256], BF16, offset=base2 + 2048)
    _bk = nc.lookup_mloc(kch).addr
    assert base1 - _bk >= 8 * 1024 * 4, (base1, _bk)
    sacc = nc.alloc_sbuf_tensor_at("sacc", [128, 8, 1024], F32, offset=_bk)

    PB = [nc.alloc_psum_tensor(f"pb{i}", [128, 512], F32) for i in range(8)]

    sems = {}
    for s in ["ld", "la", "lb", "sa", "sb", "sf", "sc", "st",
              "gp", "ga", "gb", "gc", "gr", "gs",
              "po", "cc", "cd", "ce", "pe", "ve", "ac"]:
        sems[s] = nc.alloc_semaphore(s)

    sched = Sched()

    def ssync(w, sem):
        n = sched.ssn.get(sem, 0)
        sched.ssn[sem] = n + 1
        lbl = f"zz_{sem}_{n}"
        w.cp(lbl, sem)
        w.wait(lbl)

    def vc(nm, dc):
        j = {"g1": 0, "b1": 1, "g2": 2, "b2": 3, "mk": 4, "mv": 5, "mr": 6,
             "td": 7, "tf": 8}[nm]
        return vcol[:, j, dc:dc + 1]

    def p3(pb, w_):
        return pb[:, :].rearrange("p (c t) -> p c t", t=w_)

    C_LN = lambda: cvals[:, 0:1]
    C_E32 = lambda: cvals[:, 1:2]
    C_WKV = lambda: cvals[:, 2:3]
    C_NH = lambda: cvals[:, 3:4]
    C_N1 = lambda: cvals[:, 4:5]
    C1_LN = lambda: cvals[0:1, 0:1]
    C1_NH = lambda: cvals[0:1, 3:4]

    def ph0(en, w):
        if en == "sync":
            for j in range(9):
                w.dma_start(out=vecr[j:j + 1, :], in_=vecs_d.ap()[j:j + 1, :]).then_inc("ld", 16)
            w.dma_start(out=hrow[:, :], in_=xhalo.ap()).then_inc("ld", 16)
            w.dma_start(out=selprev[:, :], in_=m_selprev.ap()).then_inc("ld", 16)
            w.dma_start(out=myid[:, :], in_=m_myid.ap()).then_inc("ld", 16)
            w.dma_start(out=flags[:, :], in_=m_flags.ap()).then_inc("ld", 16)
            w.dma_start(out=caridx[:, :], in_=m_caridx.ap()).then_inc("ld", 16)
            w.dma_start(out=rowidx[:, :], in_=m_rowidx.ap()).then_inc("ld", 16)
            w.dma_start(out=predoh[:, :], in_=m_predoh.ap()).then_inc("ld", 16)
            w.dma_start(out=ident_s[:, :], in_=ident32.ap()).then_inc("ld", 16)
            w.dma_start(out=ustr_s[:, :], in_=ustrict.ap()).then_inc("ld", 16)
            w.dma_start(out=ctile_s[:, :], in_=ctile.ap()).then_inc("ld", 16)
            w.dma_start(out=onesr_s[:, :], in_=ones_row.ap()).then_inc("ld", 16)
            w.dma_start(out=onesc_s[:, :], in_=ones_col.ap()).then_inc("ld", 16)
            w.dma_start(out=cvals[:, :], in_=cvals_d.ap()).then_inc("ld", 16)
            w.cp("ph0_ld", "ld")
        if en == "gpsimd":
            w.iota(iota256[:, :], [[1, CORRW]], base=1, channel_multiplier=0,
                   allow_small_or_imprecise_dtypes=True).then_inc("po", 1)
            w.iota(iota8t[:, :], [[0, 16], [1, 8]], base=0, channel_multiplier=0,
                   allow_small_or_imprecise_dtypes=True).then_inc("po", 1)
            w.cp("ph0_iota", "po")
            w.dma_start(out=rwrow16[:, :], in_=rw_d.ap()).then_inc("gp", 16)
            w.cp("ph0_rw16", "gp")
            w.dma_start(out=wka[:, :, :],
                        in_=wk_d.ap().rearrange("(a p) n -> p a n", p=128)).then_inc("ga", 16)
            w.cp("w_wk", "ga")
            w.dma_start(out=wkb[:, :, :],
                        in_=wv_d.ap().rearrange("(a p) n -> p a n", p=128)).then_inc("gb", 16)
            w.cp("w_wv", "gb")
        if en == "tensor":
            w.wait("ph0_ld")
            for dc in range(NDC):
                w.matmul(PB[0][:, dc * 16: dc * 16 + 9],
                         vecr[0:9, dc * 128:(dc + 1) * 128],
                         ident_s[0:9, 0:9], start=True, stop=True,
                         is_transpose=True).then_inc("pe", 1)
            w.cp("ph0_vtr", "pe")
            w.matmul(PB[3][:, 0:4], onesr_s[0:1, :], flags[0:1, :],
                     start=True, stop=True).then_inc("pe", 1)
            w.cp("ph0_flagb", "pe")
        if en == "vector":
            w.wait("ph0_vtr")
            for dc in range(NDC):
                w.tensor_copy(vcol[:, :, dc], PB[0][:, dc * 16:dc * 16 + 9]).then_inc("ve", 1)
            w.cp("ph0_vcol", "ve")
            w.tensor_copy(ident16[:, :], ident_s[:, :]).then_inc("ve", 1)
            w.cp("ph0_id16", "ve")

    def ph0rw(en, w):
        if en == "vector":
            w.wait("ph0_rw16")
            w.tensor_copy(xgf32[0:8, :], rwrow16[0:8, :]).then_inc("ve", 1)
            w.cp("ph0_rw32", "ve")
            w.wait("ph0_rwtr2")
            for dc in range(NDC):
                w.tensor_copy(rw16[:, dc, :], PB[1][:, dc * 16:dc * 16 + 8]).then_inc("ve", 1)
            w.cp("ph0_rw16c", "ve")
        if en == "tensor":
            w.wait("ph0_rw32")
            for dc in range(NDC):
                w.matmul(PB[1][:, dc * 16: dc * 16 + 8],
                         xgf32[0:8, dc * 128:(dc + 1) * 128],
                         ident_s[0:8, 0:8], start=True, stop=True,
                         is_transpose=True).then_inc("pe", 1)
            w.cp("ph0_rwtr2", "pe")

    def ph0b(en, w):
        if en == "scalar":
            w.wait("ph0_vcol")
            w.activation(netd[:, :], vcol[:, 7, :], AF.Exp).then_inc("ac", 1)
            w.cp("ph0_expt", "ac")
            w.wait("ph0_negt")
            w.activation(ewcol[:, :], netd[:, :], AF.Exp).then_inc("ac", 1)
            w.cp("ph0_ew", "ac")
        if en == "vector":
            w.wait("ph0_expt")
            w.tensor_scalar(netd[:, :], netd[:, :], -1.0, None, ALU.mult).then_inc("ve", 1)
            w.cp("ph0_negt", "ve")
            w.wait("ph0_flagb")
            w.wait("ph0_vcol")
            w.tensor_copy(flagc[:, :], PB[3][:, 0:4]).then_inc("ve", 1)
            w.cp("ph0_flagc", "ve")
            for dc in range(NDC):
                w.scalar_tensor_tensor(tfm[:, dc:dc + 1], vcol[:, 8, dc:dc + 1],
                                       1.0, PB[3][:, 0:1], ALU.mult,
                                       ALU.mult).then_inc("ve", 1)
            w.cp("ph0_tfm", "ve")

    def lnrow_chain(en, w, tag, src_row, dst_row):
        if en == "vector":
            w.wait(f"{tag}_go")
            w.tensor_reduce(hstat[0:1, 0:1], src_row, AX.X, ALU.add,
                            negate=True).then_inc("ve", 1)
            w.cp(f"{tag}_c0", "ve")
            w.wait(f"{tag}_c1")
            w.tensor_scalar(hstat[0:1, 2:3], hstat[0:1, 0:1], 1.0 / D, None,
                            ALU.mult).then_inc("ve", 1)
            w.tensor_scalar(hstat[0:1, 3:4], hstat[0:1, 1:2], 1.0 / D, None,
                            ALU.mult).then_inc("ve", 1)
            ssync(w, "ve")
            w.tensor_tensor(hstat[0:1, 4:5], hstat[0:1, 2:3], hstat[0:1, 2:3],
                            ALU.mult).then_inc("ve", 1)
            ssync(w, "ve")
            w.tensor_tensor(hstat[0:1, 4:5], hstat[0:1, 3:4], hstat[0:1, 4:5],
                            ALU.subtract).then_inc("ve", 1)
            w.cp(f"{tag}_var", "ve")
            w.wait(f"{tag}_rstd")
            w.tensor_tensor(hstat[0:1, 6:7], hstat[0:1, 2:3], hstat[0:1, 5:6],
                            ALU.mult).then_inc("ve", 1)
            w.cp(f"{tag}_bias", "ve")
            ssync(w, "ve")
            w.tensor_scalar(dst_row, src_row, hstat[0:1, 5:6], hstat[0:1, 6:7],
                            ALU.mult, ALU.add).then_inc("ve", 1)
            w.cp(f"{tag}_z", "ve")
            w.cp(f"{tag}_h16", "ve")
        if en == "scalar":
            w.wait(f"{tag}_c0")
            w.activation(sqj[0:1, :], src_row, AF.Square,
                         accum_out=hstat[0:1, 1:2]).then_inc("ac", 1)
            w.cp(f"{tag}_c1", "ac")
            w.wait(f"{tag}_var")
            w.activation(hstat[0:1, 5:6], hstat[0:1, 4:5], AF.Ln,
                         bias=C1_LN()).then_inc("ac", 1)
            ssync(w, "ac")
            w.activation(hstat[0:1, 5:6], hstat[0:1, 5:6], AF.Exp,
                         scale=C1_NH()).then_inc("ac", 1)
            w.cp(f"{tag}_rstd", "ac")
        if en == "tensor":
            w.wait(f"{tag}_h16")
            for dc in range(NDC):
                w.matmul(PB[2][:, dc * 2:dc * 2 + 1],
                         dst_row[:, dc * 128:(dc + 1) * 128] if False else
                         bass.AP(dst_row.tensor, dst_row.offset + dc * 128,
                                 [[dst_row.ap[0][0], 1], [1, 128]]),
                         ident_s[0:1, 0:1], start=True, stop=True,
                         is_transpose=True).then_inc("pe", 1)
            w.cp(f"{tag}_htr", "pe")

    def ph0c(en, w):
        if en == "sync" and dbg_hr is not None:
            w.wait("ph0_ld")
            w.dma_start(out=dbg_hr.ap()[3:4, :], in_=hrow[0:1, :]).then_inc("st", 16)
            w.cp("dbg_pre", "st")
            w.wait("hx_xn")
            w.dma_start(out=dbg_h.ap()[0:1, 44:51], in_=hstat[0:1, 0:7]).then_inc("st", 16)
            w.cp("dbg_hx", "st")
        if en == "vector":
            w.wait("ph0_ld")
            if dbg_hr is not None:
                w.wait("dbg_pre")
            w.tensor_reduce(hstat[0:1, 0:1], hrow[0:1, :], AX.X, ALU.add,
                            negate=True).then_inc("ve", 1)
            w.cp("hx_c0", "ve")
            w.wait("hx_c1")
            w.tensor_scalar(hstat[0:1, 2:3], hstat[0:1, 0:1], 1.0 / D, None,
                            ALU.mult).then_inc("ve", 1)
            w.tensor_scalar(hstat[0:1, 3:4], hstat[0:1, 1:2], 1.0 / D, None,
                            ALU.mult).then_inc("ve", 1)
            ssync(w, "ve")
            w.tensor_tensor(hstat[0:1, 4:5], hstat[0:1, 2:3], hstat[0:1, 2:3],
                            ALU.mult).then_inc("ve", 1)
            ssync(w, "ve")
            w.tensor_tensor(hstat[0:1, 4:5], hstat[0:1, 3:4], hstat[0:1, 4:5],
                            ALU.subtract).then_inc("ve", 1)
            w.cp("hx_var", "ve")
            w.wait("hx_rstd")
            w.tensor_tensor(hstat[0:1, 6:7], hstat[0:1, 2:3], hstat[0:1, 5:6],
                            ALU.mult).then_inc("ve", 1)
            w.cp("hx_bias", "ve")
            ssync(w, "ve")
            w.tensor_scalar(hrow[0:1, :], hrow[0:1, :], hstat[0:1, 5:6],
                            hstat[0:1, 6:7], ALU.mult, ALU.add).then_inc("ve", 1)
            w.cp("hx_xn", "ve")
            if dbg_hr is not None:
                w.wait("dbg_hx")
            w.cp("h1_go", "ve")
        if en == "scalar":
            w.wait("hx_c0")
            w.activation(sqj[0:1, :], hrow[0:1, :], AF.Square,
                         accum_out=hstat[0:1, 1:2]).then_inc("ac", 1)
            w.cp("hx_c1", "ac")
            w.wait("hx_var")
            w.activation(hstat[0:1, 5:6], hstat[0:1, 4:5], AF.Ln,
                         bias=C1_LN()).then_inc("ac", 1)
            ssync(w, "ac")
            w.activation(hstat[0:1, 5:6], hstat[0:1, 5:6], AF.Exp,
                         scale=C1_NH()).then_inc("ac", 1)
            w.cp("hx_rstd", "ac")
        lnrow_chain(en, w, "h1", hrow[0:1, :], hrow2[0:1, :])
        if en == "vector":
            w.wait("h1_htr")
            w.wait("ph0_flagc")
            for dc in range(NDC):
                w.tensor_scalar(haloc[:, dc:dc + 1], PB[2][:, dc * 2:dc * 2 + 1],
                                vc("g1", dc), vc("b1", dc), ALU.mult,
                                ALU.add).then_inc("ve", 1)
            ssync(w, "ve")
            w.tensor_scalar(haloc[:, :], haloc[:, :], flagc[:, 1:2], None,
                            ALU.mult).then_inc("ve", 1)
            w.cp("ph0_haloc", "ve")

    def ph1(en, w):
        if en == "sync":
            def ph1_store(i):
                ssem = "sa" if i % 2 == 0 else "sb"
                w.wait(f"e_xn_{i}")
                w.dma_start(out=xn32_d.ap()[i * 128:(i + 1) * 128, :],
                            in_=xnt[:, i % 2, :]).then_inc(ssem, 16)
                w.cp(f"e_xnst_{i}", ssem)
                w.wait(f"e_xn16_{i}")
                w.dma_start(out=agx_in.ap()[i * 128:(i + 1) * 128, :],
                            in_=xn16t[:, i % 2, :]).then_inc(ssem, 16)
                w.cp(f"e_agst_{i}", ssem)
                w.wait(f"e_tre_{i}")
                w.dma_start(
                    out=xnfm_d.ap().rearrange("(a p) t -> p a t", p=128)[:, :, i * 128:(i + 1) * 128],
                    in_=trs[:, 0, :, :]).then_inc("sf", 16)
                w.dma_start(
                    out=z1fm_d.ap().rearrange("(a p) t -> p a t", p=128)[:, :, i * 128:(i + 1) * 128],
                    in_=trs[:, 1, :, :]).then_inc("sf", 16)
                w.cp(f"e_st_{i}", "sf")
            for i in range(NT):
                lsem = "la" if i % 2 == 0 else "lb"
                if i >= 2:
                    w.wait(f"e_tr_{i - 2}")
                w.dma_start(out=xt[:, i % 2, :],
                            in_=xs.ap()[i * 128:(i + 1) * 128, :]).then_inc(lsem, 16)
                w.cp(f"e_x_{i}", lsem)
                if i >= 1:
                    ph1_store(i - 1)
            ph1_store(NT - 1)
            w.cp("e_sa_done", "sa")
            w.cp("e_sb_done", "sb")
            w.cp("e_sf_done", "sf")
        if en == "vector":
            for i in range(NT):
                st8 = stat[:, i, :]
                w.wait(f"e_x_{i}")
                w.tensor_reduce(st8[:, 0:1], xt[:, i % 2, :], AX.X, ALU.add,
                                negate=True).then_inc("ve", 1)
                w.cp(f"e_c0_{i}", "ve")
                w.wait(f"e_sq_{i}")
                w.tensor_scalar(st8[:, 0:1], st8[:, 0:1], 1.0 / D, None,
                                ALU.mult).then_inc("ve", 1)
                w.tensor_scalar(st8[:, 1:2], st8[:, 1:2], 1.0 / D, None,
                                ALU.mult).then_inc("ve", 1)
                ssync(w, "ve")
                w.tensor_tensor(st8[:, 2:3], st8[:, 0:1], st8[:, 0:1],
                                ALU.mult).then_inc("ve", 1)
                ssync(w, "ve")
                w.tensor_tensor(st8[:, 2:3], st8[:, 1:2], st8[:, 2:3],
                                ALU.subtract).then_inc("ve", 1)
                w.cp(f"e_var_{i}", "ve")
                w.wait(f"e_rstd_{i}")
                w.tensor_tensor(st8[:, 3:4], st8[:, 0:1], st8[:, 2:3],
                                ALU.mult).then_inc("ve", 1)
                w.cp(f"e_bias_{i}", "ve")
                if i >= 2:
                    w.wait(f"e_xnst_{i - 2}")
                    w.wait(f"e_petr_{i - 2}")
                ssync(w, "ve")
                w.tensor_scalar(xnt[:, i % 2, :], xt[:, i % 2, :], st8[:, 2:3],
                                st8[:, 3:4], ALU.mult, ALU.add).then_inc("ve", 1)
                w.cp(f"e_xn_{i}", "ve")
                w.tensor_reduce(st8[:, 4:5], xnt[:, i % 2, :], AX.X, ALU.add,
                                negate=True).then_inc("ve", 1)
                if i >= 2:
                    w.wait(f"e_agst_{i - 2}")
                w.tensor_copy(xn16t[:, i % 2, :], xnt[:, i % 2, :]).then_inc("ve", 1)
                w.cp(f"e_xn16_{i}", "ve")
                w.wait(f"e_sq1_{i}")
                w.tensor_scalar(st8[:, 4:5], st8[:, 4:5], 1.0 / D, None,
                                ALU.mult).then_inc("ve", 1)
                w.tensor_scalar(st8[:, 5:6], st8[:, 5:6], 1.0 / D, None,
                                ALU.mult).then_inc("ve", 1)
                ssync(w, "ve")
                w.tensor_tensor(st8[:, 6:7], st8[:, 4:5], st8[:, 4:5],
                                ALU.mult).then_inc("ve", 1)
                ssync(w, "ve")
                w.tensor_tensor(st8[:, 6:7], st8[:, 5:6], st8[:, 6:7],
                                ALU.subtract).then_inc("ve", 1)
                w.cp(f"e_var1_{i}", "ve")
                w.wait(f"e_rstd1_{i}")
                w.tensor_tensor(st8[:, 7:8], st8[:, 4:5], st8[:, 6:7],
                                ALU.mult).then_inc("ve", 1)
                w.cp(f"e_bias1_{i}", "ve")
                if i >= 1:
                    w.wait(f"e_petr_{i - 1}")
                ssync(w, "ve")
                w.tensor_scalar(z1t[:, :], xnt[:, i % 2, :], st8[:, 6:7],
                                st8[:, 7:8], ALU.mult, ALU.add).then_inc("ve", 1)
                w.cp(f"e_z1_{i}", "ve")
                w.wait(f"e_petr_{i}")
                if i >= 1:
                    w.wait(f"e_st_{i - 1}")
                w.tensor_copy(trs[:, 0, 0:4, :], p3(PB[0], 128)).then_inc("ve", 1)
                w.tensor_copy(trs[:, 0, 4:8, :], p3(PB[1], 128)).then_inc("ve", 1)
                w.tensor_copy(trs[:, 1, 0:4, :], p3(PB[2], 128)).then_inc("ve", 1)
                w.tensor_copy(trs[:, 1, 4:8, :], p3(PB[3], 128)).then_inc("ve", 1)
                w.cp(f"e_tre_{i}", "ve")
                w.cp(f"e_tr_{i}", "ve")
        if en == "scalar":
            for i in range(NT):
                st8 = stat[:, i, :]
                w.wait(f"e_c0_{i}")
                w.activation(sqj[:, :], xt[:, i % 2, :], AF.Square,
                             accum_out=st8[:, 1:2]).then_inc("ac", 1)
                w.cp(f"e_sq_{i}", "ac")
                w.wait(f"e_var_{i}")
                w.activation(st8[:, 2:3], st8[:, 2:3], AF.Ln, bias=C_LN()).then_inc("ac", 1)
                ssync(w, "ac")
                w.activation(st8[:, 2:3], st8[:, 2:3], AF.Exp, scale=C_NH()).then_inc("ac", 1)
                w.cp(f"e_rstd_{i}", "ac")
                w.wait(f"e_xn_{i}")
                w.activation(sqj[:, :], xnt[:, i % 2, :], AF.Square,
                             accum_out=st8[:, 5:6]).then_inc("ac", 1)
                w.cp(f"e_sq1_{i}", "ac")
                w.wait(f"e_var1_{i}")
                w.activation(st8[:, 5:6], st8[:, 5:6], AF.Ln, bias=C_E32()).then_inc("ac", 1)
                ssync(w, "ac")
                w.activation(st8[:, 5:6], st8[:, 5:6], AF.Exp, scale=C_NH()).then_inc("ac", 1)
                w.activation(st8[:, 6:7], st8[:, 6:7], AF.Ln, bias=C_LN()).then_inc("ac", 1)
                ssync(w, "ac")
                w.activation(st8[:, 6:7], st8[:, 6:7], AF.Exp, scale=C_NH()).then_inc("ac", 1)
                w.cp(f"e_rstd1_{i}", "ac")
        if en == "tensor":
            for i in range(NT):
                w.wait(f"e_z1_{i}")
                if i == 0:
                    w.wait("ph0_vcol")
                    w.wait("ph0_rw16c")
                    w.wait("h1_htr")
                    w.wait("ph0_haloc")
                    w.wait("ph0_tfm")
                if i >= 1:
                    w.wait(f"e_tre_{i - 1}")
                for dc in range(NDC):
                    w.matmul(PB[dc // 4][:, (dc % 4) * 128:(dc % 4 + 1) * 128],
                             xnt[:, i % 2, dc * 128:(dc + 1) * 128],
                             ident_s[:, :], start=True, stop=True,
                             is_transpose=True).then_inc("pe", 1)
                for dc in range(NDC):
                    w.matmul(PB[2 + dc // 4][:, (dc % 4) * 128:(dc % 4 + 1) * 128],
                             z1t[:, dc * 128:(dc + 1) * 128],
                             ident_s[:, :], start=True, stop=True,
                             is_transpose=True).then_inc("pe", 1)
                w.cp(f"e_petr_{i}", "pe")
        if en == "gpsimd":
            w.wait(f"e_agst_{NT - 1}")
            w.collective_compute("AllGather", ALU.bypass, replica_groups=RG,
                                 ins=[agx_in.ap()], outs=[agx_out.ap()]).then_inc("cc", 1)
            w.cp("agx_done", "cc")

    def ph2(en, w):
        if en == "sync":
            w.wait("e_sa_done")
            w.wait("e_sb_done")
            w.wait("e_sf_done")
            for i in range(NT):
                lsem = "la" if i % 2 == 0 else "lb"
                if i >= 2:
                    w.wait(f"r_lg_{i - 2}")
                w.dma_start(
                    out=trs[:, i % 2, :, :],
                    in_=xnfm_d.ap().rearrange("(a p) t -> p a t", p=128)[:, :, i * 128:(i + 1) * 128],
                ).then_inc(lsem, 16)
                w.cp(f"r_fm_{i}", lsem)
            if dbg_rt is not None:
                w.wait(f"r_lg_{NT - 1}")
                w.dma_start(out=dbg_rt.ap()[:, 370:498],
                            in_=lgbuf[:, :, :].rearrange("p a b -> p (a b)")).then_inc("st", 16)
                w.cp("dbg_lg1", "st")
                w.wait("r_sub")
                w.dma_start(out=dbg_rt.ap()[:, 498:626],
                            in_=routc[:, :, :].rearrange("p a b -> p (a b)")).then_inc("st", 16)
                w.cp("dbg_lg2", "st")
                w.wait("r_exp")
                w.dma_start(out=dbg_rt.ap()[:, 626:754],
                            in_=routc2[:, :, :].rearrange("p a b -> p (a b)")).then_inc("st", 16)
                w.cp("dbg_lg3", "st")
                w.wait("r_gidx")
                w.dma_start(out=dbg_rt.ap()[:, 0:64], in_=recs[:, :]).then_inc("st", 16)
                w.dma_start(out=dbg_rt.ap()[:, 64:192], in_=prefx[:, :]).then_inc("st", 16)
                w.dma_start(out=dbg_rt.ap()[:, 192:320], in_=offsb[:, :]).then_inc("st", 16)
                w.dma_start(out=dbg_rt.ap()[:, 320:328], in_=offmb[:, :]).then_inc("st", 16)
                w.dma_start(out=dbg_rt.ap()[:, 328:329], in_=tsum[:, :], allow_slow_non_contiguous=True).then_inc("st", 16)
                w.dma_start(out=dbg_rt.ap()[:, 329:330], in_=offsc[:, :], allow_slow_non_contiguous=True).then_inc("st", 16)
                w.dma_start(out=dbg_rt.ap()[0:8, 330:338], in_=cnts[:, :]).then_inc("st", 16)
                w.dma_start(out=dbg_rt.ap()[:, 338:354], in_=r16a[:, :]).then_inc("st", 16)
                w.dma_start(out=dbg_rt.ap()[:, 354:370], in_=i16a[:, :]).then_inc("st", 16)
                w.cp("dbg_rt_st", "st")
        if en == "tensor":
            for i in range(NT):
                w.wait(f"r_fm_{i}")
                if i >= 4:
                    w.wait(f"r_lg_{i - 4}")
                for dc in range(NDC):
                    w.matmul(PB[4 + i % 4][:, 0:8],
                             trs[:, i % 2, dc, :], rw16[:, dc, :],
                             start=(dc == 0), stop=(dc == NDC - 1)).then_inc("pe", 1)
                w.cp(f"r_mm_{i}", "pe")
            w.wait("r_mask")
            w.matmul(PB[5][:, 0:128], ustr_s[:, :],
                     routc[:, :, :].rearrange("p a b -> p (a b)"),
                     start=True, stop=True).then_inc("pe", 1)
            w.matmul(PB[5][:, 128:129],
                     routc[:, :, :].rearrange("p a b -> p (a b)"),
                     onesc_s[:, :], start=True, stop=True).then_inc("pe", 1)
            w.cp("r_pfx", "pe")
            w.wait("r_pfxe")
            w.matmul(PB[5][:, 129:130], ctile_s[:, :], tsum[:, :],
                     start=True, stop=True).then_inc("pe", 1)
            w.cp("r_offs", "pe")
            w.wait("r_offse")
            w.matmul(PB[5][0:1, 130:258], offsc[:, :], ident_s[:, :],
                     start=True, stop=True, is_transpose=True).then_inc("pe", 1)
            w.cp("r_offtr", "pe")
            w.wait("r_offre")
            w.matmul(PB[6][:, 0:128], onesr_s[0:1, :], offsr[0:1, :],
                     start=True, stop=True).then_inc("pe", 1)
            w.matmul(PB[6][0:1, 128:256], onesc_s[:, 0:1],
                     routc[:, :, :].rearrange("p a b -> p (a b)"),
                     start=True, stop=True).then_inc("pe", 1)
            w.cp("r_offb", "pe")
            w.wait("r_cnts_ld")
            w.wait("r_cnt8")
            w.matmul(PB[6][0:1, 256:264], selprev[:, :], cnts[:, :],
                     start=True, stop=True).then_inc("pe", 1)
            w.cp("r_offm_mm", "pe")
            w.wait("r_offm")
            w.matmul(PB[6][:, 264:272], onesr_s[0:1, :], offm[0:1, :],
                     start=True, stop=True).then_inc("pe", 1)
            w.cp("r_offmb_mm", "pe")
        if en == "vector":
            for i in range(NT):
                w.wait(f"r_mm_{i}")
                w.tensor_scalar(lgbuf[:, i, :], PB[4 + i % 4][:, 0:8],
                                stat[:, i, 5:6], None, ALU.mult).then_inc("ve", 1)
                w.cp(f"r_lg_{i}", "ve")
            L3 = lgbuf[:, :, :]
            i8 = iota8t[:, :].rearrange("p (a b) -> p a b", b=8)
            w.wait("ph0_iota")
            ssync(w, "ve")
            w.tensor_reduce(tmpc[:, :], L3, AX.X, ALU.max, negate=True).then_inc("ve", 1)
            ssync(w, "ve")
            a_, b_ = bass.broadcast_tensor_aps(L3, tmpc[:, :, None])
            w.tensor_tensor(routc[:, :, :], a_, b_, ALU.add).then_inc("ve", 1)
            w.cp("r_sub", "ve")
            w.wait("r_exp")
            w.tensor_reduce(tmpc[:, :], routc2[:, :, :], AX.X, ALU.add).then_inc("ve", 1)
            ssync(w, "ve")
            w.reciprocal(tmpc2[:, :], tmpc[:, :]).then_inc("ve", 1)
            ssync(w, "ve")
            if dbg_rt is not None:
                w.wait("dbg_lg1")
            a_, b_ = bass.broadcast_tensor_aps(routc2[:, :, :], tmpc2[:, :, None])
            w.tensor_tensor(lgbuf[:, :, :], a_, b_, ALU.mult).then_inc("ve", 1)
            w.tensor_reduce(w1c[:, :], lgbuf[:, :, :], AX.X, ALU.max).then_inc("ve", 1)
            ssync(w, "ve")
            if dbg_rt is not None:
                w.wait("dbg_lg2")
            a_, b_ = bass.broadcast_tensor_aps(lgbuf[:, :, :], w1c[:, :, None])
            w.tensor_tensor(routc[:, :, :], a_, b_, ALU.is_ge).then_inc("ve", 1)
            if dbg_rt is not None:
                w.wait("dbg_lg3")
            w.tensor_scalar(routc2[:, :, :], routc[:, :, :], -BIG, BIG, ALU.mult,
                            ALU.add).then_inc("ve", 1)
            w.tensor_tensor(routc2[:, :, :], i8, routc2[:, :, :],
                            ALU.add).then_inc("ve", 1)
            w.tensor_reduce(i16a[:, :], routc2[:, :, :], AX.X, ALU.min).then_inc("ve", 1)
            ssync(w, "ve")
            a_, b_ = bass.broadcast_tensor_aps(i8, i16a[:, :, None])
            w.tensor_tensor(routc[:, :, :], a_, b_, ALU.is_equal).then_inc("ve", 1)
            w.tensor_scalar(routc2[:, :, :], routc[:, :, :], -BIG, None,
                            ALU.mult).then_inc("ve", 1)
            w.tensor_tensor(routc2[:, :, :], lgbuf[:, :, :], routc2[:, :, :],
                            ALU.add).then_inc("ve", 1)
            w.tensor_reduce(w2c[:, :], routc2[:, :, :], AX.X, ALU.max).then_inc("ve", 1)
            ssync(w, "ve")
            a_, b_ = bass.broadcast_tensor_aps(routc2[:, :, :], w2c[:, :, None])
            w.tensor_tensor(routc2[:, :, :], a_, b_, ALU.is_ge).then_inc("ve", 1)
            w.tensor_scalar(routc2[:, :, :], routc2[:, :, :], -BIG, BIG, ALU.mult,
                            ALU.add).then_inc("ve", 1)
            w.tensor_tensor(routc2[:, :, :], i8, routc2[:, :, :],
                            ALU.add).then_inc("ve", 1)
            w.tensor_reduce(i16b[:, :], routc2[:, :, :], AX.X, ALU.min).then_inc("ve", 1)
            ssync(w, "ve")
            a_, b_ = bass.broadcast_tensor_aps(i8, i16b[:, :, None])
            w.tensor_tensor(routc2[:, :, :], a_, b_, ALU.is_equal).then_inc("ve", 1)
            w.tensor_tensor(routc[:, :, :], routc[:, :, :], routc2[:, :, :],
                            ALU.add).then_inc("ve", 1)
            w.cp("r_mask", "ve")
            w.tensor_tensor(tmpc[:, :], w1c[:, :], w2c[:, :], ALU.add).then_inc("ve", 1)
            ssync(w, "ve")
            w.reciprocal(tmpc[:, :], tmpc[:, :]).then_inc("ve", 1)
            ssync(w, "ve")
            w.tensor_tensor(w1c[:, :], w1c[:, :], tmpc[:, :], ALU.mult).then_inc("ve", 1)
            w.tensor_tensor(w2c[:, :], w2c[:, :], tmpc[:, :], ALU.mult).then_inc("ve", 1)
            w.cp("r_w", "ve")
            w.wait("r_pfx")
            w.tensor_copy(prefx[:, :], PB[5][:, 0:128]).then_inc("ve", 1)
            w.tensor_copy(tsum[:, :], PB[5][:, 128:129]).then_inc("ve", 1)
            w.cp("r_pfxe", "ve")
            w.wait("r_offs")
            w.tensor_copy(offsc[:, :], PB[5][:, 129:130]).then_inc("ve", 1)
            w.cp("r_offse", "ve")
            w.wait("r_offtr")
            w.tensor_copy(offsr[0:1, :], PB[5][0:1, 130:258]).then_inc("ve", 1)
            w.cp("r_offre", "ve")
            w.wait("r_offb")
            w.tensor_copy(offsb[:, :], PB[6][:, 0:128]).then_inc("ve", 1)
            w.tensor_copy(cntrow[0:1, :], PB[6][0:1, 128:256]).then_inc("ve", 1)
            w.tensor_reduce(offm[0:1, 0:8],
                            bass.AP(cntrow, 0, [[128, 1], [1, 8], [8, 16]]),
                            AX.X, ALU.add).then_inc("ve", 1)
            w.cp("r_cnt8", "ve")
            w.wait("r_offm_mm")
            w.tensor_copy(offm[0:1, :], PB[6][0:1, 256:264]).then_inc("ve", 1)
            w.cp("r_offm", "ve")
            w.wait("r_offmb_mm")
            w.tensor_copy(offmb[:, :], PB[6][:, 264:272]).then_inc("ve", 1)
            pf3 = prefx[:, :].rearrange("p (a b) -> p a b", b=8)
            ob3 = offsb[:, :].rearrange("p (a b) -> p a b", b=8)
            w.tensor_tensor(routc2[:, :, :], pf3, ob3, ALU.add).then_inc("ve", 1)
            w.tensor_tensor(routc2[:, :, :], routc2[:, :, :],
                            bass.AP(offmb, 0, [[8, 128], [0, 16], [1, 8]]),
                            ALU.add).then_inc("ve", 1)
            w.cp("r_rank3", "ve")
            for (idxs, ranks) in [(i16a, r16a), (i16b, r16b)]:
                a_, b_ = bass.broadcast_tensor_aps(i8, idxs[:, :, None])
                w.tensor_tensor(lgbuf[:, :, :], a_, b_, ALU.is_equal).then_inc("ve", 1)
                w.tensor_tensor(lgbuf[:, :, :], lgbuf[:, :, :], routc2[:, :, :],
                                ALU.mult).then_inc("ve", 1)
                w.tensor_reduce(ranks[:, :], lgbuf[:, :, :], AX.X,
                                ALU.add).then_inc("ve", 1)
            w.cp("r_rank", "ve")
            ssync(w, "ve")
            w.tensor_copy(recs[:, 0:16], i16a[:, :]).then_inc("ve", 1)
            w.tensor_copy(recs[:, 16:32], r16a[:, :]).then_inc("ve", 1)
            w.tensor_copy(recs[:, 32:48], i16b[:, :]).then_inc("ve", 1)
            w.tensor_copy(recs[:, 48:64], r16b[:, :]).then_inc("ve", 1)
            w.cp("r_recs", "ve")
            ZROW = float(NCORE * CAP)
            for (idxs, ranks, gout) in [(i16a, r16a, gidx1), (i16b, r16b, gidx2)]:
                w.tensor_scalar(tmpc[:, :], ranks[:, :], float(CAP), None,
                                ALU.is_lt).then_inc("ve", 1)
                w.tensor_scalar(tmpc2[:, :], idxs[:, :], float(CAP), None,
                                ALU.mult).then_inc("ve", 1)
                ssync(w, "ve")
                w.tensor_tensor(tmpc2[:, :], tmpc2[:, :], ranks[:, :],
                                ALU.add).then_inc("ve", 1)
                ssync(w, "ve")
                w.tensor_scalar(tmpc2[:, :], tmpc2[:, :], -ZROW, None,
                                ALU.add).then_inc("ve", 1)
                ssync(w, "ve")
                w.tensor_tensor(tmpc2[:, :], tmpc2[:, :], tmpc[:, :],
                                ALU.mult).then_inc("ve", 1)
                ssync(w, "ve")
                w.tensor_scalar(tmpc2[:, :], tmpc2[:, :], ZROW, None,
                                ALU.add).then_inc("ve", 1)
                ssync(w, "ve")
                w.tensor_copy(gout[:, :], tmpc2[:, :]).then_inc("ve", 1)
            w.cp("r_gidx", "ve")
        if en == "scalar":
            w.wait("r_sub")
            w.activation(routc2[:, :, :], routc[:, :, :], AF.Exp).then_inc("ac", 1)
            w.cp("r_exp", "ac")
        if en == "gpsimd":
            w.wait("r_cnt8")
            w.dma_start(out=agcnt_in.ap(), in_=offm[0:1, 0:8]).then_inc("gr", 16)
            w.cp("r_cnt_st", "gr")
            w.wait("r_cnt_st")
            w.collective_compute("AllGather", ALU.bypass, replica_groups=RG,
                                 ins=[agcnt_in.ap()], outs=[agcnt_out.ap()]).then_inc("ce", 1)
            w.cp("agcnt_done", "ce")
            w.wait("agcnt_done")
            w.dma_start(out=cnts[:, :], in_=agcnt_out.ap()).then_inc("gr", 16)
            w.cp("r_cnts_ld", "gr")
            w.wait("r_recs")
            w.wait("r_offm_mm")
            w.dma_start(out=agrec_in.ap(), in_=recs[:, :]).then_inc("gr", 16)
            w.cp("r_rec_st", "gr")
            w.wait("r_rec_st")
            w.collective_compute("AllGather", ALU.bypass, replica_groups=RG,
                                 ins=[agrec_in.ap()], outs=[agrec_out.ap()]).then_inc("cd", 1)
            w.cp("agrec_done", "cd")

    def ph3q(en, w, q):
        t0 = q * Q
        if en == "sync":
            if q == 0:
                w.wait("e_sa_done")
                w.wait("e_sb_done")
                w.wait("e_sf_done")
            if q >= 1:
                w.wait(f"a_rwkv_{q - 1}_7")
            w.dma_start(out=z1h[:, :, :],
                        in_=z1fm_d.ap().rearrange("(a p) t -> p a t", p=128)[:, :, t0:t0 + Q],
                        ).then_inc("ld", 16)
            w.cp(f"a_z1_{q}", "ld")
            if dbg_sc is not None and q == 0:
                w.wait("ph0_haloc")
                w.wait("ph0_tfm")
                w.dma_start(out=dbg_h16.ap(), in_=haloc[:, :]).then_inc("st", 16)
                w.dma_start(out=dbg_h.ap()[:, 8:12], in_=flagc[:, :], allow_slow_non_contiguous=True).then_inc("st", 16)
                w.dma_start(out=dbg_h.ap()[:, 12:20], in_=vcol[:, 0, :], allow_slow_non_contiguous=True).then_inc("st", 16)
                w.dma_start(out=dbg_h.ap()[:, 20:28], in_=vcol[:, 1, :], allow_slow_non_contiguous=True).then_inc("st", 16)
                w.dma_start(out=dbg_h.ap()[:, 28:36], in_=tfm[:, :], allow_slow_non_contiguous=True).then_inc("st", 16)
                w.dma_start(out=dbg_h.ap()[:, 36:44], in_=ewcol[:, :], allow_slow_non_contiguous=True).then_inc("st", 16)
                w.wait("h1_z")
                w.dma_start(out=dbg_hr.ap()[0:1, :], in_=hrow[0:1, :]).then_inc("st", 16)
                w.dma_start(out=dbg_hr.ap()[1:2, :], in_=hrow2[0:1, :]).then_inc("st", 16)
                w.dma_start(out=dbg_hr.ap()[2:3, 0:7], in_=hstat[0:1, 0:7]).then_inc("st", 16)
                w.cp("dbg_h_st", "st")
                w.wait(f"a_mixk_{q}")
                w.dma_start(out=dbg_mx.ap()[:, 0:4096],
                            in_=mixa[:, :, :]).then_inc("st", 16)
                w.wait(f"a_mixv_{q}")
                w.dma_start(out=dbg_mx.ap()[:, 4096:8192],
                            in_=mixb[:, :, :]).then_inc("st", 16)
                w.cp("dbg_mx_st", "st")
                for dc2 in range(NDC):
                    w.wait(f"a_kvev_{q}_{dc2}")
                    w.dma_start(out=dbg_sc.ap()[:, 12288 + dc2 * 512:12288 + (dc2 + 1) * 512],
                                in_=kch[:, :]).then_inc("st", 16)
                    w.dma_start(out=dbg_sc.ap()[:, 16384 + dc2 * 512:16384 + (dc2 + 1) * 512],
                                in_=vch[:, :]).then_inc("st", 16)
                    w.cp(f"dbg_kv_{dc2}", "st")
                    w.wait(f"a_scan_{q}_{dc2}")
                    w.dma_start(out=dbg_sc.ap()[:, dc2 * 512:(dc2 + 1) * 512],
                                in_=vch[:, :]).then_inc("st", 16)
                    w.dma_start(out=dbg_sc.ap()[:, 4096 + dc2 * 512:4096 + (dc2 + 1) * 512],
                                in_=kch[:, :]).then_inc("st", 16)
                    w.wait(f"a_recip_{q}_{dc2}")
                    w.dma_start(out=dbg_sc.ap()[:, 8192 + dc2 * 512:8192 + (dc2 + 1) * 512],
                                in_=ekb[:, :]).then_inc("st", 16)
                    w.cp(f"dbg_sc_{dc2}", "st")
                w.wait("a_rwkv_0_7")
                w.dma_start(out=dbg_wk.ap(),
                            in_=wkvh[:, :, :]).then_inc("st", 16)
                w.cp("dbg_wk_st", "st")
            for j in range(4):
                m = q * 4 + j
                lsem = "la" if j % 2 == 0 else "lb"
                if m < 2:
                    w.wait(f"r_lg_{NT - 1}")
                if m >= 2:
                    w.wait(f"a_x2_{(m - 2) // 4}_{(m - 2) % 4}")
                w.dma_start(out=xt[:, j % 2, :],
                            in_=xn32_d.ap()[t0 + j * 128: t0 + j * 128 + 128, :],
                            ).then_inc(lsem, 16)
                w.cp(f"a_xnrow_{q}_{j}", lsem)
            for j in range(4):
                ssem = "sa" if j % 2 == 0 else "sb"
                w.wait(f"a_x2_{q}_{j}")
                if q == 0 and j == 1:
                    w.wait("agx_done")
                w.dma_start(out=x2_d.ap()[t0 + j * 128: t0 + j * 128 + 128, :],
                            in_=xnt[:, j % 2, :]).then_inc(ssem, 16)
                w.cp(f"a_x2st_{q}_{j}", ssem)
                w.wait(f"a_ztre_{q}_{j}")
                w.dma_start(
                    out=z2fm_d.ap().rearrange("(a p) t -> p a t", p=128)[:, :, t0 + j * 128:t0 + (j + 1) * 128],
                    in_=trs[:, j % 2, :, :]).then_inc(ssem, 16)
                w.cp(f"a_z2st_{q}_{j}", ssem)
            if q == NQ - 1:
                w.dma_start(out=agx2_in.ap(), in_=xnt[127:128, 1, :]).then_inc("sc", 16)
                w.cp("x2row_pub", "sc")
                w.wait("a_scstw_3")
                w.dma_start(out=agcar_in.ap(), in_=scst[:, :]).then_inc("sc", 16)
                w.cp("car_pub", "sc")
        if en == "gpsimd":
            if q >= 1:
                w.wait(f"a_womm_{q - 1}_3")
            w.dma_start(out=wkc[:, :, :],
                        in_=wr_d.ap().rearrange("(a p) n -> p a n", p=128)).then_inc("gc", 16)
            w.cp(f"w_wr_{q}", "gc")
            w.wait(f"a_rmm_{q}_7")
            w.dma_start(out=wkc[:, :, :],
                        in_=wo_d.ap().rearrange("(a p) n -> p a n", p=128)).then_inc("gc", 16)
            w.cp(f"w_wo_{q}", "gc")
        if en == "vector":
            w.wait(f"a_z1_{q}")
            if q == 0:
                w.wait("ph0_vcol")
                w.wait("ph0_haloc")
            for dc in range(NDC):
                w.tensor_scalar(xn1b[:, dc, 1:Q + 1], z1h[:, dc, :],
                                vc("g1", dc), vc("b1", dc), ALU.mult,
                                ALU.add).then_inc("ve", 1)
            if q == 0:
                w.tensor_copy(xn1b[:, :, 0:1], haloc[:, :, None]).then_inc("ve", 1)
            else:
                w.tensor_copy(xn1b[:, :, 0:1], ynprev[:, :, None]).then_inc("ve", 1)
            ssync(w, "ve")
            w.tensor_copy(ynprev[:, :, None], xn1b[:, :, Q:Q + 1]).then_inc("ve", 1)
            ssync(w, "ve")
            for dc in range(NDC):
                w.tensor_tensor(z1h[:, dc, :], xn1b[:, dc, 1:Q + 1],
                                xn1b[:, dc, 0:Q], ALU.subtract).then_inc("ve", 1)
            w.cp(f"a_diff_{q}", "ve")
            for dc in range(NDC):
                w.scalar_tensor_tensor(mixa[:, dc, :], z1h[:, dc, :], vc("mk", dc),
                                       xn1b[:, dc, 0:Q], ALU.mult,
                                       ALU.add).then_inc("ve", 1)
            w.cp(f"a_mixk_{q}", "ve")
            for dc in range(NDC):
                w.scalar_tensor_tensor(mixb[:, dc, :], z1h[:, dc, :], vc("mv", dc),
                                       xn1b[:, dc, 0:Q], ALU.mult,
                                       ALU.add).then_inc("ve", 1)
            w.cp(f"a_mixv_{q}", "ve")
            for dc2 in range(NDC):
                w.wait(f"a_kvmm_{q}_{dc2}")
                if dc2 >= 1:
                    w.wait(f"a_recip_{q}_{dc2 - 1}")
                    if dbg_sc is not None and q == 0:
                        w.wait(f"dbg_sc_{dc2 - 1}")
                w.tensor_copy(kch[:, :], PB[0 + (dc2 % 2) * 2][:, :]).then_inc("ve", 1)
                w.tensor_copy(vch[:, :], PB[1 + (dc2 % 2) * 2][:, :]).then_inc("ve", 1)
                w.cp(f"a_kvev_{q}_{dc2}", "ve")
                w.wait(f"a_ek_{q}_{dc2}")
                w.tensor_tensor(ekv[:, :], ekb[:, :], vch[:, :], ALU.mult).then_inc("ve", 1)
                w.tensor_scalar(ewb[:, :], kch[:, :], 0.0, ewcol[:, dc2:dc2 + 1],
                                ALU.mult, ALU.add).then_inc("ve", 1)
                ini_a = scst[:, dc2:dc2 + 1] if q > 0 else 0.0
                ini_b = scst[:, 8 + dc2:8 + dc2 + 1] if q > 0 else 0.0
                if dbg_sc is not None and q == 0:
                    w.wait(f"dbg_kv_{dc2}")
                w.tensor_tensor_scan(vch[:, :], ewb[:, :], ekv[:, :], ini_a,
                                     ALU.mult, ALU.add).then_inc("ve", 1)
                w.tensor_tensor_scan(kch[:, :], ewb[:, :], ekb[:, :], ini_b,
                                     ALU.mult, ALU.add).then_inc("ve", 1)
                w.tensor_copy(scst[:, dc2:dc2 + 1], vch[:, Q - 1:Q]).then_inc("ve", 1)
                ssync(w, "ve")
                w.tensor_copy(scst[:, 8 + dc2:8 + dc2 + 1],
                              kch[:, Q - 1:Q]).then_inc("ve", 1)
                if q == 0:
                    w.tensor_copy(corra[:, dc2, :], vch[:, 0:CORRW]).then_inc("ve", 1)
                    w.tensor_copy(corrb[:, dc2, :], kch[:, 0:CORRW]).then_inc("ve", 1)
                w.cp(f"a_scan_{q}_{dc2}", "ve")
                if q == NQ - 1 and dc2 == NDC - 1:
                    w.cp("a_scstw_3", "ve")
                w.wait(f"a_recip_{q}_{dc2}")
                w.tensor_tensor(wkvh[:, dc2, :], vch[:, :], ekb[:, :],
                                ALU.mult).then_inc("ve", 1)
                w.cp(f"a_wkv_{q}_{dc2}", "ve")
            w.wait(f"a_kvmm_{q}_7")
            if dbg_sc is not None and q == 0:
                w.wait("dbg_mx_st")
            for dc in range(NDC):
                w.scalar_tensor_tensor(mixa[:, dc, :], z1h[:, dc, :], vc("mr", dc),
                                       xn1b[:, dc, 0:Q], ALU.mult,
                                       ALU.add).then_inc("ve", 1)
            w.cp(f"a_rmix_{q}", "ve")
            for dc2 in range(NDC):
                w.wait(f"a_rsig_{q}_{dc2}")
                if q == 0:
                    w.tensor_copy(rcorr[:, dc2, :], z1h[:, dc2, 0:CORRW]).then_inc("ve", 1)
                w.tensor_tensor(wkvh[:, dc2, :], wkvh[:, dc2, :], z1h[:, dc2, :],
                                ALU.mult).then_inc("ve", 1)
                w.cp(f"a_rwkv_{q}_{dc2}", "ve")
            for j in range(4):
                w.wait(f"a_womm_{q}_{j}")
                w.wait(f"a_xnrow_{q}_{j}")
                w.tensor_tensor(xnt[:, j % 2, 0:512], PB[4 + (j % 2) * 2][:, :],
                                xt[:, j % 2, 0:512], ALU.add).then_inc("ve", 1)
                w.tensor_tensor(xnt[:, j % 2, 512:1024], PB[5 + (j % 2) * 2][:, :],
                                xt[:, j % 2, 512:1024], ALU.add).then_inc("ve", 1)
                w.cp(f"a_x2_{q}_{j}", "ve")
                w.tensor_reduce(tmpc[:, 0:1], xnt[:, j % 2, :], AX.X, ALU.add,
                                negate=True).then_inc("ve", 1)
                w.cp(f"a_c2_{q}_{j}", "ve")
                w.wait(f"a_sq2_{q}_{j}")
                w.tensor_scalar(tmpc[:, 0:1], tmpc[:, 0:1], 1.0 / D, None,
                                ALU.mult).then_inc("ve", 1)
                w.tensor_scalar(tmpc[:, 1:2], tmpc[:, 1:2], 1.0 / D, None,
                                ALU.mult).then_inc("ve", 1)
                ssync(w, "ve")
                w.tensor_tensor(tmpc[:, 2:3], tmpc[:, 0:1], tmpc[:, 0:1],
                                ALU.mult).then_inc("ve", 1)
                ssync(w, "ve")
                w.tensor_tensor(tmpc[:, 2:3], tmpc[:, 1:2], tmpc[:, 2:3],
                                ALU.subtract).then_inc("ve", 1)
                w.cp(f"a_var2_{q}_{j}", "ve")
                w.wait(f"a_rstd2_{q}_{j}")
                w.tensor_tensor(tmpc[:, 3:4], tmpc[:, 0:1], tmpc[:, 2:3],
                                ALU.mult).then_inc("ve", 1)
                w.cp(f"a_bias2_{q}_{j}", "ve")
                if j >= 1:
                    w.wait(f"a_ztr_{q}_{j - 1}")
                elif q >= 1:
                    w.wait(f"a_ztr_{q - 1}_3")
                ssync(w, "ve")
                w.tensor_scalar(z1t[:, :], xnt[:, j % 2, :], tmpc[:, 2:3],
                                tmpc[:, 3:4], ALU.mult, ALU.add).then_inc("ve", 1)
                w.cp(f"a_z2_{q}_{j}", "ve")
                w.wait(f"a_ztr_{q}_{j}")
                if q == 0 and j == 0:
                    w.wait("r_offmb_mm")
                else:
                    pj = q * 4 + j - 1
                    w.wait(f"a_z2st_{pj // 4}_{pj % 4}")
                w.tensor_copy(trs[:, j % 2, 0:4, :], p3(PB[0], 128)).then_inc("ve", 1)
                w.tensor_copy(trs[:, j % 2, 4:8, :], p3(PB[1], 128)).then_inc("ve", 1)
                w.cp(f"a_ztre_{q}_{j}", "ve")
        if en == "scalar":
            for dc2 in range(NDC):
                w.wait(f"a_kvev_{q}_{dc2}")
                w.activation(ekb[:, :], kch[:, :], AF.Exp).then_inc("ac", 1)
                if q == 0:
                    w.wait("ph0_tfm")
                    w.activation(ekb[:, 0:1], kch[:, 0:1], AF.Exp,
                                 bias=tfm[:, dc2:dc2 + 1]).then_inc("ac", 1)
                w.cp(f"a_ek_{q}_{dc2}", "ac")
                w.wait(f"a_scan_{q}_{dc2}")
                w.activation(ekb[:, :], kch[:, :], AF.Ln, bias=C_WKV()).then_inc("ac", 1)
                w.activation(ekb[:, :], ekb[:, :], AF.Exp, scale=C_N1()).then_inc("ac", 1)
                w.cp(f"a_recip_{q}_{dc2}", "ac")
            for dc2 in range(NDC):
                w.wait(f"a_rmm_{q}_{dc2}")
                w.activation(z1h[:, dc2, :], PB[4 + dc2 % 4][:, :],
                             AF.Sigmoid).then_inc("ac", 1)
                w.cp(f"a_rsig_{q}_{dc2}", "ac")
            for j in range(4):
                w.wait(f"a_c2_{q}_{j}")
                w.activation(sqj[:, :], xnt[:, j % 2, :], AF.Square,
                             accum_out=tmpc[:, 1:2]).then_inc("ac", 1)
                w.cp(f"a_sq2_{q}_{j}", "ac")
                w.wait(f"a_var2_{q}_{j}")
                w.activation(tmpc[:, 2:3], tmpc[:, 2:3], AF.Ln,
                             bias=C_LN()).then_inc("ac", 1)
                ssync(w, "ac")
                w.activation(tmpc[:, 2:3], tmpc[:, 2:3], AF.Exp,
                             scale=C_NH()).then_inc("ac", 1)
                w.cp(f"a_rstd2_{q}_{j}", "ac")
        if en == "tensor":
            for dc2 in range(NDC):
                if dc2 == 0:
                    w.wait(f"a_mixk_{q}")
                    w.wait(f"a_mixv_{q}")
                    w.wait("w_wk")
                    w.wait("w_wv")
                    if q == 0:
                        w.wait("r_gidx")
                        w.wait(f"e_tr_{NT - 1}")
                    else:
                        w.wait(f"a_ztre_{q - 1}_3")
                if dc2 >= 2:
                    w.wait(f"a_kvev_{q}_{dc2 - 2}")
                for dcK in range(NDC):
                    w.matmul(PB[0 + (dc2 % 2) * 2][:, :],
                             wka[:, dcK, dc2 * 128:(dc2 + 1) * 128],
                             mixa[:, dcK, :],
                             start=(dcK == 0), stop=(dcK == NDC - 1)).then_inc("pe", 1)
                for dcK in range(NDC):
                    w.matmul(PB[1 + (dc2 % 2) * 2][:, :],
                             wkb[:, dcK, dc2 * 128:(dc2 + 1) * 128],
                             mixb[:, dcK, :],
                             start=(dcK == 0), stop=(dcK == NDC - 1)).then_inc("pe", 1)
                w.cp(f"a_kvmm_{q}_{dc2}", "pe")
            for dc2 in range(NDC):
                if dc2 == 0:
                    w.wait(f"a_rmix_{q}")
                    w.wait(f"w_wr_{q}")
                    if q == 0:
                        w.wait("r_gidx")
                    else:
                        w.wait(f"a_x2_{q - 1}_3")
                if dc2 >= 4:
                    w.wait(f"a_rsig_{q}_{dc2 - 4}")
                for dcK in range(NDC):
                    w.matmul(PB[4 + dc2 % 4][:, :],
                             wkc[:, dcK, dc2 * 128:(dc2 + 1) * 128],
                             mixa[:, dcK, :],
                             start=(dcK == 0), stop=(dcK == NDC - 1)).then_inc("pe", 1)
                w.cp(f"a_rmm_{q}_{dc2}", "pe")
            for j in range(4):
                if j == 0:
                    w.wait(f"w_wo_{q}")
                    w.wait(f"a_rwkv_{q}_7")
                    w.wait(f"a_rsig_{q}_7")
                if j >= 2:
                    w.wait(f"a_x2_{q}_{j - 2}")
                for n in range(2):
                    for dcK in range(NDC):
                        w.matmul(PB[4 + (j % 2) * 2 + n][:, :],
                                 wkvh[:, dcK, j * 128:(j + 1) * 128],
                                 wkc[:, dcK, n * 512:(n + 1) * 512],
                                 start=(dcK == 0), stop=(dcK == NDC - 1)).then_inc("pe", 1)
                w.cp(f"a_womm_{q}_{j}", "pe")
                w.wait(f"a_z2_{q}_{j}")
                if not (q == 0 and j == 0):
                    pj = q * 4 + j - 1
                    w.wait(f"a_ztre_{pj // 4}_{pj % 4}")
                for dc in range(NDC):
                    w.matmul(PB[dc // 4][:, (dc % 4) * 128:(dc % 4 + 1) * 128],
                             z1t[:, dc * 128:(dc + 1) * 128],
                             ident_s[:, :], start=True, stop=True,
                             is_transpose=True).then_inc("pe", 1)
                w.cp(f"a_ztr_{q}_{j}", "pe")

    def ph3fix(en, w):
        if en == "gpsimd":
            w.wait("car_pub")
            w.collective_compute("AllGather", ALU.bypass, replica_groups=RG,
                                 ins=[agcar_in.ap()], outs=[agcar_out.ap()[0:NCORE * 128, :]]).then_inc("ce", 1)
            w.cp("agcar_done", "ce")
            w.wait("agcar_done")
            w.memset(carg[:, :], 0.0).then_inc("po", 1)
            w.cp("car_z", "po")
            w.wait("car_z")
            w.dma_start(out=agcar_out.ap()[NCORE * 128:NCORE * 128 + 1, :],
                        in_=carg[0:1, :]).then_inc("gp", 16)
            w.cp("car_zrow", "gp")
            w.wait("car_zrow")
            w.indirect_dma_start(out=carg[:, :], out_offset=None,
                                 in_=agcar_out.ap(),
                                 in_offset=IndirectOffsetOnAxis(caridx[:, :], axis=0),
                                 ).then_inc("gp", 16)
            w.cp("car_g", "gp")
            w.wait("car_pub")
            w.collective_compute("AllGather", ALU.bypass, replica_groups=RG,
                                 ins=[agx2_in.ap()], outs=[agx2_out.ap()]).then_inc("ce", 1)
            w.cp("agx2_done", "ce")
            w.wait("agx2_done")
            w.wait("cf_p_0")
            w.dma_start(out=xgf32[0:8, :], in_=agx2_out.ap()).then_inc("gp", 16)
            w.cp("x2rows_ld", "gp")
        if en == "scalar":
            for dc in range(NDC):
                if dc == 0:
                    w.wait("car_g")
                    w.wait("ph0_iota")
                    w.wait("ph0_negt")
                w.activation(pcoef[:, :], iota256[:, :], AF.Exp,
                             scale=netd[:, dc:dc + 1]).then_inc("ac", 1)
                w.cp(f"cf_p_{dc}", "ac")
                w.wait(f"cf_ab_{dc}")
                w.activation(corrb[:, dc, :], corrb[:, dc, :], AF.Ln,
                             bias=C_WKV()).then_inc("ac", 1)
                ssync(w, "ac")
                w.activation(corrb[:, dc, :], corrb[:, dc, :], AF.Exp,
                             scale=C_N1()).then_inc("ac", 1)
                w.cp(f"cf_recip_{dc}", "ac")
            for j in range(2):
                w.wait(f"cf_c2_{j}")
                w.activation(sqj[:, :], xnt[:, j, :], AF.Square,
                             accum_out=tmpc[:, 1:2]).then_inc("ac", 1)
                w.cp(f"cf_sq2_{j}", "ac")
                w.wait(f"cf_var2_{j}")
                w.activation(tmpc[:, 2:3], tmpc[:, 2:3], AF.Ln,
                             bias=C_LN()).then_inc("ac", 1)
                ssync(w, "ac")
                w.activation(tmpc[:, 2:3], tmpc[:, 2:3], AF.Exp,
                             scale=C_NH()).then_inc("ac", 1)
                w.cp(f"cf_rstd2_{j}", "ac")
        if en == "vector":
            w.wait("y_hsel")
            w.wait("h1_z")
            w.tensor_copy(hrow[0:1, 0:512], PB[2][0:1, 0:512]).then_inc("ve", 1)
            w.tensor_copy(hrow[0:1, 512:1024], PB[3][0:1, 0:512]).then_inc("ve", 1)
            w.cp("y_hrow", "ve")
            for dc in range(NDC):
                w.wait(f"cf_p_{dc}")
                w.scalar_tensor_tensor(corra[:, dc, :], pcoef[:, :],
                                       carg[:, dc:dc + 1], corra[:, dc, :],
                                       ALU.mult, ALU.add).then_inc("ve", 1)
                w.scalar_tensor_tensor(corrb[:, dc, :], pcoef[:, :],
                                       carg[:, 8 + dc:8 + dc + 1], corrb[:, dc, :],
                                       ALU.mult, ALU.add).then_inc("ve", 1)
                w.cp(f"cf_ab_{dc}", "ve")
                w.wait(f"cf_recip_{dc}")
                w.tensor_tensor(corra[:, dc, :], corra[:, dc, :], corrb[:, dc, :],
                                ALU.mult).then_inc("ve", 1)
                w.tensor_tensor(cw[:, dc, :], corra[:, dc, :], rcorr[:, dc, :],
                                ALU.mult).then_inc("ve", 1)
            w.cp("cf_cw", "ve")
            for j in range(2):
                w.wait(f"cf_womm_{j}")
                w.wait(f"cf_xn_{j}")
                w.wait(f"a_x2st_3_{2 + j}")
                if j == 1:
                    w.wait("car_pub")
                w.tensor_tensor(xnt[:, j, 0:512], PB[4 + j * 2][:, :],
                                xt[:, j, 0:512], ALU.add).then_inc("ve", 1)
                w.tensor_tensor(xnt[:, j, 512:1024], PB[5 + j * 2][:, :],
                                xt[:, j, 512:1024], ALU.add).then_inc("ve", 1)
                w.cp(f"cf_x2_{j}", "ve")
                w.tensor_reduce(tmpc[:, 0:1], xnt[:, j, :], AX.X, ALU.add,
                                negate=True).then_inc("ve", 1)
                w.cp(f"cf_c2_{j}", "ve")
                w.wait(f"cf_sq2_{j}")
                w.tensor_scalar(tmpc[:, 0:1], tmpc[:, 0:1], 1.0 / D, None,
                                ALU.mult).then_inc("ve", 1)
                w.tensor_scalar(tmpc[:, 1:2], tmpc[:, 1:2], 1.0 / D, None,
                                ALU.mult).then_inc("ve", 1)
                ssync(w, "ve")
                w.tensor_tensor(tmpc[:, 2:3], tmpc[:, 0:1], tmpc[:, 0:1],
                                ALU.mult).then_inc("ve", 1)
                ssync(w, "ve")
                w.tensor_tensor(tmpc[:, 2:3], tmpc[:, 1:2], tmpc[:, 2:3],
                                ALU.subtract).then_inc("ve", 1)
                w.cp(f"cf_var2_{j}", "ve")
                w.wait(f"cf_rstd2_{j}")
                w.tensor_tensor(tmpc[:, 3:4], tmpc[:, 0:1], tmpc[:, 2:3],
                                ALU.mult).then_inc("ve", 1)
                w.cp(f"cf_bias2_{j}", "ve")
                if j == 1:
                    w.wait("cf_ztr_0")
                else:
                    w.wait(f"a_ztr_{NQ - 1}_3")
                ssync(w, "ve")
                w.tensor_scalar(z1t[:, :], xnt[:, j, :], tmpc[:, 2:3],
                                tmpc[:, 3:4], ALU.mult, ALU.add).then_inc("ve", 1)
                w.cp(f"cf_z2_{j}", "ve")
                w.wait(f"cf_ztr_{j}")
                if j == 0:
                    w.wait("a_z2st_3_3")
                else:
                    w.wait("cf_z2st_0")
                w.tensor_copy(trs[:, j, 0:4, :], p3(PB[0], 128)).then_inc("ve", 1)
                w.tensor_copy(trs[:, j, 4:8, :], p3(PB[1], 128)).then_inc("ve", 1)
                w.cp(f"cf_ztre_{j}", "ve")
        if en == "tensor":
            w.wait("x2rows_ld")
            w.wait(f"a_kvev_{NQ - 1}_7")
            w.matmul(PB[2][0:1, 0:512], predoh[:, :], xgf32[0:8, 0:512],
                     start=True, stop=True).then_inc("pe", 1)
            w.matmul(PB[3][0:1, 0:512], predoh[:, :], xgf32[0:8, 512:1024],
                     start=True, stop=True).then_inc("pe", 1)
            w.cp("y_hsel", "pe")
            for j in range(2):
                if j == 0:
                    w.wait("cf_cw")
                    w.wait(f"a_ztr_{NQ - 1}_3")
                    w.wait(f"a_x2_{NQ - 1}_3")
                for n in range(2):
                    for dcK in range(NDC):
                        w.matmul(PB[4 + j * 2 + n][:, :],
                                 cw[:, dcK, j * 128:(j + 1) * 128],
                                 wkc[:, dcK, n * 512:(n + 1) * 512],
                                 start=(dcK == 0), stop=(dcK == NDC - 1)).then_inc("pe", 1)
                w.cp(f"cf_womm_{j}", "pe")
                w.wait(f"cf_z2_{j}")
                if j == 0:
                    w.wait(f"a_ztre_{NQ - 1}_3")
                else:
                    w.wait("cf_ztre_0")
                for dc in range(NDC):
                    w.matmul(PB[dc // 4][:, (dc % 4) * 128:(dc % 4 + 1) * 128],
                             z1t[:, dc * 128:(dc + 1) * 128],
                             ident_s[:, :], start=True, stop=True,
                             is_transpose=True).then_inc("pe", 1)
                w.cp(f"cf_ztr_{j}", "pe")
        if en == "sync":
            for j in range(2):
                w.wait("cf_cw")
                w.wait(f"a_x2_3_{2 + j}")
                w.dma_start(out=xt[:, j, :],
                            in_=xn32_d.ap()[j * 128:(j + 1) * 128, :]).then_inc("la" if j == 0 else "lb", 16)
                w.cp(f"cf_xn_{j}", "la" if j == 0 else "lb")
            for j in range(2):
                ssem = "sa" if j == 0 else "sb"
                w.wait(f"cf_x2_{j}")
                w.dma_start(out=x2_d.ap()[j * 128:(j + 1) * 128, :],
                            in_=xnt[:, j, :]).then_inc(ssem, 16)
                w.cp(f"cf_x2st_{j}", ssem)
            for j in range(2):
                ssem = "sa" if j == 0 else "sb"
                w.wait(f"cf_ztre_{j}")
                w.dma_start(
                    out=z2fm_d.ap().rearrange("(a p) t -> p a t", p=128)[:, :, j * 128:(j + 1) * 128],
                    in_=trs[:, j, :, :]).then_inc(ssem, 16)
                w.cp(f"cf_z2st_{j}", ssem)

    def ph3yh(en, w):
        if en == "vector":
            w.wait("y_hrow")
            w.cp("yh_go", "ve")
        lnrow_chain(en, w, "yh", hrow[0:1, :], hrow2[0:1, :])
        if en == "vector":
            w.wait("yh_htr")
            for dc in range(NDC):
                w.tensor_scalar(yhaloc[:, dc:dc + 1], PB[2][:, dc * 2:dc * 2 + 1],
                                vc("g2", dc), vc("b2", dc), ALU.mult,
                                ALU.add).then_inc("ve", 1)
            ssync(w, "ve")
            w.tensor_scalar(yhaloc[:, :], yhaloc[:, :], flagc[:, 1:2], None,
                            ALU.mult).then_inc("ve", 1)
            w.cp("y_haloc", "ve")

    def ph3kk(en, w):
        if en == "gpsimd":
            w.wait("a_rwkv_3_7")
            w.wait("kk_rrwka_free")
            w.dma_start(out=wka[:, :, :],
                        in_=wfr_d.ap().rearrange("(a p) n -> p a n", p=128)).then_inc("ga", 16)
            w.cp("w_wfr", "ga")
            for g in range(8):
                for ic in range(8):
                    n_slot = g * 8 + ic
                    gsem = "ga" if ic % 2 == 0 else "gb"
                    if n_slot == 0:
                        w.wait("cf_womm_1")
                    if n_slot < 2:
                        w.wait("kk_rrmm_0_0")
                    if n_slot >= 2:
                        pg, pic = (n_slot - 2) // 8, (n_slot - 2) % 8
                        w.wait(f"kk_fkmm_{pg}_{pic}")
                        w.wait(f"kk_fv_{pg}_{pic}")
                    w.dma_start(out=wkb[:, :, (ic % 2) * 512:(ic % 2) * 512 + 512],
                                in_=wfk_d.ap().rearrange("(a p) n -> p a n", p=128)[:, :, ic * 512:(ic + 1) * 512],
                                ).then_inc(gsem, 16)
                    w.dma_start(out=wkc[:, (ic % 2) * 4:(ic % 2) * 4 + 4, :],
                                in_=wfv_d.ap().rearrange("(a p) n -> p a n", p=128)[:, ic * 4:(ic + 1) * 4, :],
                                ).then_inc(gsem, 16)
                    w.cp(f"kk_w_{g}_{ic}", gsem)
        if en == "sync":
            w.wait("cf_z2st_1")
            for qq in range(NQ):
                for j in range(4):
                    w.wait(f"a_z2st_{qq}_{j}")
            w.wait("a_rmm_3_7")
            w.wait("a_rwkv_3_7")
            w.dma_start(out=yn[:, :, 1:NTOK + 1],
                        in_=z2fm_d.ap().rearrange("(a p) t -> p a t", p=128),
                        ).then_inc("ld", 16)
            w.cp("kk_yn_ld", "ld")
            for g in range(8):
                for j in range(2):
                    lsem = "la" if j == 0 else "lb"
                    if g == 0:
                        w.wait("cf_x2st_1")
                    if g >= 1:
                        w.wait(f"kk_s_{g - 1}_{j}")
                    w.dma_start(out=g1t[:, j, :],
                                in_=x2_d.ap()[g * 256 + j * 128: g * 256 + j * 128 + 128, :],
                                ).then_inc(lsem, 16)
                    w.cp(f"kk_x2_{g}_{j}", lsem)
                for j in range(2):
                    ssem = "sa" if j == 0 else "sb"
                    w.wait(f"kk_s_{g}_{j}")
                    w.dma_start(out=s32_d.ap()[g * 256 + j * 128: g * 256 + j * 128 + 128, :],
                                in_=g2t[:, j, :]).then_inc(ssem, 16)
                    w.cp(f"kk_sst_{g}_{j}", ssem)
            w.cp("kk_s_all_a", "sa")
            w.cp("kk_s_all_b", "sb")
        if en == "vector":
            w.wait("kk_yn_ld")
            w.wait("y_haloc")
            for dc in range(NDC):
                w.tensor_scalar(yn[:, dc, 1:NTOK + 1], yn[:, dc, 1:NTOK + 1],
                                vc("g2", dc), vc("b2", dc), ALU.mult,
                                ALU.add).then_inc("ve", 1)
            w.tensor_copy(yn[:, :, 0:1], yhaloc[:, :, None]).then_inc("ve", 1)
            w.cp("kk_yn", "ve")
            for g in range(8):
                for dc in range(NDC):
                    w.tensor_tensor(msb[:, dc, :], yn[:, dc, 1 + g * 256:1 + (g + 1) * 256],
                                    yn[:, dc, g * 256:(g + 1) * 256],
                                    ALU.add).then_inc("ve", 1)
                w.cp(f"kk_ms_{g}", "ve")
                for ic in range(8):
                    w.wait(f"kk_fkmm_{g}_{ic}")
                    if g == 0 and ic == 0:
                        w.wait("cf_womm_1")
                    pa, pb_ = (ic % 2) * 2, (ic % 2) * 2 + 1
                    w.tensor_scalar(kkc[:, ic % 2, 0:2, :], p3(PB[pa], 256),
                                    0.5, 0.0, ALU.mult, ALU.max).then_inc("ve", 1)
                    w.tensor_scalar(kkc[:, ic % 2, 2:4, :], p3(PB[pb_], 256),
                                    0.5, 0.0, ALU.mult, ALU.max).then_inc("ve", 1)
                    w.tensor_tensor(kkc[:, ic % 2, :, :], kkc[:, ic % 2, :, :],
                                    kkc[:, ic % 2, :, :], ALU.mult).then_inc("ve", 1)
                    w.cp(f"kk_kk_{g}_{ic}", "ve")
                for j in range(2):
                    w.wait(f"kk_fvmm_{g}")
                    w.wait(f"kk_x2_{g}_0")
                    w.wait(f"kk_x2_{g}_1")
                    w.wait(f"kk_rrs_{g}_{j}")
                    if g >= 1:
                        w.wait(f"kk_sst_{g - 1}_{j}")
                    w.tensor_tensor(gs1[:, 0:512], rrg[:, j, 0:512],
                                    PB[4 + j * 2][:, :], ALU.mult).then_inc("ve", 1)
                    w.tensor_tensor(gs1[:, 512:1024], rrg[:, j, 512:1024],
                                    PB[5 + j * 2][:, :], ALU.mult).then_inc("ve", 1)
                    w.tensor_tensor(g2t[:, j, :], gs1[:, :], g1t[:, j, :],
                                    ALU.add).then_inc("ve", 1)
                    w.cp(f"kk_s_{g}_{j}", "ve")
        if en == "tensor":
            for g in range(8):
                for j in range(2):
                    if g == 0 and j == 0:
                        w.wait("w_wfr")
                        w.wait("kk_yn")
                        w.wait("cf_ztre_1")
                        w.wait("y_haloc")
                    else:
                        pg, pj = (g, 0) if j == 1 else (g - 1, 1)
                        w.wait(f"kk_rrs_{pg}_{pj}")
                        if j == 0:
                            w.wait(f"kk_kk_{g - 1}_7")
                    for n in range(2):
                        for dcK in range(NDC):
                            w.matmul(PB[2 + n][:, :],
                                     yn[:, dcK, 1 + g * 256 + j * 128: 1 + g * 256 + (j + 1) * 128],
                                     wka[:, dcK, n * 512:(n + 1) * 512],
                                     start=(dcK == 0), stop=(dcK == NDC - 1)).then_inc("pe", 1)
                    w.cp(f"kk_rrmm_{g}_{j}", "pe")
                    if g == 7 and j == 1:
                        w.cp("kk_rr_last", "pe")
                for ic in range(8):
                    w.wait(f"kk_w_{g}_{ic}")
                    w.wait(f"kk_ms_{g}")
                    if g == 0 and ic < 2:
                        w.wait("cf_ztre_1")
                        w.wait(f"kk_rrs_0_{1}") if False else None
                    if ic >= 2:
                        w.wait(f"kk_kk_{g}_{ic - 2}")
                    pa, pb_ = (ic % 2) * 2, (ic % 2) * 2 + 1
                    if ic % 2 == 1 and g == 0 and ic == 1:
                        w.wait("kk_rrs_0_1")
                    if ic % 2 == 1:
                        w.wait(f"kk_rrs_{g}_1")
                    for sub in range(2):
                        for dcK in range(NDC):
                            w.matmul(PB[pa][:, sub * 256:(sub + 1) * 256],
                                     wkb[:, dcK, (ic % 2) * 512 + sub * 128:(ic % 2) * 512 + (sub + 1) * 128],
                                     msb[:, dcK, :],
                                     start=(dcK == 0), stop=(dcK == NDC - 1)).then_inc("pe", 1)
                    for sub in range(2):
                        for dcK in range(NDC):
                            w.matmul(PB[pb_][:, sub * 256:(sub + 1) * 256],
                                     wkb[:, dcK, (ic % 2) * 512 + (2 + sub) * 128:(ic % 2) * 512 + (3 + sub) * 128],
                                     msb[:, dcK, :],
                                     start=(dcK == 0), stop=(dcK == NDC - 1)).then_inc("pe", 1)
                    w.cp(f"kk_fkmm_{g}_{ic}", "pe")
                    w.wait(f"kk_kk_{g}_{ic}")
                    if ic == 0:
                        if g >= 1:
                            w.wait(f"kk_s_{g - 1}_1")
                        else:
                            w.wait("cf_x2_1")
                    for j in range(2):
                        for n in range(2):
                            for sub in range(4):
                                w.matmul(PB[4 + j * 2 + n][:, :],
                                         kkc[:, ic % 2, sub, j * 128:(j + 1) * 128],
                                         wkc[:, (ic % 2) * 4 + sub, n * 512:(n + 1) * 512],
                                         start=(ic == 0 and sub == 0),
                                         stop=(ic == 7 and sub == 3)).then_inc("pe", 1)
                    w.cp(f"kk_fv_{g}_{ic}", "pe")
                    if ic == 7:
                        w.cp(f"kk_fvmm_{g}", "pe")
            w.cp("kk_pe_done", "pe")
        if en == "scalar":
            for g in range(8):
                for j in range(2):
                    w.wait(f"kk_rrmm_{g}_{j}")
                    if g == 0 and j == 0:
                        w.wait("cf_womm_1")
                    w.activation(rrg[:, j, 0:512], PB[2][:, :], AF.Sigmoid).then_inc("ac", 1)
                    w.activation(rrg[:, j, 512:1024], PB[3][:, :], AF.Sigmoid).then_inc("ac", 1)
                    w.cp(f"kk_rrs_{g}_{j}", "ac")

    def ph3kk_pre(en, w):
        if en == "tensor":
            pass
        if en == "gpsimd":
            pass
        if en == "vector":
            pass

    def ph4(en, w):
        PASSES = [(0, 8), (8, 8), (16, 4)]
        if en == "sync":
            for cc in range(NCORE):
                if cc == 0:
                    w.wait("agrec_done")
                    w.wait("kk_yn")
                if cc >= 1:
                    w.wait(f"x_sidx_{cc - 1}")
                w.dma_start(out=recs[:, :],
                            in_=agrec_out.ap()[cc * 128:(cc + 1) * 128, :]).then_inc("ld", 16)
                w.cp(f"x_rec_{cc}", "ld")
            w.wait("x_zt")
            for j in range(CAP // 128):
                w.dma_start(out=xg_d.ap()[j * 128:(j + 1) * 128, :],
                            in_=xg16t[:, 0, :]).then_inc("sa", 16)
            w.cp("x_zeroed", "sa")
            for i in range(NCORE * NT):
                cc, t = i // NT, i % NT
                lsem = "la" if i % 2 == 0 else "lb"
                if i < 2:
                    w.wait("agx_done")
                    w.wait("x_zeroed")
                if i >= 2:
                    w.wait(f"x_scat_{i - 2}")
                w.dma_start(out=xg16t[:, i % 2, :],
                            in_=agx_out.ap()[cc * NTOK + t * 128: cc * NTOK + t * 128 + 128, :],
                            ).then_inc(lsem, 16)
                w.cp(f"x_row_{i}", lsem)
            for p, (t0p, ntp) in enumerate(PASSES):
                for tl in range(ntp):
                    tg = t0p + tl
                    lsem = "la" if tg % 2 == 0 else "lb"
                    if tg < 2:
                        w.wait("x_scat_all_r")
                        w.wait("x_scat_all_s")
                    else:
                        w.wait(f"x_xg32_{tg - 2}")
                    w.dma_start(out=xg16t[:, tg % 2, :],
                                in_=xg_d.ap()[tg * 128:(tg + 1) * 128, :],
                                ).then_inc(lsem, 16)
                    w.cp(f"x_xgld_{tg}", lsem)
                for tl in range(ntp):
                    tg = t0p + tl
                    ssem = "sa" if tg % 2 == 0 else "sb"
                    w.wait(f"x_sacc_{p}_7_{tl // 2}")
                    w.dma_start(out=ageo_in.ap()[tg * 128:(tg + 1) * 128, :],
                                in_=sacc[:, tl, :]).then_inc(ssem, 16)
                    w.cp(f"x_eost_{tg}", ssem)
        if en == "gpsimd":
            w.wait("kk_s_all_a")
            w.wait("kk_s_all_b")
            w.memset(gs1[:, :], 0.0).then_inc("po", 1)
            w.cp("f_zrow_ms", "po")
            w.wait("f_zrow_ms")
            w.dma_start(out=ageo_out.ap()[NCORE * CAP:NCORE * CAP + 1, :],
                        in_=gs1[0:1, :]).then_inc("gp", 16)
            w.cp("f_zrow", "gp")
            for i in range(NCORE * NT):
                cc, t = i // NT, i % NT
                gsem = "gr" if i % 2 == 0 else "gs"
                if i == 0:
                    w.wait("x_zeroed")
                    w.wait("x_sidx")
                w.wait(f"x_row_{i}")
                w.indirect_dma_start(
                    out=xg_d.ap(),
                    out_offset=IndirectOffsetOnAxis(sidx[:, cc * 16 + t: cc * 16 + t + 1], axis=0),
                    in_=xg16t[:, i % 2, :], in_offset=None).then_inc(gsem, 16)
                w.cp(f"x_scat_{i}", gsem)
            w.cp("x_scat_all_r", "gr")
            w.cp("x_scat_all_s", "gs")
            for p, (t0p, ntp) in enumerate(PASSES):
                nbp = ntp // 2
                for ic in range(8):
                    n_slot = p * 8 + ic
                    gsem = "ga" if ic % 2 == 0 else "gb"
                    if n_slot == 0:
                        w.wait("kk_pe_done")
                        w.wait("kk_rr_last")
                    if n_slot >= 2:
                        pp, pic = (n_slot - 2) // 8, (n_slot - 2) % 8
                        pnb = PASSES[pp][1] // 2
                        w.wait(f"x_h2_{pp}_{pic}_{pnb - 1}")
                        w.wait(f"x_sd_{pp}_{pic}_{pnb - 1}")
                    w.dma_start(out=wka[:, :, (ic % 2) * 512:(ic % 2) * 512 + 512],
                                in_=rg_d.ap().rearrange("(a p) n -> p a n", p=128)[:, :, ic * 512:(ic + 1) * 512],
                                ).then_inc(gsem, 16)
                    w.dma_start(out=wkb[:, :, (ic % 2) * 512:(ic % 2) * 512 + 512],
                                in_=ru_d.ap().rearrange("(a p) n -> p a n", p=128)[:, :, ic * 512:(ic + 1) * 512],
                                ).then_inc(gsem, 16)
                    w.dma_start(out=wkc[:, (ic % 2) * 4:(ic % 2) * 4 + 4, :],
                                in_=rd_d.ap().rearrange("(a p) n -> p a n", p=128)[:, ic * 4:(ic + 1) * 4, :],
                                ).then_inc(gsem, 16)
                    w.cp(f"x_w_{p}_{ic}", gsem)
            w.wait("x_eost_18")
            w.wait("x_eost_19")
            w.collective_compute("AllGather", ALU.bypass, replica_groups=RG,
                                 ins=[ageo_in.ap()], outs=[ageo_out.ap()[0:NCORE * CAP, :]]).then_inc("ce", 1)
            w.cp("ageo_done", "ce")
        if en == "tensor":
            for p, (t0p, ntp) in enumerate(PASSES):
                nbp = ntp // 2
                for tl in range(ntp):
                    tg = t0p + tl
                    w.wait(f"x_xg32_{tg}")
                    if tg == 0:
                        w.wait("kk_kk_7_7")
                    if p > 0 and tl == 0:
                        w.wait(f"x_hsilu_{p - 1}_7_{PASSES[p - 1][1] // 2 - 1}")
                    if tg >= 1:
                        w.wait(f"x_xgf_{tg - 1}")
                    for dc in range(NDC):
                        w.matmul(PB[dc // 4][:, (dc % 4) * 128:(dc % 4 + 1) * 128],
                                 xgf32[:, dc * 128:(dc + 1) * 128],
                                 ident_s[:, :], start=True, stop=True,
                                 is_transpose=True).then_inc("pe", 1)
                    w.cp(f"x_xgtr_{tg}", "pe")
                for ic in range(8):
                    w.wait(f"x_w_{p}_{ic}")
                    for b in range(nbp):
                        if ic == 0:
                            w.wait(f"x_xgf_{t0p + ntp - 1}")
                        if p == 0 and ic == 0 and b == 0:
                            w.wait("kk_rrs_7_1")
                            w.wait("kk_s_7_1")
                        else:
                            pn_p, pn_ic, pn_b = (p, ic, b - 1) if b > 0 else \
                                ((p, ic - 1, nbp - 1) if ic > 0 else
                                 (p - 1, 7, PASSES[p - 1][1] // 2 - 1))
                            w.wait(f"x_hsilu_{pn_p}_{pn_ic}_{pn_b}")
                        for sub in range(2):
                            for dcK in range(NDC):
                                w.matmul(PB[0][:, sub * 256:(sub + 1) * 256],
                                         wka[:, dcK, (ic % 2) * 512 + sub * 128:(ic % 2) * 512 + (sub + 1) * 128],
                                         xnf[:, dcK, b * 256:(b + 1) * 256],
                                         start=(dcK == 0), stop=(dcK == NDC - 1)).then_inc("pe", 1)
                        for sub in range(2):
                            for dcK in range(NDC):
                                w.matmul(PB[1][:, sub * 256:(sub + 1) * 256],
                                         wka[:, dcK, (ic % 2) * 512 + (2 + sub) * 128:(ic % 2) * 512 + (3 + sub) * 128],
                                         xnf[:, dcK, b * 256:(b + 1) * 256],
                                         start=(dcK == 0), stop=(dcK == NDC - 1)).then_inc("pe", 1)
                        w.cp(f"x_h1_{p}_{ic}_{b}", "pe")
                        if not (p == 0 and ic == 0 and b == 0):
                            pn_p, pn_ic, pn_b = (p, ic, b - 1) if b > 0 else \
                                ((p, ic - 1, nbp - 1) if ic > 0 else
                                 (p - 1, 7, PASSES[p - 1][1] // 2 - 1))
                            w.wait(f"x_hp_{pn_p}_{pn_ic}_{pn_b}")
                        for sub in range(2):
                            for dcK in range(NDC):
                                w.matmul(PB[2][:, sub * 256:(sub + 1) * 256],
                                         wkb[:, dcK, (ic % 2) * 512 + sub * 128:(ic % 2) * 512 + (sub + 1) * 128],
                                         xnf[:, dcK, b * 256:(b + 1) * 256],
                                         start=(dcK == 0), stop=(dcK == NDC - 1)).then_inc("pe", 1)
                        for sub in range(2):
                            for dcK in range(NDC):
                                w.matmul(PB[3][:, sub * 256:(sub + 1) * 256],
                                         wkb[:, dcK, (ic % 2) * 512 + (2 + sub) * 128:(ic % 2) * 512 + (3 + sub) * 128],
                                         xnf[:, dcK, b * 256:(b + 1) * 256],
                                         start=(dcK == 0), stop=(dcK == NDC - 1)).then_inc("pe", 1)
                        w.cp(f"x_h2_{p}_{ic}_{b}", "pe")
                        w.wait(f"x_hp_{p}_{ic}_{b}")
                        if not (p == 0 and ic == 0 and b == 0):
                            pn_p, pn_ic, pn_b = (p, ic, b - 1) if b > 0 else \
                                ((p, ic - 1, nbp - 1) if ic > 0 else
                                 (p - 1, 7, PASSES[p - 1][1] // 2 - 1))
                            w.wait(f"x_sacc_{pn_p}_{pn_ic}_{pn_b}")
                        for j in range(2):
                            for n in range(2):
                                for sub in range(4):
                                    w.matmul(PB[4 + j * 2 + n][:, :],
                                             hpa[:, sub, j * 128:(j + 1) * 128],
                                             wkc[:, (ic % 2) * 4 + sub, n * 512:(n + 1) * 512],
                                             start=(sub == 0), stop=(sub == 3)).then_inc("pe", 1)
                        w.cp(f"x_sd_{p}_{ic}_{b}", "pe")
            w.cp("x_pe_done", "pe")
        if en == "scalar":
            for p, (t0p, ntp) in enumerate(PASSES):
                nbp = ntp // 2
                for ic in range(8):
                    for b in range(nbp):
                        w.wait(f"x_h1_{p}_{ic}_{b}")
                        w.activation(hpb[:, 0:2, :], PB[0][:, :], AF.Silu).then_inc("ac", 1)
                        w.activation(hpb[:, 2:4, :], PB[1][:, :], AF.Silu).then_inc("ac", 1)
                        w.cp(f"x_hsilu_{p}_{ic}_{b}", "ac")
        if en == "vector":
            w.wait("kk_s_all_a")
            w.wait("kk_s_all_b")
            w.memset(xg16t[:, 0, :], 0.0).then_inc("ve", 1)
            w.cp("x_zt", "ve")
            for cc in range(NCORE):
                w.wait(f"x_rec_{cc}")
                w.tensor_scalar(rec3[:, :], recs[:, 0:16], myid[:, :], None,
                                ALU.is_equal).then_inc("ve", 1)
                ssync(w, "ve")
                w.tensor_tensor(rec4[:, :], rec3[:, :], recs[:, 16:32],
                                ALU.mult).then_inc("ve", 1)
                w.tensor_scalar(rec5[:, :], recs[:, 32:48], myid[:, :], None,
                                ALU.is_equal).then_inc("ve", 1)
                ssync(w, "ve")
                w.tensor_tensor(tmpc[:, :], rec5[:, :], recs[:, 48:64],
                                ALU.mult).then_inc("ve", 1)
                ssync(w, "ve")
                w.tensor_tensor(rec4[:, :], rec4[:, :], tmpc[:, :],
                                ALU.add).then_inc("ve", 1)
                w.tensor_tensor(rec3[:, :], rec3[:, :], rec5[:, :],
                                ALU.add).then_inc("ve", 1)
                ssync(w, "ve")
                w.tensor_scalar(rec3[:, :], rec3[:, :], -1.0, 1.0,
                                ALU.mult, ALU.add).then_inc("ve", 1)
                ssync(w, "ve")
                w.scalar_tensor_tensor(rec4[:, :], rec3[:, :], float(CAP),
                                       rec4[:, :], ALU.mult,
                                       ALU.add).then_inc("ve", 1)
                ssync(w, "ve")
                w.tensor_scalar(rec4[:, :], rec4[:, :], float(CAP), None,
                                ALU.min).then_inc("ve", 1)
                ssync(w, "ve")
                w.tensor_copy(sidx[:, cc * 16:(cc + 1) * 16], rec4[:, :]).then_inc("ve", 1)
                w.cp(f"x_sidx_{cc}", "ve")
            w.cp("x_sidx", "ve")
            for p, (t0p, ntp) in enumerate(PASSES):
                nbp = ntp // 2
                for tl in range(ntp):
                    tg = t0p + tl
                    w.wait(f"x_xgld_{tg}")
                    if tg == 0:
                        w.wait("f_zrow")
                    if tg >= 1:
                        w.wait(f"x_xgtr_{tg - 1}")
                    w.tensor_copy(xgf32[:, :], xg16t[:, tg % 2, :]).then_inc("ve", 1)
                    w.cp(f"x_xg32_{tg}", "ve")
                    w.wait(f"x_xgtr_{tg}")
                    if p > 0:
                        w.wait(f"x_eost_{8 * (p - 1) + tl}")
                    w.tensor_copy(xnf[:, 0:4, tl * 128:(tl + 1) * 128],
                                  p3(PB[0], 128)).then_inc("ve", 1)
                    w.tensor_copy(xnf[:, 4:8, tl * 128:(tl + 1) * 128],
                                  p3(PB[1], 128)).then_inc("ve", 1)
                    w.cp(f"x_xgf_{tg}", "ve")
                for ic in range(8):
                    for b in range(nbp):
                        w.wait(f"x_h2_{p}_{ic}_{b}")
                        w.wait(f"x_hsilu_{p}_{ic}_{b}")
                        w.tensor_tensor(hpa[:, 0:2, :], hpb[:, 0:2, :],
                                        p3(PB[2], 256), ALU.mult).then_inc("ve", 1)
                        w.tensor_tensor(hpa[:, 2:4, :], hpb[:, 2:4, :],
                                        p3(PB[3], 256), ALU.mult).then_inc("ve", 1)
                        w.cp(f"x_hp_{p}_{ic}_{b}", "ve")
                        w.wait(f"x_sd_{p}_{ic}_{b}")
                        if ic == 0 and p > 0:
                            w.wait(f"x_eost_{8 * (p - 1) + 2 * b}")
                            w.wait(f"x_eost_{8 * (p - 1) + 2 * b + 1}")
                        for j in range(2):
                            for n in range(2):
                                if ic == 0:
                                    w.tensor_copy(sacc[:, 2 * b + j, n * 512:(n + 1) * 512],
                                                  PB[4 + j * 2 + n][:, :]).then_inc("ve", 1)
                                else:
                                    w.tensor_tensor(sacc[:, 2 * b + j, n * 512:(n + 1) * 512],
                                                    sacc[:, 2 * b + j, n * 512:(n + 1) * 512],
                                                    PB[4 + j * 2 + n][:, :],
                                                    ALU.add).then_inc("ve", 1)
                        w.cp(f"x_sacc_{p}_{ic}_{b}", "ve")

    def ph5(en, w):
        NB = 4
        if en == "gpsimd":
            for h in range(2):
                for ic in range(16):
                    n_slot = h * 16 + ic
                    gsem = "ga" if ic % 2 == 0 else "gb"
                    if n_slot == 0:
                        w.wait("x_pe_done")
                    if n_slot >= 2:
                        ph_, pic = (n_slot - 2) // 16, (n_slot - 2) % 16
                        w.wait(f"f_h2_{ph_}_{pic}_{NB - 1}")
                        w.wait(f"f_sd_{ph_}_{pic}_{NB - 1}")
                    w.dma_start(out=wka[:, :, (ic % 2) * 512:(ic % 2) * 512 + 512],
                                in_=sg_d.ap().rearrange("(a p) n -> p a n", p=128)[:, :, ic * 512:(ic + 1) * 512],
                                ).then_inc(gsem, 16)
                    w.dma_start(out=wkb[:, :, (ic % 2) * 512:(ic % 2) * 512 + 512],
                                in_=su_d.ap().rearrange("(a p) n -> p a n", p=128)[:, :, ic * 512:(ic + 1) * 512],
                                ).then_inc(gsem, 16)
                    w.dma_start(out=wkc[:, (ic % 2) * 4:(ic % 2) * 4 + 4, :],
                                in_=sd_d.ap().rearrange("(a p) n -> p a n", p=128)[:, ic * 4:(ic + 1) * 4, :],
                                ).then_inc(gsem, 16)
                    w.cp(f"f_w_{h}_{ic}", gsem)
                for ti in range(h * 8, h * 8 + 8):
                    gsem = "gr" if ti % 2 == 0 else "gs"
                    j = ti % 2
                    if ti == 0:
                        w.wait("ageo_done")
                        w.wait("f_zrow")
                    if ti >= 2:
                        w.wait(f"f_rt_{ti - 2}")
                        w.wait(f"f_out_{ti - 2}")
                    w.indirect_dma_start(
                        out=g1t[:, j, :], out_offset=None, in_=ageo_out.ap(),
                        in_offset=IndirectOffsetOnAxis(gidx1[:, ti:ti + 1], axis=0),
                        ).then_inc(gsem, 16)
                    w.indirect_dma_start(
                        out=g2t[:, j, :], out_offset=None, in_=ageo_out.ap(),
                        in_offset=IndirectOffsetOnAxis(gidx2[:, ti:ti + 1], axis=0),
                        ).then_inc(gsem, 16)
                    w.cp(f"f_g_{ti}", gsem)
        if en == "sync":
            for h in range(2):
                if h == 0:
                    w.wait("x_h2_2_7_1")
                else:
                    w.wait(f"f_h2_0_15_{NB - 1}")
                w.dma_start(out=xnf[:, :, :],
                            in_=xnfm_d.ap().rearrange("(a p) t -> p a t", p=128)[:, :, h * 1024:(h + 1) * 1024],
                            ).then_inc("ld", 16)
                w.cp(f"f_xnf_{h}", "ld")
                for ti in range(h * 8, h * 8 + 8):
                    j = ti % 2
                    lsem = "la" if j == 0 else "lb"
                    if ti >= 2:
                        w.wait(f"f_o_{ti - 2}")
                    w.dma_start(out=xt[:, j, :],
                                in_=xs.ap()[ti * 128:(ti + 1) * 128, :]).then_inc(lsem, 16)
                    w.dma_start(out=xnt[:, j, :],
                                in_=s32_d.ap()[ti * 128:(ti + 1) * 128, :]).then_inc(lsem, 16)
                    w.cp(f"f_xs_{ti}", lsem)
                    w.wait(f"f_o_{ti}")
                    ssem = "sa" if j == 0 else "sb"
                    w.dma_start(out=out_d.ap()[ti * 128:(ti + 1) * 128, :],
                                in_=g2t[:, j, :]).then_inc(ssem, 16)
                    w.cp(f"f_out_{ti}", ssem)
            w.cp("f_out_all_a", "sa")
            w.cp("f_out_all_b", "sb")
            w.wait("f_out_all_a")
            w.wait("f_out_all_b")
        if en == "tensor":
            for h in range(2):
                for ic in range(16):
                    w.wait(f"f_w_{h}_{ic}")
                    w.wait(f"f_xnf_{h}")
                    if h == 0 and ic == 0:
                        w.wait("x_hsilu_2_7_1")
                        w.wait("x_hp_2_7_1")
                        w.wait("x_sacc_2_7_1")
                    for b in range(NB):
                        n_slot = (h * 16 + ic) * NB + b
                        if n_slot >= 1:
                            pn = n_slot - 1
                            w.wait(f"f_hsilu_{pn // (16 * NB)}_{(pn // NB) % 16}_{pn % NB}")
                        for sub in range(2):
                            for dcK in range(NDC):
                                w.matmul(PB[0][:, sub * 256:(sub + 1) * 256],
                                         wka[:, dcK, (ic % 2) * 512 + sub * 128:(ic % 2) * 512 + (sub + 1) * 128],
                                         xnf[:, dcK, b * 256:(b + 1) * 256],
                                         start=(dcK == 0), stop=(dcK == NDC - 1)).then_inc("pe", 1)
                        for sub in range(2):
                            for dcK in range(NDC):
                                w.matmul(PB[1][:, sub * 256:(sub + 1) * 256],
                                         wka[:, dcK, (ic % 2) * 512 + (2 + sub) * 128:(ic % 2) * 512 + (3 + sub) * 128],
                                         xnf[:, dcK, b * 256:(b + 1) * 256],
                                         start=(dcK == 0), stop=(dcK == NDC - 1)).then_inc("pe", 1)
                        w.cp(f"f_h1_{h}_{ic}_{b}", "pe")
                        if n_slot >= 1:
                            pn = n_slot - 1
                            w.wait(f"f_hp_{pn // (16 * NB)}_{(pn // NB) % 16}_{pn % NB}")
                        for sub in range(2):
                            for dcK in range(NDC):
                                w.matmul(PB[2][:, sub * 256:(sub + 1) * 256],
                                         wkb[:, dcK, (ic % 2) * 512 + sub * 128:(ic % 2) * 512 + (sub + 1) * 128],
                                         xnf[:, dcK, b * 256:(b + 1) * 256],
                                         start=(dcK == 0), stop=(dcK == NDC - 1)).then_inc("pe", 1)
                        for sub in range(2):
                            for dcK in range(NDC):
                                w.matmul(PB[3][:, sub * 256:(sub + 1) * 256],
                                         wkb[:, dcK, (ic % 2) * 512 + (2 + sub) * 128:(ic % 2) * 512 + (3 + sub) * 128],
                                         xnf[:, dcK, b * 256:(b + 1) * 256],
                                         start=(dcK == 0), stop=(dcK == NDC - 1)).then_inc("pe", 1)
                        w.cp(f"f_h2_{h}_{ic}_{b}", "pe")
                        w.wait(f"f_hp_{h}_{ic}_{b}")
                        if n_slot >= 1:
                            pn = n_slot - 1
                            w.wait(f"f_sacc_{pn // (16 * NB)}_{(pn // NB) % 16}_{pn % NB}")
                        for j in range(2):
                            for n in range(2):
                                for sub in range(4):
                                    w.matmul(PB[4 + j * 2 + n][:, :],
                                             hpa[:, sub, j * 128:(j + 1) * 128],
                                             wkc[:, (ic % 2) * 4 + sub, n * 512:(n + 1) * 512],
                                             start=(sub == 0), stop=(sub == 3)).then_inc("pe", 1)
                        w.cp(f"f_sd_{h}_{ic}_{b}", "pe")
        if en == "scalar":
            for h in range(2):
                for ic in range(16):
                    for b in range(NB):
                        w.wait(f"f_h1_{h}_{ic}_{b}")
                        w.activation(hpb[:, 0:2, :], PB[0][:, :], AF.Silu).then_inc("ac", 1)
                        w.activation(hpb[:, 2:4, :], PB[1][:, :], AF.Silu).then_inc("ac", 1)
                        w.cp(f"f_hsilu_{h}_{ic}_{b}", "ac")
                for ti in range(h * 8, h * 8 + 8):
                    j = ti % 2
                    w.wait(f"f_gsum_{ti}")
                    w.activation(g1t[:, j, :], g1t[:, j, :], AF.Silu).then_inc("ac", 1)
                    w.cp(f"f_silu_{ti}", "ac")
        if en == "vector":
            for h in range(2):
                for ic in range(16):
                    for b in range(NB):
                        w.wait(f"f_h2_{h}_{ic}_{b}")
                        w.wait(f"f_hsilu_{h}_{ic}_{b}")
                        w.tensor_tensor(hpa[:, 0:2, :], hpb[:, 0:2, :],
                                        p3(PB[2], 256), ALU.mult).then_inc("ve", 1)
                        w.tensor_tensor(hpa[:, 2:4, :], hpb[:, 2:4, :],
                                        p3(PB[3], 256), ALU.mult).then_inc("ve", 1)
                        w.cp(f"f_hp_{h}_{ic}_{b}", "ve")
                        w.wait(f"f_sd_{h}_{ic}_{b}")
                        if h == 0 and ic == 0:
                            for s in (2 * b, 2 * b + 1):
                                w.wait(f"x_eost_{16 + s if s < 4 else 8 + s}")
                        for j in range(2):
                            for n in range(2):
                                if ic == 0:
                                    w.tensor_copy(sacc[:, 2 * b + j, n * 512:(n + 1) * 512],
                                                  PB[4 + j * 2 + n][:, :]).then_inc("ve", 1)
                                else:
                                    w.tensor_tensor(sacc[:, 2 * b + j, n * 512:(n + 1) * 512],
                                                    sacc[:, 2 * b + j, n * 512:(n + 1) * 512],
                                                    PB[4 + j * 2 + n][:, :],
                                                    ALU.add).then_inc("ve", 1)
                        w.cp(f"f_sacc_{h}_{ic}_{b}", "ve")
                for ti in range(h * 8, h * 8 + 8):
                    j = ti % 2
                    tloc = ti - h * 8
                    w.wait(f"f_g_{ti}")
                    w.wait(f"f_sacc_{h}_15_{tloc // 2}")
                    w.tensor_scalar(g1t[:, j, :], g1t[:, j, :], w1c[:, ti:ti + 1],
                                    None, ALU.mult).then_inc("ve", 1)
                    w.scalar_tensor_tensor(g1t[:, j, :], g2t[:, j, :],
                                           w2c[:, ti:ti + 1], g1t[:, j, :],
                                           ALU.mult, ALU.add).then_inc("ve", 1)
                    w.tensor_tensor(g1t[:, j, 0:512], g1t[:, j, 0:512],
                                    sacc[:, tloc, 0:512], ALU.add).then_inc("ve", 1)
                    w.tensor_tensor(g1t[:, j, 512:1024], g1t[:, j, 512:1024],
                                    sacc[:, tloc, 512:1024], ALU.add).then_inc("ve", 1)
                    w.cp(f"f_gsum_{ti}", "ve")
                    w.cp(f"f_rt_{ti}", "ve")
                    w.wait(f"f_silu_{ti}")
                    w.wait(f"f_xs_{ti}")
                    if ti >= 2:
                        w.wait(f"f_out_{ti - 2}")
                    w.tensor_tensor(g2t[:, j, :], xnt[:, j, :], g1t[:, j, :],
                                    ALU.mult).then_inc("ve", 1)
                    w.tensor_tensor(g2t[:, j, :], g2t[:, j, :], xt[:, j, :],
                                    ALU.add).then_inc("ve", 1)
                    w.cp(f"f_o_{ti}", "ve")

    def ph3(en, w):
        for q in range(NQ):
            ph3q(en, w, q)
        ph3fix(en, w)
        ph3yh(en, w)
        if en == "tensor":
            w.cp("kk_rrwka_free_mark", "pe")
        if en == "gpsimd":
            pass
        ph3kk(en, w)

    def alias(en, w):
        if en == "tensor":
            pass
        if en == "gpsimd":
            pass

    phases = [ph0, ph0rw, ph0b, ph0c, ph1, ph2, ph3, ph4, ph5]

    for en in ENGINES:
        w = EW(sched, sems, en, None)
        for ph in phases:
            ph(en, w)
    sched.cur["kk_rrwka_free"] = sched.cur["a_kvmm_3_7"]
    sched.start_real()

    with nc.Block() as block:
        def mk(en):
            def emit(real_engine):
                w = EW(sched, sems, en, real_engine)
                for ph in phases:
                    ph(en, w)
            return emit
        block.sync(mk("sync"))
        block.scalar(mk("scalar"))
        block.vector(mk("vector"))
        block.tensor(mk("tensor"))
        block.gpsimd(mk("gpsimd"))

    sched.cur["kk_rrwka_free"] = sched.cur["a_kvmm_3_7"]
    sched.check()
    return nc


_CACHE = {}


def kernel(**inputs):
    x = np.asarray(inputs["L0_attn_x"], np.float32)
    B, L, _ = x.shape
    xf = np.ascontiguousarray(x.reshape(B * L, D))

    vecs = np.ascontiguousarray(np.stack([
        np.asarray(inputs["L0_attn_ln1_g"], np.float32),
        np.asarray(inputs["L0_attn_ln1_b"], np.float32),
        np.asarray(inputs["L0_attn_ln2_g"], np.float32),
        np.asarray(inputs["L0_attn_ln2_b"], np.float32),
        np.asarray(inputs["L0_attn_mix_k"], np.float32),
        np.asarray(inputs["L0_attn_mix_v"], np.float32),
        np.asarray(inputs["L0_attn_mix_r"], np.float32),
        np.asarray(inputs["L0_attn_time_decay"], np.float32),
        np.asarray(inputs["L0_attn_time_first"], np.float32),
    ], axis=0))

    if "nc" not in _CACHE:
        _CACHE["nc"] = build_nc()
    nc = _CACHE["nc"]

    shared = {
        "wk": np.ascontiguousarray(np.asarray(inputs["L0_attn_Wk"], np.float32)),
        "wv": np.ascontiguousarray(np.asarray(inputs["L0_attn_Wv"], np.float32)),
        "wr": np.ascontiguousarray(np.asarray(inputs["L0_attn_Wr"], np.float32)),
        "wo": np.ascontiguousarray(np.asarray(inputs["L0_attn_Wo"], np.float32)),
        "wfr": np.ascontiguousarray(np.asarray(inputs["L0_attn_Wfr"], np.float32)),
        "wfk": np.ascontiguousarray(np.asarray(inputs["L0_attn_Wfk"], np.float32)),
        "wfv": np.ascontiguousarray(np.asarray(inputs["L0_attn_Wfv"], np.float32)),
        "sgw": np.ascontiguousarray(np.asarray(inputs["L0_ffn_sg"], np.float32)),
        "suw": np.ascontiguousarray(np.asarray(inputs["L0_ffn_su"], np.float32)),
        "sdw": np.ascontiguousarray(np.asarray(inputs["L0_ffn_sd"], np.float32)),
        "rww": np.ascontiguousarray(np.asarray(inputs["L0_ffn_router_w"], np.float32)),
        "vecs": vecs,
    }
    rg_a = np.asarray(inputs["L0_ffn_rg"], np.float32)
    ru_a = np.asarray(inputs["L0_ffn_ru"], np.float32)
    rd_a = np.asarray(inputs["L0_ffn_rd"], np.float32)

    in_maps = []
    for c in range(NCORE):
        h = c % 2
        im = dict(shared)
        im["xs"] = np.ascontiguousarray(xf[c * NTOK:(c + 1) * NTOK])
        im["xhalo"] = (np.ascontiguousarray(xf[c * NTOK - 1:c * NTOK])
                       if h == 1 else np.zeros((1, D), np.float32))
        im["rgw"] = np.ascontiguousarray(rg_a[c])
        im["ruw"] = np.ascontiguousarray(ru_a[c])
        im["rdw"] = np.ascontiguousarray(rd_a[c])
        sp = np.zeros((E, 1), np.float32)
        sp[:c, 0] = 1.0
        im["m_selprev"] = sp
        im["m_myid"] = np.full((128, 1), float(c), np.float32)
        im["m_flags"] = np.array([[1.0 if h == 0 else 0.0,
                                   1.0 if h == 1 else 0.0, 0.0, 0.0]], np.float32)
        poh = np.zeros((E, 1), np.float32)
        if h == 1:
            im["m_caridx"] = ((c - 1) * 128 + np.arange(128, dtype=np.int32)
                              ).reshape(128, 1)
            im["m_rowidx"] = np.array([[c - 1]], np.int32)
            poh[c - 1, 0] = 1.0
        else:
            im["m_caridx"] = np.full((128, 1), NCORE * 128, np.int32)
            im["m_rowidx"] = np.array([[OOB]], np.int32)
        im["m_predoh"] = poh
        in_maps.append(im)

    res = run_bass_kernel_spmd(nc, in_maps, core_ids=list(range(NCORE)))
    out = np.concatenate([res.results[c]["out"] for c in range(NCORE)], axis=0)
    return np.ascontiguousarray(out.reshape(B, L, D).astype(np.float32))
